# version 01
# baseline (speedup 1.0000x reference)
import numpy as np
import ml_dtypes
import concourse.bass as bass
import concourse.mybir as mybir
from concourse.bass_utils import run_bass_kernel_spmd

F32 = mybir.dt.float32
BF16 = mybir.dt.bfloat16
AF = mybir.ActivationFunctionType
ALU = mybir.AluOpType

B_LOC = 2
T = 2048
D = 1024
NT = T // 128
ND = D // 128

ENGS = ("t", "v", "s", "g", "y")

SEM_NAMES = ("LD", "ST", "MM", "EV", "VV", "SS", "GG",
             "Bt", "Bv", "Bs", "Bg", "By")


class R:

    def __init__(self, nc):
        self.nc = nc
        self.ops = {e: [] for e in ENGS}
        self.cnt = {}
        self.sems = {}
        self.dirty = {e: False for e in ENGS}

    def setup_sems(self, handles):
        for nm, h in handles.items():
            self.sems[nm] = h
            self.cnt[nm] = 0

    def emit(self, eng, fn):
        self.ops[eng].append(fn)
        self.dirty[eng] = True

    def wait(self, eng, sem, val):
        if val <= 0:
            return
        h = self.sems[sem]
        self.emit(eng, lambda e, h=h, v=val: e.wait_ge(h, v))

    def add(self, eng, build, inc=None, n=1):
        if inc is not None:
            self.cnt[inc] += n
            h = self.sems[inc]

            def fn(e, build=build, h=h, n=n):
                build(e).then_inc(h, n)
        else:
            def fn(e, build=build):
                build(e)
        self.emit(eng, fn)
        return self.cnt[inc] if inc is not None else None

    def addf(self, eng, build, sem=None):
        if sem is None:
            sem = {"v": "VV", "s": "SS", "g": "GG", "t": "MM", "y": "By"}[eng]
        self.add(eng, build, inc=sem)
        self.wait(eng, sem, self.cnt[sem])
        return self.cnt[sem]

    def dma(self, eng, dst, src, inc="LD"):
        return self.add(eng, lambda e, d=dst, s=src: e.dma_start(d, s), inc=inc, n=16)

    def barrier(self, engs=ENGS):
        targets = {}
        for e in engs:
            if self.dirty[e]:
                name = "B" + e
                if e == "v":
                    self.add("v", lambda v: v.tensor_copy(self.dummy2[:], self.dummy1[:]), inc=name)
                elif e == "s":
                    self.add("s", lambda s: s.copy(self.dummy2[:], self.dummy1[:]), inc=name)
                elif e == "g":
                    self.add("g", lambda g: g.tensor_copy(self.dummy2[:], self.dummy1[:]), inc=name)
                else:
                    self.add(e, lambda x: x.nop(), inc=name)
                targets[e] = self.cnt[name]
        for e in engs:
            for o, tv in targets.items():
                if o != e:
                    self.wait(e, "B" + o, tv)
        for e in engs:
            self.dirty[e] = False

    def finalize(self, block):
        for eng_key, attr in (("t", "tensor"), ("v", "vector"), ("s", "scalar"),
                              ("g", "gpsimd"), ("y", "sync")):
            ops = self.ops[eng_key]

            def make(ops=ops):
                def _(e):
                    for fn in ops:
                        fn(e)
                return _
            getattr(block, attr)(make())


def mm_group(r, psum_ap, pairs, inc="MM"):
    k = len(pairs)
    for i, (l, rh) in enumerate(pairs):
        st, sp = (i == 0), (i == k - 1)
        r.add("t", lambda t, l=l, rh=rh, st=st, sp=sp, p=psum_ap:
              t.matmul(p, l, rh, start=st, stop=sp),
              inc=(inc if sp else None))


class PsumPipe:

    def __init__(self, r, bufs):
        self.r, self.bufs, self.n = r, bufs, len(bufs)
        self.i = 0

    def step(self, pairs, evacs, extra_waits=(), fence_evac=False, width=None, rows=None):
        r = self.r
        buf = self.bufs[self.i % self.n]
        r.wait("t", "EV", r.cnt["MM"] + 1 - self.n)
        for w_eng, w_sem, w_val in extra_waits:
            if w_eng == "t":
                r.wait("t", w_sem, w_val)
        if width is None and rows is None:
            out_ap = buf[:]
        else:
            out_ap = buf[0:(rows if rows else 128), 0:(width if width else 512)]
        mm_group(r, out_ap, pairs, inc="MM")
        mmv = r.cnt["MM"]
        seen = set()
        for j, (eng, fn) in enumerate(evacs):
            if eng not in seen:
                r.wait(eng, "MM", mmv)
                for w_eng, w_sem, w_val in extra_waits:
                    if w_eng == eng:
                        r.wait(eng, w_sem, w_val)
                seen.add(eng)
            last = (j == len(evacs) - 1)
            r.add(eng, lambda e, fn=fn, buf=buf: fn(e, buf),
                  inc=("EV" if last else None))
            if last and fence_evac:
                r.wait(eng, "EV", r.cnt["EV"])
        self.i += 1


def emit_weight_T(r, w_dram, E, outT, cc):
    stage, stage_b, ident, pts = cc["wstage"], cc["wstage_b"], cc["identb"], cc["ptb"]
    net = (E + 127) // 128
    for et in range(net):
        rows = min(128, E - et * 128)
        r.wait("y", "VV", r.cnt["VV"])
        ld = r.dma("y", stage[0:rows, :], w_dram[et * 128:et * 128 + rows, :])
        r.wait("v", "LD", ld)
        r.wait("v", "MM", r.cnt["MM"])
        r.add("v", lambda v, rw=rows: v.tensor_copy(stage_b[0:rw, :], stage[0:rw, :]),
              inc="VV")
        r.wait("t", "VV", r.cnt["VV"])
        for half in range(2):
            p = pts[half]
            r.wait("t", "EV", r.cnt["MM"] + 1 - 2)
            for j in range(4):
                dt = half * 4 + j
                r.add("t", lambda t, rw=rows, dt=dt, j=j, p=p:
                      t.transpose(p[:, j * 128:j * 128 + rw],
                                  stage_b[0:rw, dt * 128:(dt + 1) * 128],
                                  ident[0:rw, 0:rw]),
                      inc=("MM" if j == 3 else None))
            r.wait("v", "MM", r.cnt["MM"])
            r.add("v", lambda v, rw=rows, half=half, et=et, p=p:
                  v.tensor_copy(
                      outT[:, half * 4:half * 4 + 4, et * 128:et * 128 + rows],
                      p.rearrange("p (a c) -> p a c", a=4)[:, :, 0:rows]),
                  inc="EV")


def emit_ln_transpose(r, x_dram_b, xlnT, cc):
    xt, cen, sq = cc["xt"], cc["cen"], cc["sq"]
    cols = cc["cols"]
    ptb = cc["ptb"]
    xlnb = cc["xlnb"]
    vv_after = {}
    for tt in range(NT):
        xb = xt[tt % 2]
        if tt >= 2:
            r.wait("y", "VV", vv_after[tt - 2])
        ld = r.dma("y", xb[:], x_dram_b[tt * 128:(tt + 1) * 128, :])
        r.wait("v", "LD", ld)
        r.add("v", lambda v, xb=xb: v.reduce_sum(cols[:, 0:1], xb[:], axis=mybir.AxisListType.X))
        r.add("v", lambda v: v.tensor_scalar(out=cols[:, 1:2], in0=cols[:, 0:1],
                                             scalar1=1.0 / D, scalar2=None, op0=ALU.mult))
        r.add("v", lambda v, xb=xb: v.tensor_scalar(out=cen[:], in0=xb[:],
                                                    scalar1=cols[:, 1:2], scalar2=None,
                                                    op0=ALU.subtract))
        r.add("v", lambda v: v.tensor_mul(sq[:], cen[:], cen[:]))
        r.add("v", lambda v: v.reduce_sum(cols[:, 2:3], sq[:], axis=mybir.AxisListType.X))
        r.add("v", lambda v: v.tensor_scalar(out=cols[:, 3:4], in0=cols[:, 2:3],
                                             scalar1=1.0 / D, scalar2=1e-5,
                                             op0=ALU.mult, op1=ALU.add), inc="VV")
        r.wait("s", "VV", r.cnt["VV"])
        r.add("s", lambda s: s.activation(cols[:, 4:5], cols[:, 3:4], AF.Ln))
        r.add("s", lambda s: s.activation(cols[:, 5:6], cols[:, 4:5], AF.Exp, scale=-0.5),
              inc="SS")
        r.wait("v", "SS", r.cnt["SS"])
        xo = xlnb[tt % 2]
        r.add("v", lambda v, xo=xo: v.tensor_scalar(out=xo[:], in0=cen[:],
                                                    scalar1=cols[:, 5:6], scalar2=None,
                                                    op0=ALU.mult), inc="VV")
        vv_after[tt] = r.cnt["VV"]
        r.wait("t", "VV", r.cnt["VV"])
        for half in range(2):
            p = ptb[half]
            r.wait("t", "EV", r.cnt["MM"] + 1 - 2)
            for j in range(4):
                dt = half * 4 + j
                r.add("t", lambda t, xo=xo, dt=dt, j=j, p=p:
                      t.transpose(p[:, j * 128:(j + 1) * 128],
                                  xo[:, dt * 128:(dt + 1) * 128], cc["identb"]),
                      inc=("MM" if j == 3 else None))
            r.wait("v", "MM", r.cnt["MM"])
            r.add("v", lambda v, half=half, tt=tt, p=p:
                  v.tensor_copy(xlnT[:, half * 4:half * 4 + 4, tt * 128:(tt + 1) * 128],
                                p.rearrange("p (a c) -> p a c", a=4)),
                  inc="EV")


def common_sems(nc):
    import contextlib
    stack = contextlib.ExitStack()
    handles = {}
    for nm in SEM_NAMES:
        handles[nm] = stack.enter_context(nc.semaphore("sem" + nm))
    return stack, handles



def emit_ln_tile(r, x_tile_dram, cc, tt, dest, tcol):
    xt, cen, sq, cols = cc["xt"], cc["cen"], cc["sq"], cc["cols"]
    xb = xt[tt % 2]
    vv_prev = cc.get("vv_ln", {})
    if tt - 2 in vv_prev:
        r.wait("y", "VV", vv_prev[tt - 2])
    ld = r.dma("y", xb[:], x_tile_dram)
    r.wait("v", "LD", ld)
    r.addf("v", lambda v, xb=xb: v.reduce_sum(cols[:, 0:1], xb[:], axis=mybir.AxisListType.X))
    r.addf("v", lambda v: v.tensor_scalar(out=cols[:, 1:2], in0=cols[:, 0:1],
                                          scalar1=1.0 / D, scalar2=None,
                                          op0=ALU.mult))
    r.add("v", lambda v, xb=xb: v.tensor_scalar(out=cen[:], in0=xb[:],
                                                scalar1=cols[:, 1:2], scalar2=None,
                                                op0=ALU.subtract))
    r.add("v", lambda v: v.tensor_mul(sq[:], cen[:], cen[:]))
    r.addf("v", lambda v: v.reduce_sum(cols[:, 2:3], sq[:], axis=mybir.AxisListType.X))
    r.addf("v", lambda v: v.tensor_scalar(out=cols[:, 3:4], in0=cols[:, 2:3],
                                          scalar1=1.0 / D, scalar2=1e-5,
                                          op0=ALU.mult, op1=ALU.add))
    r.wait("s", "VV", r.cnt["VV"])
    r.addf("s", lambda s: s.activation(cols[:, 4:5], cols[:, 3:4], AF.Ln))
    r.addf("s", lambda s: s.activation(cols[:, 5:6], cols[:, 4:5], AF.Exp, scale=-0.5))
    r.wait("v", "SS", r.cnt["SS"])
    xo = cc["xlnb"][tt % 2]
    r.add("v", lambda v, xo=xo: v.tensor_scalar(out=xo[:], in0=cen[:],
                                                scalar1=cols[:, 5:6], scalar2=None,
                                                op0=ALU.mult), inc="VV")
    vv_prev[tt] = r.cnt["VV"]
    cc["vv_ln"] = vv_prev
    r.wait("t", "VV", r.cnt["VV"])
    for half in range(2):
        p = cc["ptb"][half]
        r.wait("t", "EV", r.cnt["MM"] + 1 - 2)
        for j in range(4):
            dt = half * 4 + j
            r.add("t", lambda t, xo=xo, dt=dt, j=j, p=p:
                  t.transpose(p[:, j * 128:(j + 1) * 128],
                              xo[:, dt * 128:(dt + 1) * 128], cc["identb"]),
                  inc=("MM" if j == 3 else None))
        r.wait("v", "MM", r.cnt["MM"])
        r.add("v", lambda v, half=half, p=p, tcol=tcol, dest=dest:
              v.tensor_copy(dest[:, half * 4:half * 4 + 4, tcol * 128:(tcol + 1) * 128],
                            p.rearrange("p (a c) -> p a c", a=4)),
              inc="EV")


def emit_ln_transpose(r, x_dram_b, xlnT, cc):
    for tt in range(NT):
        emit_ln_tile(r, x_dram_b[tt * 128:(tt + 1) * 128, :], cc, tt, xlnT, tt)
    cc.pop("vv_ln", None)


def build_l1ffn(debug=False):
    nc = bass.Bass("TRN2", target_bir_lowering=False)
    x = nc.dram_tensor("x", [B_LOC, T, D], F32, kind="ExternalInput").ap()
    xo = nc.dram_tensor("xo", [B_LOC, T, D], F32, kind="ExternalOutput").ap()
    w_r = nc.dram_tensor("L1_ffn_W_r", [D, D], F32, kind="ExternalInput").ap()
    w_k = nc.dram_tensor("L1_ffn_W_k", [D, D], F32, kind="ExternalInput").ap()
    w_v = nc.dram_tensor("L1_ffn_W_v", [D, D], F32, kind="ExternalInput").ap()
    mu_r = nc.dram_tensor("L1_ffn_mu_r", [1, D], F32, kind="ExternalInput").ap()
    mu_k = nc.dram_tensor("L1_ffn_mu_k", [1, D], F32, kind="ExternalInput").ap()
    identb_d = nc.dram_tensor("c_identb", [128, 128], BF16, kind="ExternalInput").ap()
    identf_d = nc.dram_tensor("c_identf", [128, 128], F32, kind="ExternalInput").ap()
    ones_d = nc.dram_tensor("c_ones", [1, 128], F32, kind="ExternalInput").ap()
    if debug:
        dbg_xr = nc.dram_tensor("dbg_xr", [128, 8, T], BF16, kind="ExternalOutput").ap()
        dbg_xk = nc.dram_tensor("dbg_xk", [128, 8, T], BF16, kind="ExternalOutput").ap()
        dbg_ka = nc.dram_tensor("dbg_ka", [128, 8, T], BF16, kind="ExternalOutput").ap()
        dbg_mus = nc.dram_tensor("dbg_mus", [128, 8, 4], F32, kind="ExternalOutput").ap()

    r = R(nc)
    stack, handles = common_sems(nc)
    al = stack.enter_context
    identb = al(nc.sbuf_tensor("identb", [128, 128], BF16))
    identf = al(nc.sbuf_tensor("identf", [128, 128], F32))
    onesf = al(nc.sbuf_tensor("onesf", [1, 128], F32))
    dummy1 = al(nc.sbuf_tensor("dummy1", [1, 4], F32))
    dummy2 = al(nc.sbuf_tensor("dummy2", [1, 4], F32))
    wrT = al(nc.sbuf_tensor("wrT", [128, 8, D], BF16))
    wkT = al(nc.sbuf_tensor("wkT", [128, 8, D], BF16))
    wvT = al(nc.sbuf_tensor("wvT", [128, 8, D], BF16))
    murow = al(nc.sbuf_tensor("murow", [1, D], F32))
    mus = al(nc.sbuf_tensor("mus", [128, 8, 4], F32))
    xrT = al(nc.sbuf_tensor("xrT", [128, 8, T], BF16))
    xkT = al(nc.sbuf_tensor("xkT", [128, 8, T], BF16))
    kaT = al(nc.sbuf_tensor("kaT", [128, 8, T], BF16))
    wstage = al(nc.sbuf_tensor("wstage", [128, 1024], F32))
    wstage_b = al(nc.sbuf_tensor("wstage_b", [128, 1024], BF16))
    xt0 = al(nc.sbuf_tensor("xt0", [128, 1024], F32))
    xt1 = al(nc.sbuf_tensor("xt1", [128, 1024], F32))
    cen = al(nc.sbuf_tensor("cen", [128, 1024], F32))
    sq = al(nc.sbuf_tensor("sq", [128, 1024], F32))
    cols = al(nc.sbuf_tensor("cols", [128, 8], F32))
    xlnb0 = al(nc.sbuf_tensor("xlnb0", [128, 1024], BF16))
    xlnb1 = al(nc.sbuf_tensor("xlnb1", [128, 1024], BF16))
    xlntile = al(nc.sbuf_tensor("xlntile", [128, 8, 128], BF16))
    xlntile3 = xlntile
    prevcol = al(nc.sbuf_tensor("prevcol", [128, 8], BF16))
    shtmp = al(nc.sbuf_tensor("shtmp", [128, 128], F32))
    ko0 = al(nc.sbuf_tensor("ko0", [128, 512], F32))
    rtile = al(nc.sbuf_tensor("rtile", [128, 512], BF16))
    vtmp = al(nc.sbuf_tensor("vtmp", [128, 512], F32))
    pb0 = al(nc.psum_tensor("pb0", [128, 512], BF16))
    pb1 = al(nc.psum_tensor("pb1", [128, 512], BF16))
    pf0 = al(nc.psum_tensor("pf0", [128, 512], F32))
    pf1 = al(nc.psum_tensor("pf1", [128, 512], F32))
    pf2 = al(nc.psum_tensor("pf2", [128, 512], F32))
    pf3 = al(nc.psum_tensor("pf3", [128, 512], F32))
    with nc.Block() as block:

        r.setup_sems(handles)
        r.dummy1, r.dummy2 = dummy1, dummy2
        cc = {"xt": [xt0, xt1], "cen": cen, "sq": sq, "cols": cols,
              "ptb": [pb0, pb1], "xlnb": [xlnb0, xlnb1], "identb": identb[:],
              "wstage": wstage, "wstage_b": wstage_b, "xlntile": xlntile}
        pfs = [pf0, pf1, pf2, pf3]

        r.dma("y", identb[:], identb_d)
        r.dma("y", identf[:], identf_d)
        r.dma("y", onesf[:], ones_d)
        ld = r.dma("y", murow[:], mu_r)
        r.wait("t", "LD", ld)
        for dt in range(ND):
            r.add("t", lambda t, dt=dt: t.matmul(
                pf0[:, dt:dt + 1], murow[0:1, dt * 128:(dt + 1) * 128],
                onesf[0:1, 0:1], start=True, stop=True),
                inc=("MM" if dt == ND - 1 else None))
        mu_r_mm = r.cnt["MM"]
        r.wait("v", "MM", mu_r_mm)
        r.addf("v", lambda v: v.tensor_copy(mus[:, :, 0], pf0[:, 0:8]))
        r.addf("v", lambda v: v.tensor_scalar(out=mus[:, :, 1], in0=pf0[:, 0:8],
                                              scalar1=-1.0, scalar2=1.0,
                                              op0=ALU.mult, op1=ALU.add))
        r.add("v", lambda v: v.tensor_copy(r.dummy2[:], r.dummy1[:]), inc="EV")
        r.wait("y", "MM", mu_r_mm)
        ld2 = r.dma("y", murow[:], mu_k)
        r.wait("t", "LD", ld2)
        r.wait("t", "EV", r.cnt["EV"])
        for dt in range(ND):
            r.add("t", lambda t, dt=dt: t.matmul(
                pf1[:, dt:dt + 1], murow[0:1, dt * 128:(dt + 1) * 128],
                onesf[0:1, 0:1], start=True, stop=True),
                inc=("MM" if dt == ND - 1 else None))
        r.wait("v", "MM", r.cnt["MM"])
        r.addf("v", lambda v: v.tensor_copy(mus[:, :, 2], pf1[:, 0:8]))
        r.addf("v", lambda v: v.tensor_scalar(out=mus[:, :, 3], in0=pf1[:, 0:8],
                                              scalar1=-1.0, scalar2=1.0,
                                              op0=ALU.mult, op1=ALU.add))
        r.add("v", lambda v: v.tensor_copy(r.dummy2[:], r.dummy1[:]), inc="EV")
        r.barrier()
        emit_weight_T(r, w_r, D, wrT, cc)
        emit_weight_T(r, w_k, D, wkT, cc)
        emit_weight_T(r, w_v, D, wvT, cc)
        r.barrier()

        for b in range(B_LOC):
            r.addf("v", lambda v: v.tensor_scalar(out=prevcol[:], in0=mus[:, :, 0],
                                                  scalar1=0.0, scalar2=None,
                                                  op0=ALU.mult))
            for tt in range(NT):
                emit_ln_tile(r, x[b, tt * 128:(tt + 1) * 128, :], cc, tt, xlntile3, 0)
                r.wait("v", "EV", r.cnt["EV"])
                for dt in range(ND):
                    for j, dst in ((0, xrT), (1, xkT)):
                        mu_c = mus[:, dt, 2 * j:2 * j + 1]
                        om_c = mus[:, dt, 2 * j + 1:2 * j + 2]
                        r.add("v", lambda v, dt=dt, mu_c=mu_c:
                              v.tensor_scalar(out=shtmp[:], in0=xlntile[:, dt, :],
                                              scalar1=mu_c, scalar2=None, op0=ALU.mult))
                        r.add("v", lambda v, dt=dt, dst=dst, om_c=om_c, tt=tt:
                              v.scalar_tensor_tensor(
                                  out=dst[:, dt, tt * 128 + 1:(tt + 1) * 128],
                                  in0=xlntile[:, dt, 0:127], scalar=om_c,
                                  in1=shtmp[:, 1:128], op0=ALU.mult, op1=ALU.add))
                        r.add("v", lambda v, dt=dt, dst=dst, om_c=om_c, tt=tt:
                              v.scalar_tensor_tensor(
                                  out=dst[:, dt, tt * 128:tt * 128 + 1],
                                  in0=prevcol[:, dt:dt + 1], scalar=om_c,
                                  in1=shtmp[:, 0:1], op0=ALU.mult, op1=ALU.add))
                    r.addf("v", lambda v, dt=dt: v.tensor_copy(
                        prevcol[:, dt:dt + 1], xlntile[:, dt, 127:128]))
            r.barrier()
            if debug and b == 0:
                r.dma("g", dbg_xr, xrT[:], inc="ST")
                r.dma("g", dbg_xk, xkT[:], inc="ST")
                r.dma("g", dbg_mus, mus[:], inc="ST")
                r.wait("g", "ST", r.cnt["ST"])
            pipe = PsumPipe(r, pfs)
            for ot in range(ND):
                for tc in range(4):
                    pairs = [(wkT[:, dt, ot * 128:(ot + 1) * 128],
                              xkT[:, dt, tc * 512:(tc + 1) * 512]) for dt in range(ND)]
                    pipe.step(pairs, [
                        ("s", lambda s, p: s.activation(ko0[:], p[:], AF.Relu)),
                        ("s", lambda s, p, ot=ot, tc=tc: s.activation(
                            kaT[:, ot, tc * 512:(tc + 1) * 512], ko0[:], AF.Square)),
                    ])
            r.barrier()
            if debug and b == 0:
                r.dma("g", dbg_ka, kaT[:], inc="ST")
                r.wait("g", "ST", r.cnt["ST"])
            ev_base = r.cnt["EV"]
            for tt in range(NT):
                xb = xt0 if tt % 2 == 0 else xt1
                if tt >= 2:
                    r.wait("y", "EV", ev_base + 4 * (tt - 1))
                ldx = r.dma("y", xb[:], x[b, tt * 128:(tt + 1) * 128, :])
                for oc in range(2):
                    prev_ev = r.cnt["EV"]
                    rpairs = [(xrT[:, dt, tt * 128:(tt + 1) * 128],
                               wrT[:, dt, oc * 512:(oc + 1) * 512]) for dt in range(ND)]
                    pipe.step(rpairs, [
                        ("s", lambda s, p: s.activation(rtile[:], p[:], AF.Sigmoid)),
                    ], extra_waits=[("s", "EV", prev_ev)])
                    ev_r = r.cnt["EV"]
                    vpairs = [(kaT[:, dt, tt * 128:(tt + 1) * 128],
                               wvT[:, dt, oc * 512:(oc + 1) * 512]) for dt in range(ND)]
                    pipe.step(vpairs, [
                        ("v", lambda v, p: v.tensor_tensor(
                            out=vtmp[:], in0=rtile[:], in1=p[:], op=ALU.mult)),
                        ("v", lambda v, p, oc=oc, xb=xb: v.scalar_tensor_tensor(
                            out=cen[:, oc * 512:(oc + 1) * 512], in0=vtmp[:],
                            scalar=0.5, in1=xb[:, oc * 512:(oc + 1) * 512],
                            op0=ALU.mult, op1=ALU.add)),
                    ], extra_waits=[("v", "LD", ldx), ("v", "EV", ev_r)])
                r.wait("v", "ST", r.cnt["ST"])
                r.add("v", lambda v: v.tensor_copy(sq[:], cen[:]), inc="VV")
                r.wait("g", "VV", r.cnt["VV"])
                r.dma("g", xo[b, tt * 128:(tt + 1) * 128, :], sq[:], inc="ST")
            r.barrier()
            cc.pop("vv_ln", None)
        r.wait("y", "ST", r.cnt["ST"])
        r.barrier()
        r.finalize(block)
    stack.close()
    return nc



def build_l0ffn():
    nc = bass.Bass("TRN2", target_bir_lowering=False)
    x = nc.dram_tensor("x", [B_LOC, T, D], F32, kind="ExternalInput").ap()
    xo = nc.dram_tensor("xo", [B_LOC, T, D], F32, kind="ExternalOutput").ap()
    wd = {}
    for p in ("q", "k", "v"):
        wd[p + "_node_w"] = nc.dram_tensor(f"L0_ffn_{p}_node_w", [15, D], F32, kind="ExternalInput").ap()
        wd[p + "_node_b"] = nc.dram_tensor(f"L0_ffn_{p}_node_b", [1, 15], F32, kind="ExternalInput").ap()
        wd[p + "_w1"] = nc.dram_tensor(f"L0_ffn_{p}_w1", [16, D, 64], F32, kind="ExternalInput").ap()
        wd[p + "_b1"] = nc.dram_tensor(f"L0_ffn_{p}_b1", [1, D], F32, kind="ExternalInput").ap()
        wd[p + "_w2"] = nc.dram_tensor(f"L0_ffn_{p}_w2", [16, 64, D], F32, kind="ExternalInput").ap()
        wd[p + "_b2"] = nc.dram_tensor(f"L0_ffn_{p}_b2", [16, D], F32, kind="ExternalInput").ap()
    identb_d = nc.dram_tensor("c_identb", [128, 128], BF16, kind="ExternalInput").ap()
    identf_d = nc.dram_tensor("c_identf", [128, 128], F32, kind="ExternalInput").ap()
    ones_d = nc.dram_tensor("c_ones", [1, 128], F32, kind="ExternalInput").ap()
    emat_d = nc.dram_tensor("c_emat", [16, 8, 128], BF16, kind="ExternalInput").ap()

    import contextlib
    r = R(nc)
    stack, handles = common_sems(nc)
    al = stack.enter_context
    identb = al(nc.sbuf_tensor("identb", [128, 128], BF16))
    identf = al(nc.sbuf_tensor("identf", [128, 128], F32))
    onesf = al(nc.sbuf_tensor("onesf", [1, 128], F32))
    ematsb = al(nc.sbuf_tensor("ematsb", [16, 8, 128], BF16))
    dummy1 = al(nc.sbuf_tensor("dummy1", [1, 4], F32))
    dummy2 = al(nc.sbuf_tensor("dummy2", [1, 4], F32))
    qT = al(nc.sbuf_tensor("qT", [128, 8, T], BF16))
    kT = al(nc.sbuf_tensor("kT", [128, 8, T], BF16))
    vnat = al(nc.sbuf_tensor("vnat", [128, NT, D], BF16))
    nwT = al(nc.sbuf_tensor("nwT", [128, 3, 8, 15], BF16))
    nbbc = al(nc.sbuf_tensor("nbbc", [128, 3, 15], F32))
    b1c = al(nc.sbuf_tensor("b1c", [128, 3, 8], F32))
    b2b = al(nc.sbuf_tensor("b2b", [16, D], BF16))
    b2ball = nc.dram_tensor("b2b_scr", [3, 16, D], BF16, kind="Internal").ap()
    xt0 = al(nc.sbuf_tensor("xt0", [128, 1024], F32))
    xt1 = al(nc.sbuf_tensor("xt1", [128, 1024], F32))
    cen = al(nc.sbuf_tensor("cen", [128, 1024], F32))
    sq = al(nc.sbuf_tensor("sq", [128, 1024], F32))
    cols = al(nc.sbuf_tensor("cols", [128, 8], F32))
    xlnb0 = al(nc.sbuf_tensor("xlnb0", [128, 1024], BF16))
    xlnb1 = al(nc.sbuf_tensor("xlnb1", [128, 1024], BF16))
    S_all = al(nc.sbuf_tensor("S_all", [128, NT, 15], F32))
    gbits = al(nc.sbuf_tensor("gbits", [128, NT, 4], F32))
    sv = al(nc.sbuf_tensor("sv", [128, NT, 20], F32))
    sel = al(nc.sbuf_tensor("sel", [128, NT, 16], F32))
    selTb = al(nc.sbuf_tensor("selTb", [16, T], BF16))
    pb0 = al(nc.psum_tensor("pb0", [128, 512], BF16))
    pb1 = al(nc.psum_tensor("pb1", [128, 512], BF16))

    with nc.Block() as block:
        r.setup_sems(handles)
        r.dummy1, r.dummy2 = dummy1, dummy2
        cc = {"xt": [xt0, xt1], "cen": cen, "sq": sq, "cols": cols,
              "ptb": [pb0, pb1], "xlnb": [xlnb0, xlnb1], "identb": identb[:]}

        with contextlib.ExitStack() as prep_stack:
            a2 = prep_stack.enter_context
            pf0 = a2(nc.psum_tensor("pf0p", [128, 512], F32))
            wstage = a2(nc.sbuf_tensor("wstage", [128, 1024], F32))
            wstage_b = a2(nc.sbuf_tensor("wstage_b", [128, 1024], BF16))
            cc["wstage"], cc["wstage_b"] = wstage, wstage_b

            r.dma("y", identb[:], identb_d)
            r.dma("y", identf[:], identf_d)
            r.dma("y", onesf[:], ones_d)
            r.dma("y", ematsb[:], emat_d)
            for pi, p in enumerate(("q", "k", "v")):
                emit_weight_T(r, wd[p + "_node_w"], 15, nwT[:, pi], cc)
                r.wait("y", "VV", r.cnt["VV"])
                ldnb = r.dma("y", wstage[0:1, 0:15], wd[p + "_node_b"])
                r.wait("t", "LD", ldnb)
                r.wait("t", "EV", r.cnt["MM"])
                r.add("t", lambda t, pi=pi: t.matmul(pf0[:, 0:15], onesf[:],
                                                     wstage[0:1, 0:15],
                                                     start=True, stop=True), inc="MM")
                r.wait("v", "MM", r.cnt["MM"])
                r.addf("v", lambda v, pi=pi: v.tensor_copy(nbbc[:, pi, :], pf0[:, 0:15]))
                r.add("v", lambda v: v.tensor_copy(r.dummy2[:], r.dummy1[:]), inc="EV")
                r.wait("y", "EV", r.cnt["EV"])
                r.wait("y", "MM", r.cnt["MM"])
                ldb1 = r.dma("y", wstage[0:1, :], wd[p + "_b1"])
                r.wait("t", "LD", ldb1)
                r.wait("t", "EV", r.cnt["MM"])
                for dt in range(ND):
                    r.add("t", lambda t, dt=dt: t.matmul(
                        pf0[:, dt:dt + 1], wstage[0:1, dt * 128:(dt + 1) * 128],
                        onesf[0:1, 0:1], start=True, stop=True),
                        inc=("MM" if dt == ND - 1 else None))
                r.wait("v", "MM", r.cnt["MM"])
                r.addf("v", lambda v, pi=pi: v.tensor_copy(b1c[:, pi, :], pf0[:, 0:8]))
                r.add("v", lambda v: v.tensor_copy(r.dummy2[:], r.dummy1[:]), inc="EV")
                r.wait("y", "VV", r.cnt["VV"])
                ldb2 = r.dma("y", wstage[0:16, :], wd[p + "_b2"])
                r.wait("v", "LD", ldb2)
                r.add("v", lambda v: v.tensor_copy(b2b[:], wstage[0:16, :]), inc="VV")
                r.wait("g", "VV", r.cnt["VV"])
                r.dma("g", b2ball[pi], b2b[:], inc="ST")
            r.barrier()

        for b in range(B_LOC):
            with contextlib.ExitStack() as px:
                a3 = px.enter_context
                xlnT = a3(nc.sbuf_tensor(f"xlnT{b}", [128, 8, T], BF16))
                hidT = a3(nc.sbuf_tensor(f"hidT{b}", [128, 8, T], BF16))
                w1sl = a3(nc.sbuf_tensor(f"w1sl{b}", [128, 8, 128], BF16))
                w2vsl = a3(nc.sbuf_tensor(f"w2vsl{b}", [128, 8, 512], BF16))
                wsl_f = a3(nc.sbuf_tensor(f"wsl_f{b}", [128, 8, 128], F32))
                pf0 = a3(nc.psum_tensor(f"pf0_{b}", [128, 512], F32))
                pf1 = a3(nc.psum_tensor(f"pf1_{b}", [128, 512], F32))
                pf2 = a3(nc.psum_tensor(f"pf2_{b}", [128, 512], F32))
                pf3 = a3(nc.psum_tensor(f"pf3_{b}", [128, 512], F32))
                ps_sel = a3(nc.psum_tensor(f"ps_sel{b}", [16, 512], F32))
                pfs = [pf0, pf1, pf2, pf3]
                emit_ln_transpose(r, x[b], xlnT, cc)
                r.barrier()
                pipe = PsumPipe(r, pfs)
                for pi, p in enumerate(("q", "k", "v")):
                    r.wait("y", "ST", r.cnt["ST"])
                    ldb2p = r.dma("y", b2b[:], b2ball[pi])
                    r.wait("t", "LD", ldb2p)
                    for tt in range(NT):
                        pairs = [(xlnT[:, dt, tt * 128:(tt + 1) * 128],
                                  nwT[:, pi, dt, :]) for dt in range(ND)]
                        pipe.step(pairs, [
                            ("v", lambda v, ps, tt=tt, pi=pi: v.tensor_tensor(
                                out=S_all[:, tt, :], in0=ps[:, 0:15],
                                in1=nbbc[:, pi, :], op=ALU.add)),
                        ], fence_evac=True, width=15)
                    emit_routing(r, S_all, gbits, sv, sel)
                    for g4 in range(4):
                        r.wait("t", "VV", r.cnt["VV"])
                        r.wait("t", "EV", r.cnt["MM"] + 1 - 1)
                        for j in range(4):
                            tt = g4 * 4 + j
                            r.add("t", lambda t, tt=tt, j=j: t.transpose(
                                ps_sel[:, j * 128:(j + 1) * 128], sel[:, tt, :],
                                identf[:]), inc=("MM" if j == 3 else None))
                        r.wait("v", "MM", r.cnt["MM"])
                        r.add("v", lambda v, g4=g4: v.tensor_copy(
                            selTb[:, g4 * 512:(g4 + 1) * 512], ps_sel[:]), inc="EV")
                    for lht in range(8):
                        r.wait("y", "VV", r.cnt["VV"])
                        r.wait("y", "MM", r.cnt["MM"])
                        for lp in range(2):
                            ldw = r.dma("y", wsl_f[:, :, lp * 64:(lp + 1) * 64],
                                        wd[p + "_w1"].rearrange(
                                            "l (dt pp) h -> pp dt l h",
                                            pp=128)[:, :, 2 * lht + lp, :])
                        r.wait("v", "LD", ldw)
                        r.add("v", lambda v: v.tensor_copy(w1sl[:], wsl_f[:]),
                              inc="VV")
                        vv_w1 = r.cnt["VV"]
                        for tc in range(4):
                            pairs = [(w1sl[:, dt, :],
                                      xlnT[:, dt, tc * 512:(tc + 1) * 512])
                                     for dt in range(ND)]
                            pipe.step(pairs, [
                                ("s", lambda sE, ps, lht=lht, tc=tc, pi=pi:
                                 sE.activation(hidT[:, lht, tc * 512:(tc + 1) * 512],
                                               ps[:], AF.Relu,
                                               bias=b1c[:, pi, lht:lht + 1])),
                            ], extra_waits=[("t", "VV", vv_w1)])
                            ev_relu = r.cnt["EV"]
                            mp = [(ematsb[:, lht, :],
                                   selTb[:, tc * 512:(tc + 1) * 512])]
                            pipe.step(mp, [
                                ("v", lambda v, ps, lht=lht, tc=tc: v.tensor_tensor(
                                    out=hidT[:, lht, tc * 512:(tc + 1) * 512],
                                    in0=hidT[:, lht, tc * 512:(tc + 1) * 512],
                                    in1=ps[:], op=ALU.mult)),
                            ], extra_waits=[("v", "EV", ev_relu)])
                    r.barrier()
                    if p in ("q", "k"):
                        outT = qT if p == "q" else kT
                        for ot in range(ND):
                            r.wait("y", "VV", r.cnt["VV"])
                            r.wait("y", "MM", r.cnt["MM"])
                            ldw = r.dma("y", wsl_f[:, :, 0:128],
                                        wd[p + "_w2"].rearrange(
                                            "(lht lp) h o -> (lp h) lht o", lp=2)
                                        [:, :, ot * 128:(ot + 1) * 128])
                            r.wait("v", "LD", ldw)
                            r.add("v", lambda v: v.tensor_copy(w1sl[:],
                                                               wsl_f[:, :, 0:128]),
                                  inc="VV")
                            vv_w2 = r.cnt["VV"]
                            for tc in range(4):
                                pairs = [(w1sl[:, lht, :],
                                          hidT[:, lht, tc * 512:(tc + 1) * 512])
                                         for lht in range(8)]
                                pairs.append((b2b[:, ot * 128:(ot + 1) * 128],
                                              selTb[:, tc * 512:(tc + 1) * 512]))
                                pipe.step(pairs, [
                                    ("s", lambda sE, ps, ot=ot, tc=tc, outT=outT:
                                     sE.copy(outT[:, ot, tc * 512:(tc + 1) * 512],
                                             ps[:])),
                                ], extra_waits=[("t", "VV", vv_w2)])
                    else:
                        for oc in range(2):
                            for half in range(4):
                                r.wait("y", "VV", r.cnt["VV"])
                                r.wait("y", "MM", r.cnt["MM"])
                                ldw = r.dma("y", wsl_f[:],
                                            wd[p + "_w2"].rearrange(
                                                "(lht lp) h o -> (lp h) lht o", lp=2)
                                            [:, :, oc * 512 + half * 128:
                                             oc * 512 + (half + 1) * 128])
                                r.wait("v", "LD", ldw)
                                r.add("v", lambda v, half=half: v.tensor_copy(
                                    w2vsl[:, :, half * 128:(half + 1) * 128],
                                    wsl_f[:]), inc="VV")
                            vv_w2 = r.cnt["VV"]
                            for tt in range(NT):
                                pairs = [(hidT[:, lht, tt * 128:(tt + 1) * 128],
                                          w2vsl[:, lht, :]) for lht in range(8)]
                                pairs.append((selTb[:, tt * 128:(tt + 1) * 128],
                                              b2b[:, oc * 512:(oc + 1) * 512]))
                                pipe.step(pairs, [
                                    ("s", lambda sE, ps, tt=tt, oc=oc:
                                     sE.copy(vnat[:, tt, oc * 512:(oc + 1) * 512],
                                             ps[:])),
                                ], extra_waits=[("t", "VV", vv_w2)])
                    r.barrier()
                r.barrier()
            with contextlib.ExitStack() as ax:
                a4 = ax.enter_context
                esb = a4(nc.sbuf_tensor(f"esb{b}", [128, T], F32))
                attnb = a4(nc.sbuf_tensor(f"attnb{b}", [128, T], BF16))
                attnT = a4(nc.sbuf_tensor(f"attnT{b}", [128, NT, 128], BF16))
                psc = a4(nc.psum_tensor(f"psc{b}", [128, 2048], F32))
                ph0 = a4(nc.psum_tensor(f"ph0_{b}", [128, 512], F32))
                ph1 = a4(nc.psum_tensor(f"ph1_{b}", [128, 512], F32))
                emit_l0ffn_attention(r, cc, qT, kT, vnat, x[b], xo[b],
                                     psc, ph0, ph1, esb, attnb, attnT)
                r.barrier()
            cc.pop("vv_ln", None)
        r.wait("y", "ST", r.cnt["ST"])
        r.barrier()
        r.finalize(block)
    stack.close()
    return nc


def emit_l0ffn_attention(r, cc, qT, kT, vnat, x_b, xo_b, psc, ph0, ph1,
                         esb, attnb, attnT):
    cols, sq = cc["cols"], cc["sq"]
    xt = cc["xt"]
    psc_gate = 0
    ph_gate = [0, 0]
    pb_gate = [0, 0]
    st_before = {}
    for qt in range(NT):
        r.wait("t", "EV", psc_gate)
        for sc in range(4):
            pairs = [(qT[:, dt, qt * 128:(qt + 1) * 128],
                      kT[:, dt, sc * 512:(sc + 1) * 512]) for dt in range(ND)]
            mm_group(r, psc[:, sc * 512:(sc + 1) * 512], pairs, inc="MM")
        mmv = r.cnt["MM"]
        r.wait("v", "MM", mmv)
        r.addf("v", lambda v: v.reduce_max(cols[:, 0:1], psc[:], axis=mybir.AxisListType.X))
        r.addf("v", lambda v: v.tensor_scalar(out=cols[:, 1:2], in0=cols[:, 0:1],
                                              scalar1=-1.0 / 32.0, scalar2=None,
                                              op0=ALU.mult))
        r.wait("s", "VV", r.cnt["VV"])
        r.wait("s", "MM", mmv)
        r.add("s", lambda sE: sE.activation(esb[:], psc[:], AF.Exp,
                                            bias=cols[:, 1:2], scale=1.0 / 32.0),
              inc="EV", n=4)
        psc_gate = r.cnt["EV"]
        r.wait("v", "EV", psc_gate)
        r.addf("v", lambda v: v.reduce_sum(cols[:, 2:3], esb[:], axis=mybir.AxisListType.X))
        r.addf("v", lambda v: v.reciprocal(cols[:, 3:4], cols[:, 2:3]))
        r.wait("s", "VV", r.cnt["VV"])
        r.add("s", lambda sE: sE.activation(attnb[:], esb[:], AF.Copy,
                                            scale=cols[:, 3:4]), inc="SS")
        r.wait("t", "SS", r.cnt["SS"])
        for g4 in range(4):
            pb = cc["ptb"][g4 % 2]
            r.wait("t", "EV", pb_gate[g4 % 2])
            for j in range(4):
                st = g4 * 4 + j
                r.add("t", lambda t, st=st, j=j, pb=pb: t.transpose(
                    pb[:, j * 128:(j + 1) * 128],
                    attnb[:, st * 128:(st + 1) * 128], cc["identb"]),
                    inc=("MM" if j == 3 else None))
            r.wait("v", "MM", r.cnt["MM"])
            r.add("v", lambda v, g4=g4, pb=pb: v.tensor_copy(
                attnT[:, g4 * 4:g4 * 4 + 4, :],
                pb.rearrange("p (a c) -> p a c", a=4)), inc="EV")
            pb_gate[g4 % 2] = r.cnt["EV"]
        xb = xt[qt % 2]
        r.wait("y", "ST", st_before.get(qt - 2, 0))
        r.wait("v", "ST", st_before.get(qt - 1, 0))
        ldx = r.dma("y", xb[:], x_b[qt * 128:(qt + 1) * 128, :])
        att_ev = r.cnt["EV"]
        for oc, ph in ((0, ph0), (1, ph1)):
            r.wait("t", "EV", ph_gate[oc])
            r.wait("t", "EV", att_ev)
            pairs = [(attnT[:, st, :], vnat[:, st, oc * 512:(oc + 1) * 512])
                     for st in range(NT)]
            mm_group(r, ph[:], pairs, inc="MM")
            r.wait("v", "MM", r.cnt["MM"])
            r.wait("v", "LD", ldx)
            r.add("v", lambda v, ph=ph, oc=oc, xb=xb: v.scalar_tensor_tensor(
                out=sq[:, oc * 512:(oc + 1) * 512], in0=ph[:], scalar=0.5,
                in1=xb[:, oc * 512:(oc + 1) * 512], op0=ALU.mult, op1=ALU.add),
                inc="EV")
            ph_gate[oc] = r.cnt["EV"]
        r.wait("g", "EV", r.cnt["EV"])
        st_before[qt] = r.cnt["ST"] + 16
        r.dma("g", xo_b[qt * 128:(qt + 1) * 128, :], sq[:], inc="ST")


def emit_routing(r, S_all, gbits, sv, sel):
    S = lambda j: S_all[:, :, j]

    def fv(build):
        r.addf("v", build)

    fv(lambda v: v.tensor_single_scalar(out=gbits[:, :, 0], in_=S(0), scalar=0.0,
                                        op=ALU.is_ge))
    fv(lambda v: v.tensor_tensor(out=sv[:, :, 0], in0=S(2), in1=S(1), op=ALU.subtract))
    fv(lambda v: v.tensor_tensor(out=sv[:, :, 0], in0=sv[:, :, 0], in1=gbits[:, :, 0], op=ALU.mult))
    fv(lambda v: v.tensor_tensor(out=sv[:, :, 1], in0=S(1), in1=sv[:, :, 0], op=ALU.add))
    fv(lambda v: v.tensor_single_scalar(out=gbits[:, :, 1], in_=sv[:, :, 1], scalar=0.0, op=ALU.is_ge))
    for j, (lo, hi) in enumerate(((3, 4), (5, 6))):
        fv(lambda v, lo=lo, hi=hi, j=j: v.tensor_tensor(out=sv[:, :, 2 + j], in0=S(hi), in1=S(lo), op=ALU.subtract))
        fv(lambda v, j=j: v.tensor_tensor(out=sv[:, :, 2 + j], in0=sv[:, :, 2 + j], in1=gbits[:, :, 1], op=ALU.mult))
        fv(lambda v, lo=lo, j=j: v.tensor_tensor(out=sv[:, :, 2 + j], in0=S(lo), in1=sv[:, :, 2 + j], op=ALU.add))
    fv(lambda v: v.tensor_tensor(out=sv[:, :, 4], in0=sv[:, :, 3], in1=sv[:, :, 2], op=ALU.subtract))
    fv(lambda v: v.tensor_tensor(out=sv[:, :, 4], in0=sv[:, :, 4], in1=gbits[:, :, 0], op=ALU.mult))
    fv(lambda v: v.tensor_tensor(out=sv[:, :, 4], in0=sv[:, :, 2], in1=sv[:, :, 4], op=ALU.add))
    fv(lambda v: v.tensor_single_scalar(out=gbits[:, :, 2], in_=sv[:, :, 4], scalar=0.0, op=ALU.is_ge))
    for j in range(4):
        lo, hi = 7 + 2 * j, 8 + 2 * j
        fv(lambda v, lo=lo, hi=hi, j=j: v.tensor_tensor(out=sv[:, :, 5 + j], in0=S(hi), in1=S(lo), op=ALU.subtract))
        fv(lambda v, j=j: v.tensor_tensor(out=sv[:, :, 5 + j], in0=sv[:, :, 5 + j], in1=gbits[:, :, 2], op=ALU.mult))
        fv(lambda v, lo=lo, j=j: v.tensor_tensor(out=sv[:, :, 5 + j], in0=S(lo), in1=sv[:, :, 5 + j], op=ALU.add))
    for j in range(2):
        a, bb = 5 + 2 * j, 6 + 2 * j
        fv(lambda v, a=a, bb=bb, j=j: v.tensor_tensor(out=sv[:, :, 9 + j], in0=sv[:, :, bb], in1=sv[:, :, a], op=ALU.subtract))
        fv(lambda v, j=j: v.tensor_tensor(out=sv[:, :, 9 + j], in0=sv[:, :, 9 + j], in1=gbits[:, :, 1], op=ALU.mult))
        fv(lambda v, a=a, j=j: v.tensor_tensor(out=sv[:, :, 9 + j], in0=sv[:, :, a], in1=sv[:, :, 9 + j], op=ALU.add))
    fv(lambda v: v.tensor_tensor(out=sv[:, :, 11], in0=sv[:, :, 10], in1=sv[:, :, 9], op=ALU.subtract))
    fv(lambda v: v.tensor_tensor(out=sv[:, :, 11], in0=sv[:, :, 11], in1=gbits[:, :, 0], op=ALU.mult))
    fv(lambda v: v.tensor_tensor(out=sv[:, :, 11], in0=sv[:, :, 9], in1=sv[:, :, 11], op=ALU.add))
    fv(lambda v: v.tensor_single_scalar(out=gbits[:, :, 3], in_=sv[:, :, 11], scalar=0.0, op=ALU.is_ge))
    for j in range(4):
        fv(lambda v, j=j: v.tensor_scalar(out=sv[:, :, 12 + j], in0=gbits[:, :, j],
                                          scalar1=-1.0, scalar2=1.0,
                                          op0=ALU.mult, op1=ALU.add))
    gb = lambda j: gbits[:, :, j]
    gc = lambda j: sv[:, :, 12 + j]
    for i2 in range(4):
        b1_, b0_ = i2 // 2, i2 % 2
        t0 = gb(0) if b1_ else gc(0)
        t1 = gb(1) if b0_ else gc(1)
        fv(lambda v, i2=i2, t0=t0, t1=t1: v.tensor_tensor(out=sv[:, :, 16 + i2], in0=t0, in1=t1, op=ALU.mult))
    for leaf in range(16):
        i2 = leaf // 4
        b2_, b3_ = (leaf // 2) % 2, leaf % 2
        t2 = gb(2) if b2_ else gc(2)
        t3 = gb(3) if b3_ else gc(3)
        fv(lambda v, leaf=leaf, i2=i2, t2=t2: v.tensor_tensor(out=sel[:, :, leaf], in0=sv[:, :, 16 + i2], in1=t2, op=ALU.mult))
        fv(lambda v, leaf=leaf, t3=t3: v.tensor_tensor(out=sel[:, :, leaf], in0=sel[:, :, leaf], in1=t3, op=ALU.mult))



def emit_mlstm_mixer(r, cc, b, qT, kTt, vext, ogs, lnwb, aT, mT, enT,
                     scr_m, mseg, colfT, mrefc, mbsb, zsc, ezb, matcT, hh, cenh,
                     sqh, ecols, houtn, houtT, woutT, tril01, maskn,
                     onesf, pd0, pd1, phh, pmb, pfs, x_b, xo_b):
    DHS = 0.08838834764831845
    cols, sq, xt = cc["cols"], cc["sq"], cc["xt"]
    phh_gate = 0
    pmb_gate = 0
    pd_gate = [0, 0]
    hmat_gate = 0
    mseg_gate = 0
    st_before = {}
    pf_gate = [0, 0]
    final_mm_gate = 0
    for qt in range(NT):
        for h in range(8):
            r.wait("y", "MM", mseg_gate)
            ldm = r.dma("y", mseg[:], scr_m[h:h + 1, qt * 128:(qt + 1) * 128])
            r.wait("t", "LD", ldm)
            r.wait("t", "EV", pmb_gate)
            r.add("t", lambda t: t.matmul(pmb[:], onesf[:], mseg[:],
                                          start=True, stop=True), inc="MM")
            mseg_gate = r.cnt["MM"]
            r.wait("v", "MM", r.cnt["MM"])
            r.add("v", lambda v: v.tensor_copy(mbsb[:], pmb[:]))
            r.addf("v", lambda v: v.tensor_scalar(out=mrefc[:, 0:1], in0=pmb[:, 0:1],
                                                  scalar1=-1.0, scalar2=None,
                                                  op0=ALU.mult))
            r.addf("v", lambda v: v.tensor_copy(mrefc[:, 1:2], pmb[:, 0:1]))
            r.add("v", lambda v: v.tensor_copy(r.dummy2[:], r.dummy1[:]), inc="EV")
            pmb_gate = r.cnt["EV"]
            r.wait("s", "VV", r.cnt["VV"])
            r.addf("s", lambda sE, h=h: sE.activation(colfT[:], aT[:, :, h],
                                                      AF.Exp, bias=mrefc[:, 0:1]))
            r.addf("s", lambda sE, h=h, qt=qt: sE.activation(
                ecols[:, 0:1], mT[:, qt, h:h + 1], AF.Exp,
                bias=mrefc[:, 1:2], scale=-1.0))
            r.wait("v", "SS", r.cnt["SS"])
            r.wait("t", "SS", r.cnt["SS"])
            for st in range(qt + 1):
                pd = pd0 if st % 2 == 0 else pd1
                r.wait("t", "EV", pd_gate[st % 2])
                r.add("t", lambda t, st=st, h=h, qt=qt, pd=pd: t.matmul(
                    pd[:], kTt[:, st * 128:(st + 1) * 128],
                    qT[:, h, qt * 128:(qt + 1) * 128], start=True, stop=True),
                    inc="MM")
                mmv = r.cnt["MM"]
                r.wait("v", "MM", mmv)
                r.wait("v", "MM", hmat_gate)
                if st < qt:
                    r.add("v", lambda v, st=st, pd=pd: v.tensor_scalar(
                        out=matcT[:, st, :], in0=pd[:], scalar1=colfT[:, st:st + 1],
                        scalar2=DHS, op0=ALU.mult, op1=ALU.mult), inc="EV")
                else:
                    r.add("v", lambda v, qt=qt, h=h: v.tensor_scalar(
                        out=zsc[:], in0=mbsb[:], scalar1=-1.0,
                        scalar2=aT[:, qt, h:h + 1], op0=ALU.mult, op1=ALU.add))
                    r.add("v", lambda v: v.tensor_mul(zsc[:], zsc[:], tril01[:]))
                    r.add("v", lambda v: v.tensor_add(zsc[:], zsc[:], maskn[:]),
                          inc="VV")
                    r.wait("s", "VV", r.cnt["VV"])
                    r.add("s", lambda sE: sE.activation(ezb[:], zsc[:], AF.Exp),
                          inc="SS")
                    r.wait("v", "SS", r.cnt["SS"])
                    r.add("v", lambda v, qt=qt, pd=pd: v.scalar_tensor_tensor(
                        out=matcT[:, qt, :], in0=pd[:], scalar=DHS, in1=ezb[:],
                        op0=ALU.mult, op1=ALU.mult), inc="EV")
                pd_gate[st % 2] = r.cnt["EV"]
            matc_ev = r.cnt["EV"]
            r.wait("t", "EV", phh_gate)
            r.wait("t", "EV", matc_ev)
            n_groups = 0
            if qt > 0:
                pairs = [(matcT[:, st, :], vext[:, st, :]) for st in range(qt)]
                mm_group(r, phh[:, 0:129], pairs, inc="MM")
                n_groups += 1
            mm_group(r, phh[:, 129:258], [(matcT[:, qt, :], vext[:, qt, :])],
                     inc="MM")
            n_groups += 1
            hmat_gate = r.cnt["MM"]
            r.wait("v", "MM", r.cnt["MM"])
            if qt > 0:
                r.addf("v", lambda v: v.tensor_scalar(out=ecols[:, 1:2],
                                                      in0=phh[:, 128:129],
                                                      scalar1=ecols[:, 0:1],
                                                      scalar2=None, op0=ALU.mult))
                r.addf("v", lambda v: v.tensor_add(ecols[:, 1:2], ecols[:, 1:2],
                                                   phh[:, 257:258]))
            else:
                r.addf("v", lambda v: v.tensor_copy(ecols[:, 1:2], phh[:, 257:258]))
            r.wait("s", "VV", r.cnt["VV"])
            r.addf("s", lambda sE: sE.activation(ecols[:, 2:3], ecols[:, 1:2], AF.Abs))
            r.wait("v", "SS", r.cnt["SS"])
            r.addf("v", lambda v, qt=qt, h=h: v.tensor_tensor(
                out=ecols[:, 3:4], in0=ecols[:, 2:3], in1=enT[:, qt, h:h + 1],
                op=ALU.max))
            r.addf("v", lambda v: v.tensor_scalar(out=ecols[:, 4:5], in0=ecols[:, 3:4],
                                                  scalar1=1e-6, scalar2=None,
                                                  op0=ALU.add))
            r.addf("v", lambda v: v.reciprocal(ecols[:, 5:6], ecols[:, 4:5]))
            r.addf("v", lambda v: v.tensor_tensor(out=ecols[:, 6:7], in0=ecols[:, 5:6],
                                                  in1=ecols[:, 0:1], op=ALU.mult))
            if qt > 0:
                r.add("v", lambda v: v.tensor_scalar(out=hh[:], in0=phh[:, 0:128],
                                                     scalar1=ecols[:, 6:7],
                                                     scalar2=None, op0=ALU.mult))
                r.add("v", lambda v: v.scalar_tensor_tensor(
                    out=hh[:], in0=phh[:, 129:257], scalar=ecols[:, 5:6], in1=hh[:],
                    op0=ALU.mult, op1=ALU.add), inc="EV", n=2)
            else:
                r.add("v", lambda v: v.tensor_scalar(out=hh[:], in0=phh[:, 129:257],
                                                     scalar1=ecols[:, 5:6],
                                                     scalar2=None, op0=ALU.mult),
                      inc="EV")
            phh_gate = r.cnt["EV"]
            r.addf("v", lambda v: v.reduce_sum(ecols[:, 7:8], hh[:],
                                               axis=mybir.AxisListType.X))
            r.addf("v", lambda v: v.tensor_scalar(out=ecols[:, 8:9], in0=ecols[:, 7:8],
                                                  scalar1=1.0 / 128, scalar2=None,
                                                  op0=ALU.mult))
            r.add("v", lambda v: v.tensor_scalar(out=cenh[:], in0=hh[:],
                                                 scalar1=ecols[:, 8:9], scalar2=None,
                                                 op0=ALU.subtract))
            r.add("v", lambda v: v.tensor_mul(sqh[:], cenh[:], cenh[:]))
            r.addf("v", lambda v: v.reduce_sum(ecols[:, 9:10], sqh[:],
                                               axis=mybir.AxisListType.X))
            r.addf("v", lambda v: v.tensor_scalar(out=ecols[:, 10:11],
                                                  in0=ecols[:, 9:10],
                                                  scalar1=1.0 / 128, scalar2=1e-6,
                                                  op0=ALU.mult, op1=ALU.add))
            r.wait("s", "VV", r.cnt["VV"])
            r.addf("s", lambda sE: sE.activation(ecols[:, 11:12], ecols[:, 10:11],
                                                 AF.Ln))
            r.addf("s", lambda sE: sE.activation(ecols[:, 12:13], ecols[:, 11:12],
                                                 AF.Exp, scale=-0.5))
            r.wait("v", "SS", r.cnt["SS"])
            r.add("v", lambda v: v.tensor_scalar(out=sqh[:], in0=cenh[:],
                                                 scalar1=ecols[:, 12:13],
                                                 scalar2=None, op0=ALU.mult))
            r.add("v", lambda v, h=h: v.tensor_mul(sqh[:], sqh[:], lnwb[:, h, :]))
            r.add("v", lambda v, h=h, qt=qt: v.tensor_tensor(
                out=houtn[:, h * 128:(h + 1) * 128], in0=sqh[:],
                in1=ogs[:, qt, h * 128:(h + 1) * 128], op=ALU.mult), inc="VV")
        r.wait("t", "VV", r.cnt["VV"])
        for half in range(2):
            pb = cc["ptb"][half]
            r.wait("v", "MM", final_mm_gate)
            r.wait("t", "EV", r.cnt["MM"] + 1 - 2)
            for j in range(4):
                dt = half * 4 + j
                r.add("t", lambda t, dt=dt, j=j, pb=pb: t.transpose(
                    pb[:, j * 128:(j + 1) * 128],
                    houtn[:, dt * 128:(dt + 1) * 128], cc["identb"]),
                    inc=("MM" if j == 3 else None))
            r.wait("v", "MM", r.cnt["MM"])
            r.add("v", lambda v, half=half, pb=pb: v.tensor_copy(
                houtT[:, half * 4:half * 4 + 4, :],
                pb.rearrange("p (a c) -> p a c", a=4)), inc="EV")
        hout_ev = r.cnt["EV"]
        xb = xt[qt % 2]
        r.wait("y", "ST", st_before.get(qt - 2, 0))
        r.wait("v", "ST", st_before.get(qt - 1, 0))
        ldx = r.dma("y", xb[:], x_b[qt * 128:(qt + 1) * 128, :])
        for oc in range(2):
            pf = pfs[oc]
            r.wait("t", "EV", pf_gate[oc])
            r.wait("t", "EV", hout_ev)
            pairs = [(houtT[:, dt, :], woutT[:, dt, oc * 512:(oc + 1) * 512])
                     for dt in range(ND)]
            mm_group(r, pf[:], pairs, inc="MM")
            r.wait("v", "MM", r.cnt["MM"])
            r.wait("v", "LD", ldx)
            r.add("v", lambda v, pf=pf, oc=oc, xb=xb: v.tensor_tensor(
                out=sq[:, oc * 512:(oc + 1) * 512], in0=pf[:],
                in1=xb[:, oc * 512:(oc + 1) * 512], op=ALU.add), inc="EV")
            pf_gate[oc] = r.cnt["EV"]
            final_mm_gate = r.cnt["MM"]
        r.wait("g", "EV", r.cnt["EV"])
        st_before[qt] = r.cnt["ST"] + 16
        r.dma("g", xo_b[qt * 128:(qt + 1) * 128, :], sq[:], inc="ST")


def build_l0attn():
    nc = bass.Bass("TRN2", target_bir_lowering=False)
    x = nc.dram_tensor("x", [B_LOC, T, D], F32, kind="ExternalInput").ap()
    xo = nc.dram_tensor("xo", [B_LOC, T, D], F32, kind="ExternalOutput").ap()
    Wq = nc.dram_tensor("L0_attn_Wq", [D, D], F32, kind="ExternalInput").ap()
    Wk = nc.dram_tensor("L0_attn_Wk", [128, D], F32, kind="ExternalInput").ap()
    Wv = nc.dram_tensor("L0_attn_Wv", [128, D], F32, kind="ExternalInput").ap()
    Wog = nc.dram_tensor("L0_attn_Wog", [D, D], F32, kind="ExternalInput").ap()
    Wi = nc.dram_tensor("L0_attn_Wi", [8, D], F32, kind="ExternalInput").ap()
    bi = nc.dram_tensor("L0_attn_bi", [1, 8], F32, kind="ExternalInput").ap()
    Wf = nc.dram_tensor("L0_attn_Wf", [8, D], F32, kind="ExternalInput").ap()
    bf_ = nc.dram_tensor("L0_attn_bf", [1, 8], F32, kind="ExternalInput").ap()
    lnw = nc.dram_tensor("L0_attn_ln_w", [1, D], F32, kind="ExternalInput").ap()
    Wout = nc.dram_tensor("L0_attn_Wout", [D, D], F32, kind="ExternalInput").ap()
    identb_d = nc.dram_tensor("c_identb", [128, 128], BF16, kind="ExternalInput").ap()
    identf_d = nc.dram_tensor("c_identf", [128, 128], F32, kind="ExternalInput").ap()
    ones_d = nc.dram_tensor("c_ones", [1, 128], F32, kind="ExternalInput").ap()
    tril01_d = nc.dram_tensor("c_tril01", [128, 128], F32, kind="ExternalInput").ap()
    maskn_d = nc.dram_tensor("c_maskadd", [128, 128], F32, kind="ExternalInput").ap()
    scr_m = nc.dram_tensor("scr_m", [8, T], F32, kind="Internal").ap()

    import contextlib
    r = R(nc)
    stack, handles = common_sems(nc)
    al = stack.enter_context
    identb = al(nc.sbuf_tensor("identb", [128, 128], BF16))
    identf = al(nc.sbuf_tensor("identf", [128, 128], F32))
    onesf = al(nc.sbuf_tensor("onesf", [1, 128], F32))
    tril01 = al(nc.sbuf_tensor("tril01", [128, 128], F32))
    maskn = al(nc.sbuf_tensor("maskn", [128, 128], F32))
    dummy1 = al(nc.sbuf_tensor("dummy1", [1, 4], F32))
    dummy2 = al(nc.sbuf_tensor("dummy2", [1, 4], F32))
    wkT = al(nc.sbuf_tensor("wkT", [128, 8, 128], BF16))
    wvT = al(nc.sbuf_tensor("wvT", [128, 8, 128], BF16))
    wiT = al(nc.sbuf_tensor("wiT", [128, 8, 8], BF16))
    wfT = al(nc.sbuf_tensor("wfT", [128, 8, 8], BF16))
    lnwb = al(nc.sbuf_tensor("lnwb", [128, 8, 128], BF16))
    bcols = al(nc.sbuf_tensor("bcols", [8, 4], F32))
    wstage = al(nc.sbuf_tensor("wstage", [128, 1024], F32))
    wstage_b = al(nc.sbuf_tensor("wstage_b", [128, 1024], BF16))
    xt0 = al(nc.sbuf_tensor("xt0", [128, 1024], F32))
    xt1 = al(nc.sbuf_tensor("xt1", [128, 1024], F32))
    cen = al(nc.sbuf_tensor("cen", [128, 1024], F32))
    sq = al(nc.sbuf_tensor("sq", [128, 1024], F32))
    cols = al(nc.sbuf_tensor("cols", [128, 16], F32))
    xlnb0 = al(nc.sbuf_tensor("xlnb0", [128, 1024], BF16))
    xlnb1 = al(nc.sbuf_tensor("xlnb1", [128, 1024], BF16))
    xlnT = al(nc.sbuf_tensor("xlnT", [128, 8, T], BF16))
    qT = al(nc.sbuf_tensor("qT", [128, 8, T], BF16))
    ogs = al(nc.sbuf_tensor("ogs", [128, NT, D], BF16))
    kTt = al(nc.sbuf_tensor("kTt", [128, T], BF16))
    vext = al(nc.sbuf_tensor("vext", [128, NT, 129], BF16))
    gA = al(nc.sbuf_tensor("gA", [40, T], F32))
    gB = al(nc.sbuf_tensor("gB", [8, T], F32))
    gC = al(nc.sbuf_tensor("gC", [8, T], F32))
    gD = al(nc.sbuf_tensor("gD", [8, T], F32))
    aT = al(nc.sbuf_tensor("aT", [128, 16, 8], F32))
    mT = al(nc.sbuf_tensor("mT", [128, 16, 8], F32))
    enT = al(nc.sbuf_tensor("enT", [128, 16, 8], F32))
    colfT = al(nc.sbuf_tensor("colfT", [128, 16], F32))
    mseg = al(nc.sbuf_tensor("mseg", [1, 128], F32))
    mrefc = al(nc.sbuf_tensor("mrefc", [128, 2], F32))
    mbsb = al(nc.sbuf_tensor("mbsb", [128, 128], F32))
    zsc = al(nc.sbuf_tensor("zsc", [128, 128], F32))
    ezb = al(nc.sbuf_tensor("ezb", [128, 128], BF16))
    matcT = al(nc.sbuf_tensor("matcT", [128, NT, 128], BF16))
    hh = al(nc.sbuf_tensor("hh", [128, 128], F32))
    cenh = al(nc.sbuf_tensor("cenh", [128, 128], F32))
    sqh = al(nc.sbuf_tensor("sqh", [128, 128], F32))
    ecols = al(nc.sbuf_tensor("ecols", [128, 16], F32))
    houtn = al(nc.sbuf_tensor("houtn", [128, 1024], BF16))
    houtT = al(nc.sbuf_tensor("houtT", [128, 8, 128], BF16))
    woutT = al(nc.sbuf_tensor("woutT", [128, 8, D], BF16))
    pb0 = al(nc.psum_tensor("pb0", [128, 512], BF16))
    pb1 = al(nc.psum_tensor("pb1", [128, 512], BF16))
    pf0 = al(nc.psum_tensor("pf0", [128, 512], F32))
    pf1 = al(nc.psum_tensor("pf1", [128, 512], F32))
    pd0 = al(nc.psum_tensor("pd0", [128, 128], F32))
    pd1 = al(nc.psum_tensor("pd1", [128, 128], F32))
    phh = al(nc.psum_tensor("phh", [128, 258], F32))
    pmb = al(nc.psum_tensor("pmb", [128, 128], F32))

    with nc.Block() as block:
        r.setup_sems(handles)
        r.dummy1, r.dummy2 = dummy1, dummy2
        cc = {"xt": [xt0, xt1], "cen": cen, "sq": sq, "cols": cols,
              "ptb": [pb0, pb1], "xlnb": [xlnb0, xlnb1], "identb": identb[:],
              "wstage": wstage, "wstage_b": wstage_b}
        pfs = [pf0, pf1]

        r.dma("y", identb[:], identb_d)
        r.dma("y", identf[:], identf_d)
        r.dma("y", onesf[:], ones_d)
        r.dma("y", tril01[:], tril01_d)
        r.dma("y", maskn[:], maskn_d)
        emit_weight_T(r, Wk, 128, wkT, cc)
        emit_weight_T(r, Wv, 128, wvT, cc)
        emit_weight_T(r, Wi, 8, wiT, cc)
        emit_weight_T(r, Wf, 8, wfT, cc)
        r.wait("y", "VV", r.cnt["VV"])
        ldb = r.dma("y", wstage[0:1, 0:8], bi)
        r.wait("t", "LD", ldb)
        r.wait("t", "EV", r.cnt["MM"])
        r.add("t", lambda t: t.matmul(pf0[0:8, 0:1], wstage[0:1, 0:8],
                                      onesf[0:1, 0:1], start=True, stop=True), inc="MM")
        r.wait("v", "MM", r.cnt["MM"])
        r.addf("v", lambda v: v.tensor_scalar(out=bcols[:, 0:1], in0=pf0[0:8, 0:1],
                                              scalar1=1.0 / 15.0, scalar2=None,
                                              op0=ALU.mult))
        r.add("v", lambda v: v.tensor_copy(r.dummy2[:], r.dummy1[:]), inc="EV")
        r.wait("y", "MM", r.cnt["MM"])
        ldb = r.dma("y", wstage[0:1, 0:8], bf_)
        r.wait("t", "LD", ldb)
        r.wait("t", "EV", r.cnt["EV"])
        r.add("t", lambda t: t.matmul(pf1[0:8, 0:1], wstage[0:1, 0:8],
                                      onesf[0:1, 0:1], start=True, stop=True), inc="MM")
        r.wait("v", "MM", r.cnt["MM"])
        r.addf("v", lambda v: v.tensor_scalar(out=bcols[:, 1:2], in0=pf1[0:8, 0:1],
                                              scalar1=1.0 / 15.0, scalar2=None,
                                              op0=ALU.mult))
        r.add("v", lambda v: v.tensor_copy(r.dummy2[:], r.dummy1[:]), inc="EV")
        r.wait("y", "MM", r.cnt["MM"])
        ldl = r.dma("y", wstage[0:1, :], lnw)
        r.wait("t", "LD", ldl)
        for h in range(8):
            r.wait("t", "EV", r.cnt["MM"] + 1 - 2)
            r.add("t", lambda t, h=h: t.matmul(
                [pf0, pf1][h % 2][:, 0:128], onesf[:],
                wstage[0:1, h * 128:(h + 1) * 128], start=True, stop=True), inc="MM")
            r.wait("v", "MM", r.cnt["MM"])
            r.add("v", lambda v, h=h: v.tensor_copy(lnwb[:, h, :],
                                                    [pf0, pf1][h % 2][:, 0:128]),
                  inc="EV")
        r.barrier()

        for b in range(B_LOC):
            emit_ln_transpose(r, x[b], xlnT, cc)
            r.barrier()
            pipe = PsumPipe(r, pfs)
            emit_weight_T(r, Wq, D, woutT, cc)
            r.barrier()
            for ot in range(ND):
                for tc in range(4):
                    pairs = [(woutT[:, dt, ot * 128:(ot + 1) * 128],
                              xlnT[:, dt, tc * 512:(tc + 1) * 512]) for dt in range(ND)]
                    pipe.step(pairs, [
                        ("s", lambda sE, ps, ot=ot, tc=tc: sE.copy(
                            qT[:, ot, tc * 512:(tc + 1) * 512], ps[:])),
                    ])
            r.barrier()
            emit_weight_T(r, Wog, D, woutT, cc)
            r.barrier()
            for tt in range(NT):
                for oc in range(2):
                    pairs = [(xlnT[:, dt, tt * 128:(tt + 1) * 128],
                              woutT[:, dt, oc * 512:(oc + 1) * 512]) for dt in range(ND)]
                    pipe.step(pairs, [
                        ("s", lambda sE, ps, tt=tt, oc=oc: sE.activation(
                            ogs[:, tt, oc * 512:(oc + 1) * 512], ps[:], AF.Sigmoid)),
                    ])
            r.barrier()
            for tc in range(4):
                pairs = [(wkT[:, dt, :], xlnT[:, dt, tc * 512:(tc + 1) * 512])
                         for dt in range(ND)]
                pipe.step(pairs, [
                    ("s", lambda sE, ps, tc=tc: sE.copy(
                        kTt[:, tc * 512:(tc + 1) * 512], ps[:])),
                ])
            for tt in range(NT):
                pairs = [(xlnT[:, dt, tt * 128:(tt + 1) * 128], wvT[:, dt, :])
                         for dt in range(ND)]
                pipe.step(pairs, [
                    ("s", lambda sE, ps, tt=tt: sE.copy(vext[:, tt, 0:128],
                                                        ps[:, 0:128])),
                ], width=128)
            r.addf("v", lambda v: v.tensor_scalar(
                out=vext[:, :, 128], in0=vext[:, :, 0],
                scalar1=0.0, scalar2=1.0, op0=ALU.mult, op1=ALU.add))
            for gi, (wT, bc) in enumerate(((wiT, 0), (wfT, 1))):
                dst = gA
                base = 0 if gi == 0 else 32
                for tc in range(4):
                    pairs = [(wT[:, dt, :], xlnT[:, dt, tc * 512:(tc + 1) * 512])
                             for dt in range(ND)]
                    pipe.step(pairs, [
                        ("s", lambda sE, ps, base=base, tc=tc, bc=bc: sE.activation(
                            gA[base:base + 8, tc * 512:(tc + 1) * 512], ps[0:8, 0:512],
                            AF.Tanh, bias=bcols[0:8, bc:bc + 1], scale=1.0 / 15.0)),
                    ], width=512, rows=8)
            r.barrier()
            r.add("s", lambda sE: sE.activation(gB[:], gA[32:40, :], AF.Abs,
                                                scale=15.0))
            r.add("s", lambda sE: sE.activation(gB[:], gB[:], AF.Exp, scale=-1.0))
            r.add("s", lambda sE: sE.activation(gB[:], gB[:], AF.Ln, bias=1.0))
            r.add("s", lambda sE: sE.activation(gC[:], gA[32:40, :], AF.Relu,
                                                scale=-15.0), inc="SS")
            r.wait("v", "SS", r.cnt["SS"])
            r.add("v", lambda v: v.tensor_add(gB[:], gB[:], gC[:]))
            r.add("v", lambda v: v.tensor_scalar(out=gC[:], in0=gB[:],
                                                 scalar1=0.0, scalar2=1.0,
                                                 op0=ALU.mult, op1=ALU.add))
            r.add("v", lambda v: v.tensor_tensor_scan(out=gD[:], data0=gC[:],
                                                      data1=gB[:], initial=0.0,
                                                      op0=ALU.mult, op1=ALU.add))
            r.add("v", lambda v: v.scalar_tensor_tensor(out=gC[:], in0=gA[0:8, :],
                                                        scalar=15.0, in1=gD[:],
                                                        op0=ALU.mult, op1=ALU.add))
            r.add("v", lambda v: v.tensor_tensor_scan(out=gA[0:8, :], data0=gC[:],
                                                      data1=gC[:], initial=-1e30,
                                                      op0=ALU.max, op1=ALU.max))
            r.add("v", lambda v: v.tensor_sub(gB[:], gD[:], gA[0:8, :]),
                  inc="VV")
            r.wait("s", "VV", r.cnt["VV"])
            r.add("s", lambda sE: sE.activation(gB[:], gB[:], AF.Exp),
                  inc="SS")
            r.wait("g", "SS", r.cnt["SS"])
            r.wait("v", "SS", r.cnt["SS"])
            r.dma("g", scr_m[:, :], gA[0:8, :], inc="ST")
            for src, dst, ib in ((gC[:], aT, 0), (gA[0:8, :], mT, 0),
                                 (gB[:], enT, 0)):
                for tt in range(NT):
                    r.wait("t", "VV", r.cnt["VV"])
                    r.wait("t", "SS", r.cnt["SS"])
                    r.wait("t", "EV", r.cnt["MM"] + 1 - 2)
                    r.add("t", lambda t, src=src, tt=tt, ib=ib: t.transpose(
                        pd0[:, 0:8] if tt % 2 == 0 else pd1[:, 0:8],
                        src[:, tt * 128:(tt + 1) * 128],
                        identf[ib:ib + 8, ib:ib + 8]), inc="MM")
                    r.wait("v", "MM", r.cnt["MM"])
                    r.add("v", lambda v, dst=dst, tt=tt: v.tensor_copy(
                        dst[:, tt, :], (pd0 if tt % 2 == 0 else pd1)[:, 0:8]),
                        inc="EV")
            r.barrier()
            emit_weight_T(r, Wout, D, woutT, cc)
            r.barrier()
            emit_mlstm_mixer(r, cc, b, qT, kTt, vext, ogs, lnwb, aT, mT, enT,
                             scr_m, mseg, colfT, mrefc, mbsb, zsc, ezb, matcT, hh, cenh,
                             sqh, ecols, houtn, houtT, woutT, tril01, maskn,
                             onesf, pd0, pd1, phh, pmb, pfs, x[b], xo[b])
            r.barrier()
            cc.pop("vv_ln", None)
        r.wait("y", "ST", r.cnt["ST"])
        r.barrier()
        r.finalize(block)
    stack.close()
    return nc



def build_l1attn():
    nc = bass.Bass("TRN2", target_bir_lowering=False)
    HD = 4096
    NDH = 32
    x = nc.dram_tensor("x", [B_LOC, T, D], F32, kind="ExternalInput").ap()
    xo = nc.dram_tensor("xo", [B_LOC, T, D], F32, kind="ExternalOutput").ap()
    ip = nc.dram_tensor("L1_attn_input_proj", [D, HD], F32, kind="ExternalInput").ap()
    op = nc.dram_tensor("L1_attn_output_proj", [HD, D], F32, kind="ExternalInput").ap()
    alog = nc.dram_tensor("L1_attn_a_log", [1, 16], F32, kind="ExternalInput").ap()
    bp = nc.dram_tensor("L1_attn_b_proj", [HD, 16], F32, kind="ExternalInput").ap()
    cp = nc.dram_tensor("L1_attn_c_proj", [HD, 16], F32, kind="ExternalInput").ap()
    dtp = nc.dram_tensor("L1_attn_dt_proj", [1, HD], F32, kind="ExternalInput").ap()
    skp = nc.dram_tensor("L1_attn_skip_proj", [HD, HD], F32, kind="ExternalInput").ap()
    flq = nc.dram_tensor("L1_attn_floquet_phases", [1, HD], F32, kind="ExternalInput").ap()
    identb_d = nc.dram_tensor("c_identb", [128, 128], BF16, kind="ExternalInput").ap()
    ones_d = nc.dram_tensor("c_ones", [1, 128], F32, kind="ExternalInput").ap()
    iota_d = nc.dram_tensor("c_iota", [1, T], F32, kind="ExternalInput").ap()
    scr_hdT = nc.dram_tensor("scr_hdT", [HD, T], BF16, kind="Internal").ap()
    scr_henc = nc.dram_tensor("scr_henc", [HD, T], BF16, kind="Internal").ap()
    scr_yskip = nc.dram_tensor("scr_yskip", [HD, T], BF16, kind="Internal").ap()
    scr_outhd = nc.dram_tensor("scr_outhd", [HD, T], BF16, kind="Internal").ap()
    scr_bc = nc.dram_tensor("scr_bc", [2, 16, T], F32, kind="Internal").ap()

    import contextlib
    r = R(nc)
    stack, handles = common_sems(nc)
    al = stack.enter_context
    identb = al(nc.sbuf_tensor("identb", [128, 128], BF16))
    onesf = al(nc.sbuf_tensor("onesf", [1, 128], F32))
    dummy1 = al(nc.sbuf_tensor("dummy1", [1, 4], F32))
    dummy2 = al(nc.sbuf_tensor("dummy2", [1, 4], F32))
    xt0 = al(nc.sbuf_tensor("xt0", [128, 1024], F32))
    xt1 = al(nc.sbuf_tensor("xt1", [128, 1024], F32))
    cen = al(nc.sbuf_tensor("cen", [128, 1024], F32))
    sq = al(nc.sbuf_tensor("sq", [128, 1024], F32))
    cols = al(nc.sbuf_tensor("cols", [128, 16], F32))
    xlnb0 = al(nc.sbuf_tensor("xlnb0", [128, 1024], BF16))
    xlnb1 = al(nc.sbuf_tensor("xlnb1", [128, 1024], BF16))
    phi2p = al(nc.sbuf_tensor("phi2p", [128, NDH], F32))
    dts = al(nc.sbuf_tensor("dts", [128, NDH], F32))
    dss = al(nc.sbuf_tensor("dss", [128, NDH, 3], F32))
    wtab = al(nc.sbuf_tensor("wtab", [128, NDH, 16], F32))
    abc = al(nc.sbuf_tensor("abc", [128, 16], F32))
    arow = al(nc.sbuf_tensor("arow", [1, 16], F32))
    rrow = al(nc.sbuf_tensor("rrow", [1, 4096], F32))
    iotab = al(nc.sbuf_tensor("iotab", [128, T], F32))
    pb0 = al(nc.psum_tensor("pb0", [128, 512], BF16))
    pb1 = al(nc.psum_tensor("pb1", [128, 512], BF16))
    pf0 = al(nc.psum_tensor("pf0", [128, 512], F32))
    pf1 = al(nc.psum_tensor("pf1", [128, 512], F32))
    pf2 = al(nc.psum_tensor("pf2", [128, 512], F32))
    pf3 = al(nc.psum_tensor("pf3", [128, 512], F32))
    pBc = al(nc.psum_tensor("pBc", [16, 512], F32))
    pCc = al(nc.psum_tensor("pCc", [16, 512], F32))

    with nc.Block() as block:
        r.setup_sems(handles)
        r.dummy1, r.dummy2 = dummy1, dummy2
        cc = {"xt": [xt0, xt1], "cen": cen, "sq": sq, "cols": cols,
              "ptb": [pb0, pb1], "xlnb": [xlnb0, xlnb1], "identb": identb[:]}
        pfs = [pf0, pf1, pf2, pf3]
        TWO_PI = 6.283185307179586
        INV2P = 0.15915494309189535
        RINT_C = 12582912.0

        r.dma("y", identb[:], identb_d)
        r.dma("y", onesf[:], ones_d)
        ldi = r.dma("y", rrow[0:1, 0:T], iota_d)
        r.wait("t", "LD", r.cnt["LD"])
        for tc2 in range(4):
            r.wait("t", "EV", r.cnt["MM"] + 1 - 4)
            r.add("t", lambda t, tc2=tc2: t.matmul(
                pfs[tc2][:], onesf[:], rrow[0:1, tc2 * 512:(tc2 + 1) * 512],
                start=True, stop=True), inc="MM")
            r.wait("v", "MM", r.cnt["MM"])
            r.add("v", lambda v, tc2=tc2: v.tensor_copy(
                iotab[:, tc2 * 512:(tc2 + 1) * 512], pfs[tc2][:]), inc="EV")
        for name, src, dst in (("phi", flq, phi2p), ("dt", dtp, dts)):
            r.wait("y", "VV", r.cnt["VV"])
            r.wait("y", "MM", r.cnt["MM"])
            ldr = r.dma("y", rrow[0:1, 0:HD], src)
            r.wait("t", "LD", ldr)
            for j in range(NDH):
                if j % 16 == 0:
                    r.wait("t", "EV", r.cnt["MM"] + 1 - 2)
                r.add("t", lambda t, j=j: t.matmul(
                    pfs[(j // 16) % 2][:, (j % 16):(j % 16) + 1],
                    rrow[0:1, j * 128:(j + 1) * 128], onesf[0:1, 0:1],
                    start=True, stop=True),
                    inc=("MM" if j % 16 == 15 else None))
                if j % 16 == 15:
                    r.wait("v", "MM", r.cnt["MM"])
                    r.add("v", lambda v, j=j, dst=dst: v.tensor_copy(
                        dst[:, j - 15:j + 1], pfs[(j // 16) % 2][:, 0:16]), inc="EV")
        r.addf("v", lambda v: v.tensor_scalar(out=phi2p[:], in0=phi2p[:],
                                              scalar1=INV2P, scalar2=None,
                                              op0=ALU.mult))
        r.wait("s", "VV", r.cnt["VV"])
        r.addf("s", lambda sE: sE.activation(dss[:, :, 0], dts[:], AF.Abs))
        r.addf("s", lambda sE: sE.activation(dss[:, :, 0], dss[:, :, 0], AF.Exp,
                                             scale=-1.0))
        r.addf("s", lambda sE: sE.activation(dss[:, :, 0], dss[:, :, 0], AF.Ln,
                                             bias=1.0))
        r.addf("s", lambda sE: sE.activation(dss[:, :, 1], dts[:], AF.Relu))
        r.wait("v", "SS", r.cnt["SS"])
        r.addf("v", lambda v: v.tensor_add(dts[:], dss[:, :, 0], dss[:, :, 1]))
        r.wait("y", "MM", r.cnt["MM"])
        lda = r.dma("y", arow[:], alog)
        r.wait("s", "LD", lda)
        r.addf("s", lambda sE: sE.activation(arow[:], arow[:], AF.Exp))
        r.wait("t", "SS", r.cnt["SS"])
        r.wait("t", "EV", r.cnt["MM"] + 1 - 4)
        r.add("t", lambda t: t.matmul(pf0[:, 0:16], onesf[:], arow[:],
                                      start=True, stop=True), inc="MM")
        r.wait("v", "MM", r.cnt["MM"])
        r.addf("v", lambda v: v.tensor_scalar(out=abc[:], in0=pf0[:, 0:16],
                                              scalar1=-1.0, scalar2=None,
                                              op0=ALU.mult))
        r.add("v", lambda v: v.tensor_copy(r.dummy2[:], r.dummy1[:]), inc="EV")
        for dht in range(NDH):
            r.addf("v", lambda v, dht=dht: v.tensor_scalar(
                out=wtab[:, dht, :], in0=abc[:], scalar1=dts[:, dht:dht + 1],
                scalar2=None, op0=ALU.mult))
        r.wait("s", "VV", r.cnt["VV"])
        r.addf("s", lambda sE: sE.activation(
            wtab.rearrange("p a b -> p (a b)"), wtab.rearrange("p a b -> p (a b)"),
            AF.Exp))
        r.barrier()

        for b in range(B_LOC):
            with contextlib.ExitStack() as sA:
                aa = sA.enter_context
                xlnT = aa(nc.sbuf_tensor(f"xlnT{b}", [128, 8, T], BF16))
                emit_ln_transpose(r, x[b], xlnT, cc)
                r.barrier()
                ipf = aa(nc.sbuf_tensor(f"ipf{b}", [128, 8, 128], F32))
                ipb = aa(nc.sbuf_tensor(f"ipb{b}", [128, 8, 128], BF16))
                hdt = aa(nc.sbuf_tensor(f"hdt{b}", [128, T], BF16))
                het = aa(nc.sbuf_tensor(f"het{b}", [128, T], BF16))
                mtl = aa(nc.sbuf_tensor(f"mtl{b}", [128, T], F32))
                rtl = aa(nc.sbuf_tensor(f"rtl{b}", [128, T], F32))
                cosv = aa(nc.sbuf_tensor(f"cosv{b}", [128, T], BF16))
                pipe = PsumPipe(r, pfs)
                st_hd = {}
                for dht in range(NDH):
                    r.wait("y", "VV", r.cnt["VV"])
                    r.wait("y", "MM", r.cnt["MM"])
                    ldw = r.dma("y", ipf[:],
                                ip.rearrange("(dt p) c -> p dt c", p=128)
                                [:, :, dht * 128:(dht + 1) * 128])
                    r.wait("v", "LD", ldw)
                    r.wait("v", "ST", st_hd.get(dht - 2, 0))
                    r.add("v", lambda v: v.tensor_copy(ipb[:], ipf[:]), inc="VV")
                    vv_w = r.cnt["VV"]
                    for tc in range(4):
                        pairs = [(ipb[:, dt, :], xlnT[:, dt, tc * 512:(tc + 1) * 512])
                                 for dt in range(ND)]
                        pipe.step(pairs, [
                            ("s", lambda sE, ps, tc=tc: sE.copy(
                                hdt[:, tc * 512:(tc + 1) * 512], ps[:])),
                        ], extra_waits=[("t", "VV", vv_w),
                                        ("s", "ST", st_hd.get(dht - 1, 0))])
                    r.wait("v", "EV", r.cnt["EV"])
                    r.addf("v", lambda v, dht=dht: v.tensor_scalar(
                        out=mtl[:], in0=iotab[:], scalar1=phi2p[:, dht:dht + 1],
                        scalar2=-0.25, op0=ALU.mult, op1=ALU.add))
                    r.add("v", lambda v: v.tensor_scalar(
                        out=rtl[:], in0=mtl[:], scalar1=RINT_C, scalar2=-RINT_C,
                        op0=ALU.add, op1=ALU.add))
                    r.add("v", lambda v: v.tensor_sub(mtl[:], mtl[:], rtl[:]),
                          inc="VV")
                    r.wait("s", "VV", r.cnt["VV"])
                    r.add("s", lambda sE: sE.activation(cosv[:], mtl[:], AF.Sin,
                                                        scale=-TWO_PI), inc="SS")
                    r.wait("v", "SS", r.cnt["SS"])
                    r.add("v", lambda v: v.tensor_mul(het[:], hdt[:], cosv[:]),
                          inc="VV")
                    r.wait("g", "VV", r.cnt["VV"])
                    r.wait("g", "SS", r.cnt["SS"])
                    r.dma("g", scr_hdT[dht * 128:(dht + 1) * 128, :], hdt[:], inc="ST")
                    r.dma("g", scr_henc[dht * 128:(dht + 1) * 128, :], het[:], inc="ST")
                    st_hd[dht] = r.cnt["ST"]
                r.barrier()
            r.wait("y", "ST", r.cnt["ST"])
            with contextlib.ExitStack() as sC:
                ac = sC.enter_context
                bpb = ac(nc.sbuf_tensor(f"bpb{b}", [128, NDH, 16], BF16))
                bpf = ac(nc.sbuf_tensor(f"bpf{b}", [128, NDH, 16], F32))
                cpb = ac(nc.sbuf_tensor(f"cpb{b}", [128, NDH, 16], BF16))
                het2a = ac(nc.sbuf_tensor(f"het2a{b}", [128, 512], BF16))
                het2b = ac(nc.sbuf_tensor(f"het2b{b}", [128, 512], BF16))
                Bf = ac(nc.sbuf_tensor(f"Bf{b}", [16, T], F32))
                Cf = ac(nc.sbuf_tensor(f"Cf{b}", [16, T], F32))
                ldb = r.dma("y", bpf[:], bp.rearrange("(a p) n -> p a n", p=128))
                r.wait("v", "LD", ldb)
                r.add("v", lambda v: v.tensor_copy(bpb[:], bpf[:]), inc="VV")
                r.wait("y", "VV", r.cnt["VV"])
                ldc = r.dma("y", bpf[:], cp.rearrange("(a p) n -> p a n", p=128))
                r.wait("v", "LD", ldc)
                r.add("v", lambda v: v.tensor_copy(cpb[:], bpf[:]), inc="VV")
                r.wait("t", "VV", r.cnt["VV"])
                gg_flat = []
                for tc in range(4):
                    for dht in range(NDH):
                        buf = het2a if dht % 2 == 0 else het2b
                        gi = tc * NDH + dht
                        r.wait("y", "GG", gg_flat[gi - 2] if gi >= 2 else 0)
                        ldh = r.dma("y", buf[:],
                                    scr_henc[dht * 128:(dht + 1) * 128,
                                             tc * 512:(tc + 1) * 512])
                        r.wait("t", "LD", ldh)
                        if dht == 0:
                            r.wait("t", "EV", r.cnt["EV"])
                        r.add("t", lambda t, dht=dht, buf=buf: t.matmul(
                            pBc[:], bpb[:, dht, :], buf[:],
                            start=(dht == 0), stop=(dht == NDH - 1)),
                            inc=("MM" if dht == NDH - 1 else None))
                        r.add("t", lambda t, dht=dht, buf=buf: t.matmul(
                            pCc[:], cpb[:, dht, :], buf[:],
                            start=(dht == 0), stop=(dht == NDH - 1)), inc="GG")
                        gg_flat.append(r.cnt["GG"])
                    r.wait("v", "MM", r.cnt["MM"])
                    r.add("v", lambda v, tc=tc: v.tensor_copy(
                        Bf[:, tc * 512:(tc + 1) * 512], pBc[:]))
                    r.add("v", lambda v, tc=tc: v.tensor_copy(
                        Cf[:, tc * 512:(tc + 1) * 512], pCc[:]), inc="EV")
                r.wait("g", "EV", r.cnt["EV"])
                r.dma("g", scr_bc[0], Bf[:], inc="ST")
                r.dma("g", scr_bc[1], Cf[:], inc="ST")
                r.barrier()
            r.wait("y", "ST", r.cnt["ST"])
            with contextlib.ExitStack() as sD:
                ad = sD.enter_context
                hdf = ad(nc.sbuf_tensor(f"hdf{b}", [128, NDH, T], BF16))
                skf = ad(nc.sbuf_tensor(f"skf{b}", [128, NDH, 128], F32))
                skb = ad(nc.sbuf_tensor(f"skb{b}", [128, NDH, 128], BF16))
                ysk = ad(nc.sbuf_tensor(f"ysk{b}", [128, T], BF16))
                ldh = r.dma("y", hdf[:], scr_hdT.rearrange("(a p) t -> p a t", p=128))
                pipe = PsumPipe(r, pfs)
                st_sk = {}
                for dho in range(NDH):
                    r.wait("y", "VV", r.cnt["VV"])
                    r.wait("y", "MM", r.cnt["MM"])
                    ldw = r.dma("y", skf[:],
                                skp.rearrange("(a p) c -> p a c", p=128)
                                [:, :, dho * 128:(dho + 1) * 128])
                    r.wait("v", "LD", ldw)
                    r.wait("v", "ST", st_sk.get(dho - 1, 0))
                    r.add("v", lambda v: v.tensor_copy(skb[:], skf[:]), inc="VV")
                    vv_w = r.cnt["VV"]
                    if dho == 0:
                        r.wait("t", "LD", ldh)
                    for tc in range(4):
                        pairs = [(skb[:, a, :], hdf[:, a, tc * 512:(tc + 1) * 512])
                                 for a in range(NDH)]
                        pipe.step(pairs, [
                            ("s", lambda sE, ps, tc=tc: sE.copy(
                                ysk[:, tc * 512:(tc + 1) * 512], ps[:])),
                        ], extra_waits=[("t", "VV", vv_w)])
                    r.wait("g", "EV", r.cnt["EV"])
                    r.dma("g", scr_yskip[dho * 128:(dho + 1) * 128, :], ysk[:],
                          inc="ST")
                    st_sk[dho] = r.cnt["ST"]
                r.barrier()
            r.wait("y", "ST", r.cnt["ST"])
            with contextlib.ExitStack() as sE_:
                ae = sE_.enter_context
                ygrp = ae(nc.sbuf_tensor(f"ygrp{b}", [128, 4, T], F32))
                hencg = ae(nc.sbuf_tensor(f"hencg{b}", [128, 4, T], BF16))
                Bbc = ae(nc.sbuf_tensor(f"Bbc{b}", [128, T], BF16))
                Cbc = ae(nc.sbuf_tensor(f"Cbc{b}", [128, T], BF16))
                wt = ae(nc.sbuf_tensor(f"wt{b}", [128, T], F32))
                ut = ae(nc.sbuf_tensor(f"ut{b}", [128, T], F32))
                Ft = ae(nc.sbuf_tensor(f"Ft{b}", [128, T], F32))
                tmpt = ae(nc.sbuf_tensor(f"tmpt{b}", [128, T], F32))
                yskg = ae(nc.sbuf_tensor(f"yskg{b}", [128, 4, T], BF16))
                emit_ssm_scan(r, b, ygrp, hencg, Bbc, Cbc, wt, ut, Ft, tmpt, yskg,
                              rrow, onesf, wtab, dts, pfs, scr_henc, scr_yskip,
                              scr_outhd, scr_bc)
                r.barrier()
            r.wait("y", "ST", r.cnt["ST"])
            with contextlib.ExitStack() as sF:
                af = sF.enter_context
                opf = af(nc.sbuf_tensor(f"opf{b}", [128, NDH, 512], F32))
                opb = af(nc.sbuf_tensor(f"opb{b}", [128, NDH, D], BF16))
                otl = af(nc.sbuf_tensor(f"otl{b}", [128, NDH, 128], BF16))
                for oc in range(2):
                    ldo = r.dma("y", opf[:],
                                op.rearrange("(a p) e -> p a e", p=128)
                                [:, :, oc * 512:(oc + 1) * 512])
                    r.wait("v", "LD", ldo)
                    r.add("v", lambda v, oc=oc: v.tensor_copy(
                        opb[:, :, oc * 512:(oc + 1) * 512], opf[:]), inc="VV")
                    if oc == 0:
                        r.wait("y", "VV", r.cnt["VV"])
                r.wait("t", "VV", r.cnt["VV"])
                pipe = PsumPipe(r, pfs)
                st_f = {}
                for tt in range(NT):
                    r.wait("y", "MM", r.cnt["MM"])
                    ldt = r.dma("y", otl[:],
                                scr_outhd.rearrange("(a p) t -> p a t", p=128)
                                [:, :, tt * 128:(tt + 1) * 128])
                    xb = xt0 if tt % 2 == 0 else xt1
                    r.wait("y", "ST", st_f.get(tt - 2, 0))
                    r.wait("v", "ST", st_f.get(tt - 1, 0))
                    ldx = r.dma("y", xb[:], x[b, tt * 128:(tt + 1) * 128, :])
                    for oc in range(2):
                        pairs = [(otl[:, a, :], opb[:, a, oc * 512:(oc + 1) * 512])
                                 for a in range(NDH)]
                        pipe.step(pairs, [
                            ("v", lambda v, ps, oc=oc, xb=xb: v.tensor_tensor(
                                out=sq[:, oc * 512:(oc + 1) * 512], in0=ps[:],
                                in1=xb[:, oc * 512:(oc + 1) * 512], op=ALU.add)),
                        ], extra_waits=[("t", "LD", ldt), ("v", "LD", ldx)])
                    r.wait("g", "EV", r.cnt["EV"])
                    st_f[tt] = r.cnt["ST"] + 16
                    r.dma("g", xo[b, tt * 128:(tt + 1) * 128, :], sq[:], inc="ST")
                r.barrier()
            cc.pop("vv_ln", None)
        r.wait("y", "ST", r.cnt["ST"])
        r.barrier()
        r.finalize(block)
    stack.close()
    return nc


def emit_ssm_scan(r, b, ygrp, hencg, Bbc, Cbc, wt, ut, Ft, tmpt, yskg,
                  rrow, onesf, wtab, dts, pfs, scr_henc, scr_yskip,
                  scr_outhd, scr_bc):
    henc_ev = 0
    st_out = 0
    for grp in range(8):
        r.wait("y", "VV", henc_ev)
        r.wait("y", "ST", st_out)
        ldh = r.dma("y", hencg[:],
                    scr_henc.rearrange("(g j p) t -> p (g j) t", p=128, j=4)
                    [:, grp * 4:(grp + 1) * 4, :])
        lds = r.dma("y", yskg[:],
                    scr_yskip.rearrange("(g j p) t -> p (g j) t", p=128, j=4)
                    [:, grp * 4:(grp + 1) * 4, :])
        for n in range(16):
            r.wait("y", "MM", r.cnt["MM"])
            ldr = r.dma("y", rrow[0:1, 0:T], scr_bc[0, n:n + 1, :])
            r.wait("t", "LD", ldr)
            r.wait("t", "EV", r.cnt["EV"])
            for tc in range(4):
                r.add("t", lambda t, tc=tc: t.matmul(
                    pfs[tc][:], onesf[:], rrow[0:1, tc * 512:(tc + 1) * 512],
                    start=True, stop=True), inc=("MM" if tc == 3 else None))
            r.wait("v", "MM", r.cnt["MM"])
            for tc in range(4):
                r.add("v", lambda v, tc=tc: v.tensor_copy(
                    Bbc[:, tc * 512:(tc + 1) * 512], pfs[tc][:]),
                    inc=("EV" if tc == 3 else None))
            r.wait("y", "MM", r.cnt["MM"])
            ldr = r.dma("y", rrow[0:1, 0:T], scr_bc[1, n:n + 1, :])
            r.wait("t", "LD", ldr)
            r.wait("t", "EV", r.cnt["EV"])
            for tc in range(4):
                r.add("t", lambda t, tc=tc: t.matmul(
                    pfs[tc][:], onesf[:], rrow[0:1, tc * 512:(tc + 1) * 512],
                    start=True, stop=True), inc=("MM" if tc == 3 else None))
            r.wait("v", "MM", r.cnt["MM"])
            for tc in range(4):
                r.add("v", lambda v, tc=tc: v.tensor_copy(
                    Cbc[:, tc * 512:(tc + 1) * 512], pfs[tc][:]),
                    inc=("EV" if tc == 3 else None))
            if n == 0:
                r.wait("v", "LD", ldh)
            for j in range(4):
                dht = grp * 4 + j
                r.addf("v", lambda v, dht=dht, n=n: v.tensor_scalar(
                    out=wt[:], in0=Bbc[:], scalar1=0.0,
                    scalar2=wtab[:, dht, n:n + 1], op0=ALU.mult, op1=ALU.add))
                r.add("v", lambda v, j=j, dht=dht: v.scalar_tensor_tensor(
                    out=ut[:], in0=hencg[:, j, :], scalar=dts[:, dht:dht + 1],
                    in1=Bbc[:], op0=ALU.mult, op1=ALU.mult))
                r.add("v", lambda v: v.tensor_tensor_scan(
                    out=Ft[:], data0=wt[:], data1=ut[:], initial=0.0,
                    op0=ALU.mult, op1=ALU.add))
                if n == 0:
                    r.add("v", lambda v, j=j: v.tensor_mul(ygrp[:, j, :], Ft[:],
                                                           Cbc[:]))
                else:
                    r.add("v", lambda v: v.tensor_mul(tmpt[:], Ft[:], Cbc[:]))
                    r.add("v", lambda v, j=j: v.tensor_add(ygrp[:, j, :],
                                                           ygrp[:, j, :], tmpt[:]))
        r.wait("v", "LD", lds)
        for j in range(4):
            r.add("v", lambda v, j=j: v.tensor_add(ygrp[:, j, :], ygrp[:, j, :],
                                                   yskg[:, j, :]))
            r.add("v", lambda v, j=j: v.tensor_copy(hencg[:, j, :], ygrp[:, j, :]),
                  inc="VV")
        henc_ev = r.cnt["VV"]
        r.wait("g", "VV", r.cnt["VV"])
        r.dma("g", scr_outhd.rearrange("(g j p) t -> p (g j) t", p=128, j=4)
              [:, grp * 4:(grp + 1) * 4, :], hencg[:], inc="ST")
        st_out = r.cnt["ST"]


def get_consts():
    ident = np.eye(128, dtype=np.float32)
    c = {}
    c["c_identb"] = ident.astype(ml_dtypes.bfloat16)
    c["c_identf"] = ident
    c["c_ones"] = np.ones((1, 128), np.float32)
    s_i = np.arange(128)[:, None]
    t_i = np.arange(128)[None, :]
    c["c_maskadd"] = np.where(s_i <= t_i, 0.0, -60.0).astype(np.float32)
    c["c_tril01"] = (s_i <= t_i).astype(np.float32)
    em = np.zeros((16, 8, 128), np.float32)
    for lht in range(8):
        for p_ in range(128):
            em[(lht * 128 + p_) // 64, lht, p_] = 1.0
    c["c_emat"] = em.astype(ml_dtypes.bfloat16)
    c["c_iota"] = np.arange(T, dtype=np.float32).reshape(1, T)
    return c


_PROGS = {}
_CONSTS = {}


def kernel(**inputs):
    if not _CONSTS:
        _CONSTS.update(get_consts())
    consts = _CONSTS
    x = np.ascontiguousarray(inputs["L0_attn_x"])
    shards = [x[2 * c:2 * c + 2] for c in range(8)]

    if "l0ffn" not in _PROGS:
        _PROGS["l0ffn"] = build_l0ffn()
        _PROGS["l0attn"] = build_l0attn()
        _PROGS["l1ffn"] = build_l1ffn()
        _PROGS["l1attn"] = build_l1attn()

    cst = lambda *ks: {k: consts[k] for k in ks}

    wd_l0ffn = {}
    for p in ("q", "k", "v"):
        wd_l0ffn[f"L0_ffn_{p}_node_w"] = np.ascontiguousarray(inputs[f"L0_ffn_{p}_node_w"])
        wd_l0ffn[f"L0_ffn_{p}_node_b"] = np.ascontiguousarray(inputs[f"L0_ffn_{p}_node_b"].reshape(1, 15))
        wd_l0ffn[f"L0_ffn_{p}_w1"] = np.ascontiguousarray(inputs[f"L0_ffn_{p}_w1"])
        wd_l0ffn[f"L0_ffn_{p}_b1"] = np.ascontiguousarray(inputs[f"L0_ffn_{p}_b1"].reshape(1, D))
        wd_l0ffn[f"L0_ffn_{p}_w2"] = np.ascontiguousarray(inputs[f"L0_ffn_{p}_w2"])
        wd_l0ffn[f"L0_ffn_{p}_b2"] = np.ascontiguousarray(inputs[f"L0_ffn_{p}_b2"])
    wd_l0ffn.update(cst("c_identb", "c_identf", "c_ones", "c_emat"))

    wd_l0attn = {}
    for kk in ("Wq", "Wk", "Wv", "Wog", "Wi", "Wf", "Wout"):
        wd_l0attn["L0_attn_" + kk] = np.ascontiguousarray(inputs["L0_attn_" + kk])
    wd_l0attn["L0_attn_ln_w"] = np.ascontiguousarray(inputs["L0_attn_ln_w"].reshape(1, D))
    wd_l0attn["L0_attn_bi"] = np.ascontiguousarray(inputs["L0_attn_bi"].reshape(1, 8))
    wd_l0attn["L0_attn_bf"] = np.ascontiguousarray(inputs["L0_attn_bf"].reshape(1, 8))
    wd_l0attn.update(cst("c_identb", "c_identf", "c_ones", "c_tril01", "c_maskadd"))

    wd_l1ffn = {k: np.ascontiguousarray(inputs[k]) for k in
                ("L1_ffn_W_r", "L1_ffn_W_k", "L1_ffn_W_v")}
    wd_l1ffn["L1_ffn_mu_r"] = np.ascontiguousarray(inputs["L1_ffn_mu_r"].reshape(1, D))
    wd_l1ffn["L1_ffn_mu_k"] = np.ascontiguousarray(inputs["L1_ffn_mu_k"].reshape(1, D))
    wd_l1ffn.update(cst("c_identb", "c_identf", "c_ones"))

    wd_l1attn = {"L1_attn_input_proj": np.ascontiguousarray(inputs["L1_attn_input_proj"]),
                 "L1_attn_output_proj": np.ascontiguousarray(inputs["L1_attn_output_proj"]),
                 "L1_attn_a_log": np.ascontiguousarray(inputs["L1_attn_a_log"].reshape(1, 16)),
                 "L1_attn_b_proj": np.ascontiguousarray(inputs["L1_attn_b_proj"]),
                 "L1_attn_c_proj": np.ascontiguousarray(inputs["L1_attn_c_proj"]),
                 "L1_attn_dt_proj": np.ascontiguousarray(inputs["L1_attn_dt_proj"].reshape(1, 4096)),
                 "L1_attn_skip_proj": np.ascontiguousarray(inputs["L1_attn_skip_proj"]),
                 "L1_attn_floquet_phases": np.ascontiguousarray(
                     inputs["L1_attn_floquet_phases"].reshape(1, 4096))}
    wd_l1attn.update(cst("c_identb", "c_ones", "c_iota"))

    seq = [("l0ffn", wd_l0ffn), ("l0attn", wd_l0attn), ("l0ffn", wd_l0ffn),
           ("l1ffn", wd_l1ffn), ("l1attn", wd_l1attn), ("l1ffn", wd_l1ffn)]
    for name, wdd in seq:
        in_maps = [dict(wdd, x=shards[c]) for c in range(8)]
        res = run_bass_kernel_spmd(_PROGS[name], in_maps, core_ids=list(range(8)))
        shards = [np.asarray(res.results[c]["xo"]) for c in range(8)]
    return np.concatenate(shards, axis=0)


# version 05
# speedup vs baseline: 1.1541x; 1.1541x over previous
import numpy as np
import ml_dtypes
import concourse.bass as bass
import concourse.mybir as mybir
from concourse.bass_utils import run_bass_kernel_spmd

F32 = mybir.dt.float32
BF16 = mybir.dt.bfloat16
AF = mybir.ActivationFunctionType
ALU = mybir.AluOpType

B_LOC = 2
T = 2048
D = 1024
NT = T // 128
ND = D // 128

ENGS = ("t", "v", "s", "g", "y")

SEM_NAMES = ("LD", "ST", "MM", "EV", "VV", "SS", "GG",
             "Bt", "Bv", "Bs", "Bg", "By")


class R:

    def __init__(self, nc):
        self.nc = nc
        self.ops = {e: [] for e in ENGS}
        self.cnt = {}
        self.sems = {}
        self.dirty = {e: False for e in ENGS}

    def setup_sems(self, handles):
        for nm, h in handles.items():
            self.sems[nm] = h
            self.cnt[nm] = 0

    def emit(self, eng, fn):
        self.ops[eng].append(fn)
        self.dirty[eng] = True

    def wait(self, eng, sem, val):
        if val <= 0:
            return
        h = self.sems[sem]
        self.emit(eng, lambda e, h=h, v=val: e.wait_ge(h, v))

    def add(self, eng, build, inc=None, n=1):
        if inc is not None:
            self.cnt[inc] += n
            h = self.sems[inc]

            def fn(e, build=build, h=h, n=n):
                build(e).then_inc(h, n)
        else:
            def fn(e, build=build):
                build(e)
        self.emit(eng, fn)
        return self.cnt[inc] if inc is not None else None

    def addf(self, eng, build, sem=None):
        if sem is None:
            sem = {"v": "VV", "s": "SS", "g": "GG", "t": "MM", "y": "By"}[eng]
        self.add(eng, build, inc=sem)
        self.wait(eng, sem, self.cnt[sem])
        return self.cnt[sem]

    def dma(self, eng, dst, src, inc="LD"):
        return self.add(eng, lambda e, d=dst, s=src: e.dma_start(d, s), inc=inc, n=16)

    def barrier(self, engs=ENGS):
        targets = {}
        for e in engs:
            if self.dirty[e]:
                name = "B" + e
                if e == "v":
                    self.add("v", lambda v: v.tensor_copy(self.dummy2[:], self.dummy1[:]), inc=name)
                elif e == "s":
                    self.add("s", lambda s: s.copy(self.dummy2[:], self.dummy1[:]), inc=name)
                elif e == "g":
                    self.add("g", lambda g: g.tensor_copy(self.dummy2[:], self.dummy1[:]), inc=name)
                else:
                    self.add(e, lambda x: x.nop(), inc=name)
                targets[e] = self.cnt[name]
        for e in engs:
            for o, tv in targets.items():
                if o != e:
                    self.wait(e, "B" + o, tv)
        for e in engs:
            self.dirty[e] = False

    def finalize(self, block):
        for eng_key, attr in (("t", "tensor"), ("v", "vector"), ("s", "scalar"),
                              ("g", "gpsimd"), ("y", "sync")):
            ops = self.ops[eng_key]

            def make(ops=ops):
                def _(e):
                    for fn in ops:
                        fn(e)
                return _
            getattr(block, attr)(make())


def mm_group(r, psum_ap, pairs, inc="MM"):
    k = len(pairs)
    for i, (l, rh) in enumerate(pairs):
        st, sp = (i == 0), (i == k - 1)
        r.add("t", lambda t, l=l, rh=rh, st=st, sp=sp, p=psum_ap:
              t.matmul(p, l, rh, start=st, stop=sp),
              inc=(inc if sp else None))


class PsumPipe:

    def __init__(self, r, bufs):
        self.r, self.bufs, self.n = r, bufs, len(bufs)
        self.i = 0

    def step(self, pairs, evacs, extra_waits=(), fence_evac=False, width=None, rows=None):
        r = self.r
        buf = self.bufs[self.i % self.n]
        r.wait("t", "EV", r.cnt["MM"] + 1 - self.n)
        for w_eng, w_sem, w_val in extra_waits:
            if w_eng == "t":
                r.wait("t", w_sem, w_val)
        if width is None and rows is None:
            out_ap = buf[:]
        else:
            out_ap = buf[0:(rows if rows else 128), 0:(width if width else 512)]
        mm_group(r, out_ap, pairs, inc="MM")
        mmv = r.cnt["MM"]
        seen = set()
        for j, (eng, fn) in enumerate(evacs):
            if eng not in seen:
                r.wait(eng, "MM", mmv)
                for w_eng, w_sem, w_val in extra_waits:
                    if w_eng == eng:
                        r.wait(eng, w_sem, w_val)
                seen.add(eng)
            last = (j == len(evacs) - 1)
            r.add(eng, lambda e, fn=fn, buf=buf: fn(e, buf),
                  inc=("EV" if last else None))
            if last and fence_evac:
                r.wait(eng, "EV", r.cnt["EV"])
        self.i += 1


def emit_weight_T(r, w_dram, E, outT, cc):
    stage, stage_b, ident, pts = cc["wstage"], cc["wstage_b"], cc["identb"], cc["ptb"]
    net = (E + 127) // 128
    for et in range(net):
        rows = min(128, E - et * 128)
        r.wait("y", "VV", r.cnt["VV"])
        ld = r.dma("y", stage[0:rows, :], w_dram[et * 128:et * 128 + rows, :])
        r.wait("v", "LD", ld)
        r.wait("v", "MM", r.cnt["MM"])
        r.add("v", lambda v, rw=rows: v.tensor_copy(stage_b[0:rw, :], stage[0:rw, :]),
              inc="VV")
        r.wait("t", "VV", r.cnt["VV"])
        for half in range(2):
            p = pts[half]
            r.wait("t", "EV", r.cnt["MM"] + 1 - 2)
            for j in range(4):
                dt = half * 4 + j
                r.add("t", lambda t, rw=rows, dt=dt, j=j, p=p:
                      t.transpose(p[:, j * 128:j * 128 + rw],
                                  stage_b[0:rw, dt * 128:(dt + 1) * 128],
                                  ident[0:rw, 0:rw]),
                      inc=("MM" if j == 3 else None))
            r.wait("v", "MM", r.cnt["MM"])
            r.add("v", lambda v, rw=rows, half=half, et=et, p=p:
                  v.tensor_copy(
                      outT[:, half * 4:half * 4 + 4, et * 128:et * 128 + rows],
                      p.rearrange("p (a c) -> p a c", a=4)[:, :, 0:rows]),
                  inc="EV")


def emit_ln_transpose(r, x_dram_b, xlnT, cc):
    xt, cen, sq = cc["xt"], cc["cen"], cc["sq"]
    cols = cc["cols"]
    ptb = cc["ptb"]
    xlnb = cc["xlnb"]
    vv_after = {}
    for tt in range(NT):
        xb = xt[tt % 2]
        if tt >= 2:
            r.wait("y", "VV", vv_after[tt - 2])
        ld = r.dma("y", xb[:], x_dram_b[tt * 128:(tt + 1) * 128, :])
        r.wait("v", "LD", ld)
        r.add("v", lambda v, xb=xb: v.reduce_sum(cols[:, 0:1], xb[:], axis=mybir.AxisListType.X))
        r.add("v", lambda v: v.tensor_scalar(out=cols[:, 1:2], in0=cols[:, 0:1],
                                             scalar1=1.0 / D, scalar2=None, op0=ALU.mult))
        r.add("v", lambda v, xb=xb: v.tensor_scalar(out=cen[:], in0=xb[:],
                                                    scalar1=cols[:, 1:2], scalar2=None,
                                                    op0=ALU.subtract))
        r.add("v", lambda v: v.tensor_mul(sq[:], cen[:], cen[:]))
        r.add("v", lambda v: v.reduce_sum(cols[:, 2:3], sq[:], axis=mybir.AxisListType.X))
        r.add("v", lambda v: v.tensor_scalar(out=cols[:, 3:4], in0=cols[:, 2:3],
                                             scalar1=1.0 / D, scalar2=1e-5,
                                             op0=ALU.mult, op1=ALU.add), inc="VV")
        r.wait("s", "VV", r.cnt["VV"])
        r.add("s", lambda s: s.activation(cols[:, 4:5], cols[:, 3:4], AF.Ln))
        r.add("s", lambda s: s.activation(cols[:, 5:6], cols[:, 4:5], AF.Exp, scale=-0.5),
              inc="SS")
        r.wait("v", "SS", r.cnt["SS"])
        xo = xlnb[tt % 2]
        r.add("v", lambda v, xo=xo: v.tensor_scalar(out=xo[:], in0=cen[:],
                                                    scalar1=cols[:, 5:6], scalar2=None,
                                                    op0=ALU.mult), inc="VV")
        vv_after[tt] = r.cnt["VV"]
        r.wait("t", "VV", r.cnt["VV"])
        for half in range(2):
            p = ptb[half]
            r.wait("t", "EV", r.cnt["MM"] + 1 - 2)
            for j in range(4):
                dt = half * 4 + j
                r.add("t", lambda t, xo=xo, dt=dt, j=j, p=p:
                      t.transpose(p[:, j * 128:(j + 1) * 128],
                                  xo[:, dt * 128:(dt + 1) * 128], cc["identb"]),
                      inc=("MM" if j == 3 else None))
            r.wait("v", "MM", r.cnt["MM"])
            r.add("v", lambda v, half=half, tt=tt, p=p:
                  v.tensor_copy(xlnT[:, half * 4:half * 4 + 4, tt * 128:(tt + 1) * 128],
                                p.rearrange("p (a c) -> p a c", a=4)),
                  inc="EV")


def common_sems(nc):
    import contextlib
    stack = contextlib.ExitStack()
    handles = {}
    for nm in SEM_NAMES:
        handles[nm] = stack.enter_context(nc.semaphore("sem" + nm))
    return stack, handles



def emit_ln_tile(r, x_tile_dram, cc, tt, dest, tcol):
    xt, cen, sq, cols = cc["xt"], cc["cen"], cc["sq"], cc["cols"]
    xb = xt[tt % 2]
    vv_prev = cc.get("vv_ln", {})
    if tt - 2 in vv_prev:
        r.wait("y", "VV", vv_prev[tt - 2])
    ld = r.dma("y", xb[:], x_tile_dram)
    r.wait("v", "LD", ld)
    r.addf("v", lambda v, xb=xb: v.reduce_sum(cols[:, 0:1], xb[:], axis=mybir.AxisListType.X))
    r.addf("v", lambda v: v.tensor_scalar(out=cols[:, 1:2], in0=cols[:, 0:1],
                                          scalar1=1.0 / D, scalar2=None,
                                          op0=ALU.mult))
    r.add("v", lambda v, xb=xb: v.tensor_scalar(out=cen[:], in0=xb[:],
                                                scalar1=cols[:, 1:2], scalar2=None,
                                                op0=ALU.subtract))
    r.add("v", lambda v: v.tensor_mul(sq[:], cen[:], cen[:]))
    r.addf("v", lambda v: v.reduce_sum(cols[:, 2:3], sq[:], axis=mybir.AxisListType.X))
    r.addf("v", lambda v: v.tensor_scalar(out=cols[:, 3:4], in0=cols[:, 2:3],
                                          scalar1=1.0 / D, scalar2=1e-5,
                                          op0=ALU.mult, op1=ALU.add))
    r.wait("s", "VV", r.cnt["VV"])
    r.addf("s", lambda s: s.activation(cols[:, 4:5], cols[:, 3:4], AF.Ln))
    r.addf("s", lambda s: s.activation(cols[:, 5:6], cols[:, 4:5], AF.Exp, scale=-0.5))
    r.wait("v", "SS", r.cnt["SS"])
    xo = cc["xlnb"][tt % 2]
    r.add("v", lambda v, xo=xo: v.tensor_scalar(out=xo[:], in0=cen[:],
                                                scalar1=cols[:, 5:6], scalar2=None,
                                                op0=ALU.mult), inc="VV")
    vv_prev[tt] = r.cnt["VV"]
    cc["vv_ln"] = vv_prev
    r.wait("t", "VV", r.cnt["VV"])
    for half in range(2):
        p = cc["ptb"][half]
        r.wait("t", "EV", r.cnt["MM"] + 1 - 2)
        for j in range(4):
            dt = half * 4 + j
            r.add("t", lambda t, xo=xo, dt=dt, j=j, p=p:
                  t.transpose(p[:, j * 128:(j + 1) * 128],
                              xo[:, dt * 128:(dt + 1) * 128], cc["identb"]),
                  inc=("MM" if j == 3 else None))
        r.wait("v", "MM", r.cnt["MM"])
        r.add("v", lambda v, half=half, p=p, tcol=tcol, dest=dest:
              v.tensor_copy(dest[:, half * 4:half * 4 + 4, tcol * 128:(tcol + 1) * 128],
                            p.rearrange("p (a c) -> p a c", a=4)),
              inc="EV")


def emit_ln_transpose(r, x_dram_b, xlnT, cc):
    for tt in range(NT):
        emit_ln_tile(r, x_dram_b[tt * 128:(tt + 1) * 128, :], cc, tt, xlnT, tt)
    cc.pop("vv_ln", None)


def emit_sub_l1ffn(C, xin, xout, sfx):
    nc, r, cc = C["nc"], C["r"], C["cc"]
    x, xo = xin, xout
    w_r = C["ap"]["L1_ffn_W_r"]
    w_k = C["ap"]["L1_ffn_W_k"]
    w_v = C["ap"]["L1_ffn_W_v"]
    mu_r = C["ap"]["L1_ffn_mu_r"]
    mu_k = C["ap"]["L1_ffn_mu_k"]
    identb, identf, onesf = C["identb"], C["identf"], C["onesf"]
    dummy1, dummy2 = r.dummy1, r.dummy2
    xt0, xt1, cen, sq, cols = cc["xt"][0], cc["xt"][1], cc["cen"], cc["sq"], cc["cols"]
    xlnb0, xlnb1 = cc["xlnb"]
    pb0, pb1 = cc["ptb"]
    import contextlib
    stack = contextlib.ExitStack()
    al = stack.enter_context
    wstage = al(nc.sbuf_tensor("wstage" + sfx, [128, 1024], F32))
    wstage_b = al(nc.sbuf_tensor("wstageb" + sfx, [128, 1024], BF16))
    cc["wstage"], cc["wstage_b"] = wstage, wstage_b
    wrT = al(nc.sbuf_tensor("wrT" + sfx, [128, 8, D], BF16))
    wkT = al(nc.sbuf_tensor("wkT" + sfx, [128, 8, D], BF16))
    wvT = al(nc.sbuf_tensor("wvT" + sfx, [128, 8, D], BF16))
    murow = al(nc.sbuf_tensor("murow" + sfx, [1, D], F32))
    mus = al(nc.sbuf_tensor("mus" + sfx, [128, 8, 4], F32))
    xrT = al(nc.sbuf_tensor("xrT" + sfx, [128, 8, T], BF16))
    xkT = al(nc.sbuf_tensor("xkT" + sfx, [128, 8, T], BF16))
    kaT = al(nc.sbuf_tensor("kaT" + sfx, [128, 8, T], BF16))
    xlntile = al(nc.sbuf_tensor("xlntile" + sfx, [128, 8, 128], BF16))
    xlntile3 = xlntile
    prevcol = al(nc.sbuf_tensor("prevcol" + sfx, [128, 8], BF16))
    shtmp = al(nc.sbuf_tensor("shtmp" + sfx, [128, 128], F32))
    ko0 = al(nc.sbuf_tensor("ko0" + sfx, [128, 512], F32))
    rtile = al(nc.sbuf_tensor("rtile" + sfx, [128, 512], BF16))
    vtmp = al(nc.sbuf_tensor("vtmp" + sfx, [128, 512], F32))
    pf0 = al(nc.psum_tensor("pf0" + sfx, [128, 512], F32))
    pf1 = al(nc.psum_tensor("pf1" + sfx, [128, 512], F32))
    pf2 = al(nc.psum_tensor("pf2" + sfx, [128, 512], F32))
    pf3 = al(nc.psum_tensor("pf3" + sfx, [128, 512], F32))
    debug = False
    if True:
        pfs = [pf0, pf1, pf2, pf3]

        ld = r.dma("y", murow[:], mu_r)
        r.wait("t", "LD", ld)
        for dt in range(ND):
            r.add("t", lambda t, dt=dt: t.matmul(
                pf0[:, dt:dt + 1], murow[0:1, dt * 128:(dt + 1) * 128],
                onesf[0:1, 0:1], start=True, stop=True),
                inc=("MM" if dt == ND - 1 else None))
        mu_r_mm = r.cnt["MM"]
        r.wait("v", "MM", mu_r_mm)
        r.addf("v", lambda v: v.tensor_copy(mus[:, :, 0], pf0[:, 0:8]))
        r.addf("v", lambda v: v.tensor_scalar(out=mus[:, :, 1], in0=pf0[:, 0:8],
                                              scalar1=-1.0, scalar2=1.0,
                                              op0=ALU.mult, op1=ALU.add))
        r.add("v", lambda v: v.tensor_copy(r.dummy2[:], r.dummy1[:]), inc="EV")
        r.wait("y", "MM", mu_r_mm)
        ld2 = r.dma("y", murow[:], mu_k)
        r.wait("t", "LD", ld2)
        r.wait("t", "EV", r.cnt["EV"])
        for dt in range(ND):
            r.add("t", lambda t, dt=dt: t.matmul(
                pf1[:, dt:dt + 1], murow[0:1, dt * 128:(dt + 1) * 128],
                onesf[0:1, 0:1], start=True, stop=True),
                inc=("MM" if dt == ND - 1 else None))
        r.wait("v", "MM", r.cnt["MM"])
        r.addf("v", lambda v: v.tensor_copy(mus[:, :, 2], pf1[:, 0:8]))
        r.addf("v", lambda v: v.tensor_scalar(out=mus[:, :, 3], in0=pf1[:, 0:8],
                                              scalar1=-1.0, scalar2=1.0,
                                              op0=ALU.mult, op1=ALU.add))
        r.add("v", lambda v: v.tensor_copy(r.dummy2[:], r.dummy1[:]), inc="EV")
        r.barrier()
        emit_weight_T(r, w_r, D, wrT, cc)
        emit_weight_T(r, w_k, D, wkT, cc)
        emit_weight_T(r, w_v, D, wvT, cc)
        r.barrier()

        for b in range(B_LOC):
            r.addf("v", lambda v: v.tensor_scalar(out=prevcol[:], in0=mus[:, :, 0],
                                                  scalar1=0.0, scalar2=None,
                                                  op0=ALU.mult))
            for tt in range(NT):
                emit_ln_tile(r, x[b, tt * 128:(tt + 1) * 128, :], cc, tt, xlntile3, 0)
                r.wait("v", "EV", r.cnt["EV"])
                for dt in range(ND):
                    for j, dst in ((0, xrT), (1, xkT)):
                        mu_c = mus[:, dt, 2 * j:2 * j + 1]
                        om_c = mus[:, dt, 2 * j + 1:2 * j + 2]
                        r.add("v", lambda v, dt=dt, mu_c=mu_c:
                              v.tensor_scalar(out=shtmp[:], in0=xlntile[:, dt, :],
                                              scalar1=mu_c, scalar2=None, op0=ALU.mult))
                        r.add("v", lambda v, dt=dt, dst=dst, om_c=om_c, tt=tt:
                              v.scalar_tensor_tensor(
                                  out=dst[:, dt, tt * 128 + 1:(tt + 1) * 128],
                                  in0=xlntile[:, dt, 0:127], scalar=om_c,
                                  in1=shtmp[:, 1:128], op0=ALU.mult, op1=ALU.add))
                        r.add("v", lambda v, dt=dt, dst=dst, om_c=om_c, tt=tt:
                              v.scalar_tensor_tensor(
                                  out=dst[:, dt, tt * 128:tt * 128 + 1],
                                  in0=prevcol[:, dt:dt + 1], scalar=om_c,
                                  in1=shtmp[:, 0:1], op0=ALU.mult, op1=ALU.add))
                    r.addf("v", lambda v, dt=dt: v.tensor_copy(
                        prevcol[:, dt:dt + 1], xlntile[:, dt, 127:128]))
            r.barrier()
            if debug and b == 0:
                r.dma("g", dbg_xr, xrT[:], inc="ST")
                r.dma("g", dbg_xk, xkT[:], inc="ST")
                r.dma("g", dbg_mus, mus[:], inc="ST")
                r.wait("g", "ST", r.cnt["ST"])
            pipe = PsumPipe(r, pfs)
            for ot in range(ND):
                for tc in range(4):
                    pairs = [(wkT[:, dt, ot * 128:(ot + 1) * 128],
                              xkT[:, dt, tc * 512:(tc + 1) * 512]) for dt in range(ND)]
                    pipe.step(pairs, [
                        ("s", lambda s, p: s.activation(ko0[:], p[:], AF.Relu)),
                        ("s", lambda s, p, ot=ot, tc=tc: s.activation(
                            kaT[:, ot, tc * 512:(tc + 1) * 512], ko0[:], AF.Square)),
                    ])
            r.barrier()
            if debug and b == 0:
                r.dma("g", dbg_ka, kaT[:], inc="ST")
                r.wait("g", "ST", r.cnt["ST"])
            ev_base = r.cnt["EV"]
            for tt in range(NT):
                xb = xt0 if tt % 2 == 0 else xt1
                if tt >= 2:
                    r.wait("y", "EV", ev_base + 4 * (tt - 1))
                ldx = r.dma("y", xb[:], x[b, tt * 128:(tt + 1) * 128, :])
                for oc in range(2):
                    prev_ev = r.cnt["EV"]
                    rpairs = [(xrT[:, dt, tt * 128:(tt + 1) * 128],
                               wrT[:, dt, oc * 512:(oc + 1) * 512]) for dt in range(ND)]
                    pipe.step(rpairs, [
                        ("s", lambda s, p: s.activation(rtile[:], p[:], AF.Sigmoid)),
                    ], extra_waits=[("s", "EV", prev_ev)])
                    ev_r = r.cnt["EV"]
                    vpairs = [(kaT[:, dt, tt * 128:(tt + 1) * 128],
                               wvT[:, dt, oc * 512:(oc + 1) * 512]) for dt in range(ND)]
                    pipe.step(vpairs, [
                        ("v", lambda v, p: v.tensor_tensor(
                            out=vtmp[:], in0=rtile[:], in1=p[:], op=ALU.mult)),
                        ("v", lambda v, p, oc=oc, xb=xb: v.scalar_tensor_tensor(
                            out=cen[:, oc * 512:(oc + 1) * 512], in0=vtmp[:],
                            scalar=0.5, in1=xb[:, oc * 512:(oc + 1) * 512],
                            op0=ALU.mult, op1=ALU.add)),
                    ], extra_waits=[("v", "LD", ldx), ("v", "EV", ev_r)])
                r.wait("v", "ST", r.cnt["ST"])
                r.add("v", lambda v: v.tensor_copy(sq[:], cen[:]), inc="VV")
                r.wait("g", "VV", r.cnt["VV"])
                r.dma("g", xo[b, tt * 128:(tt + 1) * 128, :], sq[:], inc="ST")
            r.barrier()
            cc.pop("vv_ln", None)
    stack.close()


def emit_sub_l0ffn(C, xin, xout, sfx):
    nc, r, cc = C["nc"], C["r"], C["cc"]
    x, xo = xin, xout
    wd = {}
    for p in ("q", "k", "v"):
        for suf in ("node_w", "node_b", "w1", "b1", "w2", "b2"):
            wd[p + "_" + suf] = C["ap"][f"L0_ffn_{p}_{suf}"]
    identb, identf, onesf = C["identb"], C["identf"], C["onesf"]
    ematsb = C["ematsb"]
    dummy1, dummy2 = r.dummy1, r.dummy2
    xt0, xt1, cen, sq, cols = cc["xt"][0], cc["xt"][1], cc["cen"], cc["sq"], cc["cols"]
    xlnb0, xlnb1 = cc["xlnb"]
    pb0, pb1 = cc["ptb"]
    import contextlib
    stack = contextlib.ExitStack()
    al = stack.enter_context
    qT = al(nc.sbuf_tensor("qT" + sfx, [128, 8, T], BF16))
    kT = al(nc.sbuf_tensor("kT" + sfx, [128, 8, T], BF16))
    vnat = al(nc.sbuf_tensor("vnat" + sfx, [128, NT, D], BF16))
    nwT = al(nc.sbuf_tensor("nwT" + sfx, [128, 3, 8, 15], BF16))
    nbbc = al(nc.sbuf_tensor("nbbc" + sfx, [128, 3, 15], F32))
    b1c = al(nc.sbuf_tensor("b1c" + sfx, [128, 3, 8], F32))
    b2b = al(nc.sbuf_tensor("b2b" + sfx, [16, D], BF16))
    b2ball = nc.dram_tensor("b2b_scr" + sfx, [3, 16, D], BF16, kind="Internal").ap()
    S_all = al(nc.sbuf_tensor("S_all" + sfx, [128, NT, 15], F32))
    gbits = al(nc.sbuf_tensor("gbits" + sfx, [128, NT, 4], F32))
    sv = al(nc.sbuf_tensor("sv" + sfx, [128, NT, 20], F32))
    sel = al(nc.sbuf_tensor("sel" + sfx, [128, NT, 16], F32))
    selTb = al(nc.sbuf_tensor("selTb" + sfx, [16, T], BF16))
    if True:
        with contextlib.ExitStack() as prep_stack:
            a2 = prep_stack.enter_context
            pf0 = a2(nc.psum_tensor("pf0p" + sfx, [128, 512], F32))
            wstage = a2(nc.sbuf_tensor("wstage_p" + sfx, [128, 1024], F32))
            wstage_b = a2(nc.sbuf_tensor("wstage_bp" + sfx, [128, 1024], BF16))
            cc["wstage"], cc["wstage_b"] = wstage, wstage_b

            for pi, p in enumerate(("q", "k", "v")):
                emit_weight_T(r, wd[p + "_node_w"], 15, nwT[:, pi], cc)
                r.wait("y", "VV", r.cnt["VV"])
                ldnb = r.dma("y", wstage[0:1, 0:15], wd[p + "_node_b"])
                r.wait("t", "LD", ldnb)
                r.wait("t", "EV", r.cnt["MM"])
                r.add("t", lambda t, pi=pi: t.matmul(pf0[:, 0:15], onesf[:],
                                                     wstage[0:1, 0:15],
                                                     start=True, stop=True), inc="MM")
                r.wait("v", "MM", r.cnt["MM"])
                r.addf("v", lambda v, pi=pi: v.tensor_copy(nbbc[:, pi, :], pf0[:, 0:15]))
                r.add("v", lambda v: v.tensor_copy(r.dummy2[:], r.dummy1[:]), inc="EV")
                r.wait("y", "EV", r.cnt["EV"])
                r.wait("y", "MM", r.cnt["MM"])
                ldb1 = r.dma("y", wstage[0:1, :], wd[p + "_b1"])
                r.wait("t", "LD", ldb1)
                r.wait("t", "EV", r.cnt["MM"])
                for dt in range(ND):
                    r.add("t", lambda t, dt=dt: t.matmul(
                        pf0[:, dt:dt + 1], wstage[0:1, dt * 128:(dt + 1) * 128],
                        onesf[0:1, 0:1], start=True, stop=True),
                        inc=("MM" if dt == ND - 1 else None))
                r.wait("v", "MM", r.cnt["MM"])
                r.addf("v", lambda v, pi=pi: v.tensor_copy(b1c[:, pi, :], pf0[:, 0:8]))
                r.add("v", lambda v: v.tensor_copy(r.dummy2[:], r.dummy1[:]), inc="EV")
                r.wait("y", "VV", r.cnt["VV"])
                ldb2 = r.dma("y", wstage[0:16, :], wd[p + "_b2"])
                r.wait("v", "LD", ldb2)
                r.add("v", lambda v: v.tensor_copy(b2b[:], wstage[0:16, :]), inc="VV")
                r.wait("g", "VV", r.cnt["VV"])
                r.dma("g", b2ball[pi], b2b[:], inc="ST")
            r.barrier()

        for b in range(B_LOC):
            with contextlib.ExitStack() as px:
                a3 = px.enter_context
                xlnT = a3(nc.sbuf_tensor(f"xlnT{b}" + sfx, [128, 8, T], BF16))
                hidT = a3(nc.sbuf_tensor(f"hidT{b}" + sfx, [128, 8, T], BF16))
                w1sl = a3(nc.sbuf_tensor(f"w1sl{b}" + sfx, [128, 8, 128], BF16))
                w2vsl = a3(nc.sbuf_tensor(f"w2vsl{b}" + sfx, [128, 8, 512], BF16))
                wsl_f = a3(nc.sbuf_tensor(f"wsl_f{b}" + sfx, [128, 8, 128], F32))
                pf0 = a3(nc.psum_tensor(f"pf0_{b}" + sfx, [128, 512], F32))
                pf1 = a3(nc.psum_tensor(f"pf1_{b}" + sfx, [128, 512], F32))
                pf2 = a3(nc.psum_tensor(f"pf2_{b}" + sfx, [128, 512], F32))
                pf3 = a3(nc.psum_tensor(f"pf3_{b}" + sfx, [128, 512], F32))
                ps_sel = a3(nc.psum_tensor(f"ps_sel{b}" + sfx, [16, 512], F32))
                pfs = [pf0, pf1, pf2, pf3]
                emit_ln_transpose(r, x[b], xlnT, cc)
                r.barrier()
                pipe = PsumPipe(r, pfs)
                for pi, p in enumerate(("q", "k", "v")):
                    r.wait("y", "ST", r.cnt["ST"])
                    ldb2p = r.dma("y", b2b[:], b2ball[pi])
                    r.wait("t", "LD", ldb2p)
                    for tt in range(NT):
                        pairs = [(xlnT[:, dt, tt * 128:(tt + 1) * 128],
                                  nwT[:, pi, dt, :]) for dt in range(ND)]
                        pipe.step(pairs, [
                            ("v", lambda v, ps, tt=tt, pi=pi: v.tensor_tensor(
                                out=S_all[:, tt, :], in0=ps[:, 0:15],
                                in1=nbbc[:, pi, :], op=ALU.add)),
                        ], fence_evac=True, width=15)
                    emit_routing(r, S_all, gbits, sv, sel)
                    for g4 in range(4):
                        r.wait("t", "VV", r.cnt["VV"])
                        r.wait("t", "EV", r.cnt["MM"] + 1 - 1)
                        for j in range(4):
                            tt = g4 * 4 + j
                            r.add("t", lambda t, tt=tt, j=j: t.transpose(
                                ps_sel[:, j * 128:(j + 1) * 128], sel[:, tt, :],
                                identf[:]), inc=("MM" if j == 3 else None))
                        r.wait("v", "MM", r.cnt["MM"])
                        r.add("v", lambda v, g4=g4: v.tensor_copy(
                            selTb[:, g4 * 512:(g4 + 1) * 512], ps_sel[:]), inc="EV")
                    for lht in range(8):
                        r.wait("y", "VV", r.cnt["VV"])
                        r.wait("y", "MM", r.cnt["MM"])
                        for lp in range(2):
                            ldw = r.dma("y", wsl_f[:, :, lp * 64:(lp + 1) * 64],
                                        wd[p + "_w1"].rearrange(
                                            "l (dt pp) h -> pp dt l h",
                                            pp=128)[:, :, 2 * lht + lp, :])
                        r.wait("v", "LD", ldw)
                        r.add("v", lambda v: v.tensor_copy(w1sl[:], wsl_f[:]),
                              inc="VV")
                        vv_w1 = r.cnt["VV"]
                        for tc in range(4):
                            pairs = [(w1sl[:, dt, :],
                                      xlnT[:, dt, tc * 512:(tc + 1) * 512])
                                     for dt in range(ND)]
                            pipe.step(pairs, [
                                ("s", lambda sE, ps, lht=lht, tc=tc, pi=pi:
                                 sE.activation(hidT[:, lht, tc * 512:(tc + 1) * 512],
                                               ps[:], AF.Relu,
                                               bias=b1c[:, pi, lht:lht + 1])),
                            ], extra_waits=[("t", "VV", vv_w1)])
                            ev_relu = r.cnt["EV"]
                            mp = [(ematsb[:, lht, :],
                                   selTb[:, tc * 512:(tc + 1) * 512])]
                            pipe.step(mp, [
                                ("v", lambda v, ps, lht=lht, tc=tc: v.tensor_tensor(
                                    out=hidT[:, lht, tc * 512:(tc + 1) * 512],
                                    in0=hidT[:, lht, tc * 512:(tc + 1) * 512],
                                    in1=ps[:], op=ALU.mult)),
                            ], extra_waits=[("v", "EV", ev_relu)])
                    r.barrier()
                    if p in ("q", "k"):
                        outT = qT if p == "q" else kT
                        for ot in range(ND):
                            r.wait("y", "VV", r.cnt["VV"])
                            r.wait("y", "MM", r.cnt["MM"])
                            ldw = r.dma("y", wsl_f[:, :, 0:128],
                                        wd[p + "_w2"].rearrange(
                                            "(lht lp) h o -> (lp h) lht o", lp=2)
                                        [:, :, ot * 128:(ot + 1) * 128])
                            r.wait("v", "LD", ldw)
                            r.add("v", lambda v: v.tensor_copy(w1sl[:],
                                                               wsl_f[:, :, 0:128]),
                                  inc="VV")
                            vv_w2 = r.cnt["VV"]
                            for tc in range(4):
                                pairs = [(w1sl[:, lht, :],
                                          hidT[:, lht, tc * 512:(tc + 1) * 512])
                                         for lht in range(8)]
                                pairs.append((b2b[:, ot * 128:(ot + 1) * 128],
                                              selTb[:, tc * 512:(tc + 1) * 512]))
                                pipe.step(pairs, [
                                    ("s", lambda sE, ps, ot=ot, tc=tc, outT=outT:
                                     sE.copy(outT[:, ot, tc * 512:(tc + 1) * 512],
                                             ps[:])),
                                ], extra_waits=[("t", "VV", vv_w2)])
                    else:
                        for oc in range(2):
                            for half in range(4):
                                r.wait("y", "VV", r.cnt["VV"])
                                r.wait("y", "MM", r.cnt["MM"])
                                ldw = r.dma("y", wsl_f[:],
                                            wd[p + "_w2"].rearrange(
                                                "(lht lp) h o -> (lp h) lht o", lp=2)
                                            [:, :, oc * 512 + half * 128:
                                             oc * 512 + (half + 1) * 128])
                                r.wait("v", "LD", ldw)
                                r.add("v", lambda v, half=half: v.tensor_copy(
                                    w2vsl[:, :, half * 128:(half + 1) * 128],
                                    wsl_f[:]), inc="VV")
                            vv_w2 = r.cnt["VV"]
                            for tt in range(NT):
                                pairs = [(hidT[:, lht, tt * 128:(tt + 1) * 128],
                                          w2vsl[:, lht, :]) for lht in range(8)]
                                pairs.append((selTb[:, tt * 128:(tt + 1) * 128],
                                              b2b[:, oc * 512:(oc + 1) * 512]))
                                pipe.step(pairs, [
                                    ("s", lambda sE, ps, tt=tt, oc=oc:
                                     sE.copy(vnat[:, tt, oc * 512:(oc + 1) * 512],
                                             ps[:])),
                                ], extra_waits=[("t", "VV", vv_w2)])
                    r.barrier()
                r.barrier()
            with contextlib.ExitStack() as ax:
                a4 = ax.enter_context
                esb = a4(nc.sbuf_tensor(f"esb{b}" + sfx, [128, T], F32))
                attnb = a4(nc.sbuf_tensor(f"attnb{b}" + sfx, [128, T], BF16))
                attnT = a4(nc.sbuf_tensor(f"attnT{b}" + sfx, [128, NT, 128], BF16))
                psc = a4(nc.psum_tensor(f"psc{b}" + sfx, [128, 2048], F32))
                ph0 = a4(nc.psum_tensor(f"ph0_{b}" + sfx, [128, 512], F32))
                ph1 = a4(nc.psum_tensor(f"ph1_{b}" + sfx, [128, 512], F32))
                emit_l0ffn_attention(r, cc, qT, kT, vnat, x[b], xo[b],
                                     psc, ph0, ph1, esb, attnb, attnT)
                r.barrier()
            cc.pop("vv_ln", None)
    stack.close()


def emit_l0ffn_attention(r, cc, qT, kT, vnat, x_b, xo_b, psc, ph0, ph1,
                         esb, attnb, attnT):
    cols, sq = cc["cols"], cc["sq"]
    xt = cc["xt"]
    psc_gate = 0
    ph_gate = [0, 0]
    pb_gate = [0, 0]
    st_before = {}
    for qt in range(NT):
        r.wait("t", "EV", psc_gate)
        for sc in range(4):
            pairs = [(qT[:, dt, qt * 128:(qt + 1) * 128],
                      kT[:, dt, sc * 512:(sc + 1) * 512]) for dt in range(ND)]
            mm_group(r, psc[:, sc * 512:(sc + 1) * 512], pairs, inc="MM")
        mmv = r.cnt["MM"]
        r.wait("v", "MM", mmv)
        r.addf("v", lambda v: v.reduce_max(cols[:, 0:1], psc[:], axis=mybir.AxisListType.X))
        r.addf("v", lambda v: v.tensor_scalar(out=cols[:, 1:2], in0=cols[:, 0:1],
                                              scalar1=-1.0 / 32.0, scalar2=None,
                                              op0=ALU.mult))
        r.wait("s", "VV", r.cnt["VV"])
        r.wait("s", "MM", mmv)
        r.add("s", lambda sE: sE.activation(esb[:], psc[:], AF.Exp,
                                            bias=cols[:, 1:2], scale=1.0 / 32.0),
              inc="EV", n=4)
        psc_gate = r.cnt["EV"]
        r.wait("v", "EV", psc_gate)
        r.addf("v", lambda v: v.reduce_sum(cols[:, 2:3], esb[:], axis=mybir.AxisListType.X))
        r.addf("v", lambda v: v.reciprocal(cols[:, 3:4], cols[:, 2:3]))
        r.wait("s", "VV", r.cnt["VV"])
        r.add("s", lambda sE: sE.activation(attnb[:], esb[:], AF.Copy,
                                            scale=cols[:, 3:4]), inc="SS")
        r.wait("t", "SS", r.cnt["SS"])
        for g4 in range(4):
            pb = cc["ptb"][g4 % 2]
            r.wait("t", "EV", pb_gate[g4 % 2])
            for j in range(4):
                st = g4 * 4 + j
                r.add("t", lambda t, st=st, j=j, pb=pb: t.transpose(
                    pb[:, j * 128:(j + 1) * 128],
                    attnb[:, st * 128:(st + 1) * 128], cc["identb"]),
                    inc=("MM" if j == 3 else None))
            r.wait("v", "MM", r.cnt["MM"])
            r.add("v", lambda v, g4=g4, pb=pb: v.tensor_copy(
                attnT[:, g4 * 4:g4 * 4 + 4, :],
                pb.rearrange("p (a c) -> p a c", a=4)), inc="EV")
            pb_gate[g4 % 2] = r.cnt["EV"]
        xb = xt[qt % 2]
        r.wait("y", "ST", st_before.get(qt - 2, 0))
        r.wait("v", "ST", st_before.get(qt - 1, 0))
        ldx = r.dma("y", xb[:], x_b[qt * 128:(qt + 1) * 128, :])
        att_ev = r.cnt["EV"]
        for oc, ph in ((0, ph0), (1, ph1)):
            r.wait("t", "EV", ph_gate[oc])
            r.wait("t", "EV", att_ev)
            pairs = [(attnT[:, st, :], vnat[:, st, oc * 512:(oc + 1) * 512])
                     for st in range(NT)]
            mm_group(r, ph[:], pairs, inc="MM")
            r.wait("v", "MM", r.cnt["MM"])
            r.wait("v", "LD", ldx)
            r.add("v", lambda v, ph=ph, oc=oc, xb=xb: v.scalar_tensor_tensor(
                out=sq[:, oc * 512:(oc + 1) * 512], in0=ph[:], scalar=0.5,
                in1=xb[:, oc * 512:(oc + 1) * 512], op0=ALU.mult, op1=ALU.add),
                inc="EV")
            ph_gate[oc] = r.cnt["EV"]
        r.wait("g", "EV", r.cnt["EV"])
        st_before[qt] = r.cnt["ST"] + 16
        r.dma("g", xo_b[qt * 128:(qt + 1) * 128, :], sq[:], inc="ST")


def emit_routing(r, S_all, gbits, sv, sel):
    S = lambda j: S_all[:, :, j]

    def fv(build):
        r.addf("v", build)

    fv(lambda v: v.tensor_single_scalar(out=gbits[:, :, 0], in_=S(0), scalar=0.0,
                                        op=ALU.is_ge))
    fv(lambda v: v.tensor_tensor(out=sv[:, :, 0], in0=S(2), in1=S(1), op=ALU.subtract))
    fv(lambda v: v.tensor_tensor(out=sv[:, :, 0], in0=sv[:, :, 0], in1=gbits[:, :, 0], op=ALU.mult))
    fv(lambda v: v.tensor_tensor(out=sv[:, :, 1], in0=S(1), in1=sv[:, :, 0], op=ALU.add))
    fv(lambda v: v.tensor_single_scalar(out=gbits[:, :, 1], in_=sv[:, :, 1], scalar=0.0, op=ALU.is_ge))
    for j, (lo, hi) in enumerate(((3, 4), (5, 6))):
        fv(lambda v, lo=lo, hi=hi, j=j: v.tensor_tensor(out=sv[:, :, 2 + j], in0=S(hi), in1=S(lo), op=ALU.subtract))
        fv(lambda v, j=j: v.tensor_tensor(out=sv[:, :, 2 + j], in0=sv[:, :, 2 + j], in1=gbits[:, :, 1], op=ALU.mult))
        fv(lambda v, lo=lo, j=j: v.tensor_tensor(out=sv[:, :, 2 + j], in0=S(lo), in1=sv[:, :, 2 + j], op=ALU.add))
    fv(lambda v: v.tensor_tensor(out=sv[:, :, 4], in0=sv[:, :, 3], in1=sv[:, :, 2], op=ALU.subtract))
    fv(lambda v: v.tensor_tensor(out=sv[:, :, 4], in0=sv[:, :, 4], in1=gbits[:, :, 0], op=ALU.mult))
    fv(lambda v: v.tensor_tensor(out=sv[:, :, 4], in0=sv[:, :, 2], in1=sv[:, :, 4], op=ALU.add))
    fv(lambda v: v.tensor_single_scalar(out=gbits[:, :, 2], in_=sv[:, :, 4], scalar=0.0, op=ALU.is_ge))
    for j in range(4):
        lo, hi = 7 + 2 * j, 8 + 2 * j
        fv(lambda v, lo=lo, hi=hi, j=j: v.tensor_tensor(out=sv[:, :, 5 + j], in0=S(hi), in1=S(lo), op=ALU.subtract))
        fv(lambda v, j=j: v.tensor_tensor(out=sv[:, :, 5 + j], in0=sv[:, :, 5 + j], in1=gbits[:, :, 2], op=ALU.mult))
        fv(lambda v, lo=lo, j=j: v.tensor_tensor(out=sv[:, :, 5 + j], in0=S(lo), in1=sv[:, :, 5 + j], op=ALU.add))
    for j in range(2):
        a, bb = 5 + 2 * j, 6 + 2 * j
        fv(lambda v, a=a, bb=bb, j=j: v.tensor_tensor(out=sv[:, :, 9 + j], in0=sv[:, :, bb], in1=sv[:, :, a], op=ALU.subtract))
        fv(lambda v, j=j: v.tensor_tensor(out=sv[:, :, 9 + j], in0=sv[:, :, 9 + j], in1=gbits[:, :, 1], op=ALU.mult))
        fv(lambda v, a=a, j=j: v.tensor_tensor(out=sv[:, :, 9 + j], in0=sv[:, :, a], in1=sv[:, :, 9 + j], op=ALU.add))
    fv(lambda v: v.tensor_tensor(out=sv[:, :, 11], in0=sv[:, :, 10], in1=sv[:, :, 9], op=ALU.subtract))
    fv(lambda v: v.tensor_tensor(out=sv[:, :, 11], in0=sv[:, :, 11], in1=gbits[:, :, 0], op=ALU.mult))
    fv(lambda v: v.tensor_tensor(out=sv[:, :, 11], in0=sv[:, :, 9], in1=sv[:, :, 11], op=ALU.add))
    fv(lambda v: v.tensor_single_scalar(out=gbits[:, :, 3], in_=sv[:, :, 11], scalar=0.0, op=ALU.is_ge))
    for j in range(4):
        fv(lambda v, j=j: v.tensor_scalar(out=sv[:, :, 12 + j], in0=gbits[:, :, j],
                                          scalar1=-1.0, scalar2=1.0,
                                          op0=ALU.mult, op1=ALU.add))
    gb = lambda j: gbits[:, :, j]
    gc = lambda j: sv[:, :, 12 + j]
    for i2 in range(4):
        b1_, b0_ = i2 // 2, i2 % 2
        t0 = gb(0) if b1_ else gc(0)
        t1 = gb(1) if b0_ else gc(1)
        fv(lambda v, i2=i2, t0=t0, t1=t1: v.tensor_tensor(out=sv[:, :, 16 + i2], in0=t0, in1=t1, op=ALU.mult))
    for leaf in range(16):
        i2 = leaf // 4
        b2_, b3_ = (leaf // 2) % 2, leaf % 2
        t2 = gb(2) if b2_ else gc(2)
        t3 = gb(3) if b3_ else gc(3)
        fv(lambda v, leaf=leaf, i2=i2, t2=t2: v.tensor_tensor(out=sel[:, :, leaf], in0=sv[:, :, 16 + i2], in1=t2, op=ALU.mult))
        fv(lambda v, leaf=leaf, t3=t3: v.tensor_tensor(out=sel[:, :, leaf], in0=sel[:, :, leaf], in1=t3, op=ALU.mult))



def emit_mlstm_mixer(r, cc, b, qT, kTt, vext, ogs, lnwb, aT, mT, enT,
                     scr_m, mseg, colfT, mrefc, mbsb, zsc, ezb, matcT, hh, cenh,
                     sqh, ecols, houtn, houtT, woutT, tril01, maskn,
                     onesf, pd0, pd1, phh, pmb, pfs, x_b, xo_b):
    DHS = 0.08838834764831845
    cols, sq, xt = cc["cols"], cc["sq"], cc["xt"]
    phh_gate = 0
    pmb_gate = 0
    pd_gate = [0, 0]
    hmat_gate = 0
    mseg_gate = 0
    st_before = {}
    pf_gate = [0, 0]
    final_mm_gate = 0
    for qt in range(NT):
        for h in range(8):
            r.wait("y", "MM", mseg_gate)
            ldm = r.dma("y", mseg[:], scr_m[h:h + 1, qt * 128:(qt + 1) * 128])
            r.wait("t", "LD", ldm)
            r.wait("t", "EV", pmb_gate)
            r.add("t", lambda t: t.matmul(pmb[:], onesf[:], mseg[:],
                                          start=True, stop=True), inc="MM")
            mseg_gate = r.cnt["MM"]
            r.wait("v", "MM", r.cnt["MM"])
            r.add("v", lambda v: v.tensor_copy(mbsb[:], pmb[:]))
            r.addf("v", lambda v: v.tensor_scalar(out=mrefc[:, 0:1], in0=pmb[:, 0:1],
                                                  scalar1=-1.0, scalar2=None,
                                                  op0=ALU.mult))
            r.addf("v", lambda v: v.tensor_copy(mrefc[:, 1:2], pmb[:, 0:1]))
            r.add("v", lambda v: v.tensor_copy(r.dummy2[:], r.dummy1[:]), inc="EV")
            pmb_gate = r.cnt["EV"]
            r.wait("s", "VV", r.cnt["VV"])
            r.addf("s", lambda sE, h=h: sE.activation(colfT[:], aT[:, :, h],
                                                      AF.Exp, bias=mrefc[:, 0:1]))
            r.addf("s", lambda sE, h=h, qt=qt: sE.activation(
                ecols[:, 0:1], mT[:, qt, h:h + 1], AF.Exp,
                bias=mrefc[:, 1:2], scale=-1.0))
            r.wait("v", "SS", r.cnt["SS"])
            r.wait("t", "SS", r.cnt["SS"])
            for st in range(qt + 1):
                pd = pd0 if st % 2 == 0 else pd1
                r.wait("t", "EV", pd_gate[st % 2])
                r.add("t", lambda t, st=st, h=h, qt=qt, pd=pd: t.matmul(
                    pd[:], kTt[:, st * 128:(st + 1) * 128],
                    qT[:, h, qt * 128:(qt + 1) * 128], start=True, stop=True),
                    inc="MM")
                mmv = r.cnt["MM"]
                r.wait("v", "MM", mmv)
                r.wait("v", "MM", hmat_gate)
                if st < qt:
                    r.add("v", lambda v, st=st, pd=pd: v.tensor_scalar(
                        out=matcT[:, st, :], in0=pd[:], scalar1=colfT[:, st:st + 1],
                        scalar2=DHS, op0=ALU.mult, op1=ALU.mult), inc="EV")
                else:
                    r.add("v", lambda v, qt=qt, h=h: v.tensor_scalar(
                        out=zsc[:], in0=mbsb[:], scalar1=-1.0,
                        scalar2=aT[:, qt, h:h + 1], op0=ALU.mult, op1=ALU.add))
                    r.add("v", lambda v: v.tensor_mul(zsc[:], zsc[:], tril01[:]))
                    r.add("v", lambda v: v.tensor_add(zsc[:], zsc[:], maskn[:]),
                          inc="VV")
                    r.wait("s", "VV", r.cnt["VV"])
                    r.add("s", lambda sE: sE.activation(ezb[:], zsc[:], AF.Exp),
                          inc="SS")
                    r.wait("v", "SS", r.cnt["SS"])
                    r.add("v", lambda v, qt=qt, pd=pd: v.scalar_tensor_tensor(
                        out=matcT[:, qt, :], in0=pd[:], scalar=DHS, in1=ezb[:],
                        op0=ALU.mult, op1=ALU.mult), inc="EV")
                pd_gate[st % 2] = r.cnt["EV"]
            matc_ev = r.cnt["EV"]
            r.wait("t", "EV", phh_gate)
            r.wait("t", "EV", matc_ev)
            n_groups = 0
            if qt > 0:
                pairs = [(matcT[:, st, :], vext[:, st, :]) for st in range(qt)]
                mm_group(r, phh[:, 0:129], pairs, inc="MM")
                n_groups += 1
            mm_group(r, phh[:, 129:258], [(matcT[:, qt, :], vext[:, qt, :])],
                     inc="MM")
            n_groups += 1
            hmat_gate = r.cnt["MM"]
            r.wait("v", "MM", r.cnt["MM"])
            if qt > 0:
                r.addf("v", lambda v: v.tensor_scalar(out=ecols[:, 1:2],
                                                      in0=phh[:, 128:129],
                                                      scalar1=ecols[:, 0:1],
                                                      scalar2=None, op0=ALU.mult))
                r.addf("v", lambda v: v.tensor_add(ecols[:, 1:2], ecols[:, 1:2],
                                                   phh[:, 257:258]))
            else:
                r.addf("v", lambda v: v.tensor_copy(ecols[:, 1:2], phh[:, 257:258]))
            r.wait("s", "VV", r.cnt["VV"])
            r.addf("s", lambda sE: sE.activation(ecols[:, 2:3], ecols[:, 1:2], AF.Abs))
            r.wait("v", "SS", r.cnt["SS"])
            r.addf("v", lambda v, qt=qt, h=h: v.tensor_tensor(
                out=ecols[:, 3:4], in0=ecols[:, 2:3], in1=enT[:, qt, h:h + 1],
                op=ALU.max))
            r.addf("v", lambda v: v.tensor_scalar(out=ecols[:, 4:5], in0=ecols[:, 3:4],
                                                  scalar1=1e-6, scalar2=None,
                                                  op0=ALU.add))
            r.addf("v", lambda v: v.reciprocal(ecols[:, 5:6], ecols[:, 4:5]))
            r.addf("v", lambda v: v.tensor_tensor(out=ecols[:, 6:7], in0=ecols[:, 5:6],
                                                  in1=ecols[:, 0:1], op=ALU.mult))
            if qt > 0:
                r.add("v", lambda v: v.tensor_scalar(out=hh[:], in0=phh[:, 0:128],
                                                     scalar1=ecols[:, 6:7],
                                                     scalar2=None, op0=ALU.mult))
                r.add("v", lambda v: v.scalar_tensor_tensor(
                    out=hh[:], in0=phh[:, 129:257], scalar=ecols[:, 5:6], in1=hh[:],
                    op0=ALU.mult, op1=ALU.add), inc="EV", n=2)
            else:
                r.add("v", lambda v: v.tensor_scalar(out=hh[:], in0=phh[:, 129:257],
                                                     scalar1=ecols[:, 5:6],
                                                     scalar2=None, op0=ALU.mult),
                      inc="EV")
            phh_gate = r.cnt["EV"]
            r.addf("v", lambda v: v.reduce_sum(ecols[:, 7:8], hh[:],
                                               axis=mybir.AxisListType.X))
            r.addf("v", lambda v: v.tensor_scalar(out=ecols[:, 8:9], in0=ecols[:, 7:8],
                                                  scalar1=1.0 / 128, scalar2=None,
                                                  op0=ALU.mult))
            r.add("v", lambda v: v.tensor_scalar(out=cenh[:], in0=hh[:],
                                                 scalar1=ecols[:, 8:9], scalar2=None,
                                                 op0=ALU.subtract))
            r.add("v", lambda v: v.tensor_mul(sqh[:], cenh[:], cenh[:]))
            r.addf("v", lambda v: v.reduce_sum(ecols[:, 9:10], sqh[:],
                                               axis=mybir.AxisListType.X))
            r.addf("v", lambda v: v.tensor_scalar(out=ecols[:, 10:11],
                                                  in0=ecols[:, 9:10],
                                                  scalar1=1.0 / 128, scalar2=1e-6,
                                                  op0=ALU.mult, op1=ALU.add))
            r.wait("s", "VV", r.cnt["VV"])
            r.addf("s", lambda sE: sE.activation(ecols[:, 11:12], ecols[:, 10:11],
                                                 AF.Ln))
            r.addf("s", lambda sE: sE.activation(ecols[:, 12:13], ecols[:, 11:12],
                                                 AF.Exp, scale=-0.5))
            r.wait("v", "SS", r.cnt["SS"])
            r.add("v", lambda v: v.tensor_scalar(out=sqh[:], in0=cenh[:],
                                                 scalar1=ecols[:, 12:13],
                                                 scalar2=None, op0=ALU.mult))
            r.add("v", lambda v, h=h: v.tensor_mul(sqh[:], sqh[:], lnwb[:, h, :]))
            r.add("v", lambda v, h=h, qt=qt: v.tensor_tensor(
                out=houtn[:, h * 128:(h + 1) * 128], in0=sqh[:],
                in1=ogs[:, qt, h * 128:(h + 1) * 128], op=ALU.mult), inc="VV")
        r.wait("t", "VV", r.cnt["VV"])
        for half in range(2):
            pb = cc["ptb"][half]
            r.wait("v", "MM", final_mm_gate)
            r.wait("t", "EV", r.cnt["MM"] + 1 - 2)
            for j in range(4):
                dt = half * 4 + j
                r.add("t", lambda t, dt=dt, j=j, pb=pb: t.transpose(
                    pb[:, j * 128:(j + 1) * 128],
                    houtn[:, dt * 128:(dt + 1) * 128], cc["identb"]),
                    inc=("MM" if j == 3 else None))
            r.wait("v", "MM", r.cnt["MM"])
            r.add("v", lambda v, half=half, pb=pb: v.tensor_copy(
                houtT[:, half * 4:half * 4 + 4, :],
                pb.rearrange("p (a c) -> p a c", a=4)), inc="EV")
        hout_ev = r.cnt["EV"]
        xb = xt[qt % 2]
        r.wait("y", "ST", st_before.get(qt - 2, 0))
        r.wait("v", "ST", st_before.get(qt - 1, 0))
        ldx = r.dma("y", xb[:], x_b[qt * 128:(qt + 1) * 128, :])
        for oc in range(2):
            pf = pfs[oc]
            r.wait("t", "EV", pf_gate[oc])
            r.wait("t", "EV", hout_ev)
            pairs = [(houtT[:, dt, :], woutT[:, dt, oc * 512:(oc + 1) * 512])
                     for dt in range(ND)]
            mm_group(r, pf[:], pairs, inc="MM")
            r.wait("v", "MM", r.cnt["MM"])
            r.wait("v", "LD", ldx)
            r.add("v", lambda v, pf=pf, oc=oc, xb=xb: v.tensor_tensor(
                out=sq[:, oc * 512:(oc + 1) * 512], in0=pf[:],
                in1=xb[:, oc * 512:(oc + 1) * 512], op=ALU.add), inc="EV")
            pf_gate[oc] = r.cnt["EV"]
            final_mm_gate = r.cnt["MM"]
        r.wait("g", "EV", r.cnt["EV"])
        st_before[qt] = r.cnt["ST"] + 16
        r.dma("g", xo_b[qt * 128:(qt + 1) * 128, :], sq[:], inc="ST")


def emit_sub_l0attn(C, xin, xout, sfx):
    nc, r, cc = C["nc"], C["r"], C["cc"]
    x, xo = xin, xout
    A = C["ap"]
    Wq, Wk, Wv, Wog = A["L0_attn_Wq"], A["L0_attn_Wk"], A["L0_attn_Wv"], A["L0_attn_Wog"]
    Wi, bi, Wf, bf_ = A["L0_attn_Wi"], A["L0_attn_bi"], A["L0_attn_Wf"], A["L0_attn_bf"]
    lnw, Wout = A["L0_attn_ln_w"], A["L0_attn_Wout"]
    identb, identf, onesf = C["identb"], C["identf"], C["onesf"]
    dummy1, dummy2 = r.dummy1, r.dummy2
    xt0, xt1, cen, sq, cols = cc["xt"][0], cc["xt"][1], cc["cen"], cc["sq"], cc["cols"]
    xlnb0, xlnb1 = cc["xlnb"]
    pb0, pb1 = cc["ptb"]
    scr_m = nc.dram_tensor("scr_m" + sfx, [8, T], F32, kind="Internal").ap()
    import contextlib
    stack = contextlib.ExitStack()
    al = stack.enter_context
    wstage = al(nc.sbuf_tensor("wstage" + sfx, [128, 1024], F32))
    wstage_b = al(nc.sbuf_tensor("wstageb" + sfx, [128, 1024], BF16))
    cc["wstage"], cc["wstage_b"] = wstage, wstage_b
    tril01 = al(nc.sbuf_tensor("tril01" + sfx, [128, 128], F32))
    maskn = al(nc.sbuf_tensor("maskn" + sfx, [128, 128], F32))
    wkT = al(nc.sbuf_tensor("wkT" + sfx, [128, 8, 128], BF16))
    wvT = al(nc.sbuf_tensor("wvT" + sfx, [128, 8, 128], BF16))
    wiT = al(nc.sbuf_tensor("wiT" + sfx, [128, 8, 8], BF16))
    wfT = al(nc.sbuf_tensor("wfT" + sfx, [128, 8, 8], BF16))
    lnwb = al(nc.sbuf_tensor("lnwb" + sfx, [128, 8, 128], BF16))
    bcols = al(nc.sbuf_tensor("bcols" + sfx, [8, 4], F32))
    xlnT = al(nc.sbuf_tensor("xlnTa" + sfx, [128, 8, T], BF16))
    qT = al(nc.sbuf_tensor("qTa" + sfx, [128, 8, T], BF16))
    ogs = al(nc.sbuf_tensor("ogs" + sfx, [128, NT, D], BF16))
    kTt = al(nc.sbuf_tensor("kTt" + sfx, [128, T], BF16))
    vext = al(nc.sbuf_tensor("vext" + sfx, [128, NT, 129], BF16))
    gA = al(nc.sbuf_tensor("gA" + sfx, [40, T], F32))
    gB = al(nc.sbuf_tensor("gB" + sfx, [8, T], F32))
    gC = al(nc.sbuf_tensor("gC" + sfx, [8, T], F32))
    gD = al(nc.sbuf_tensor("gD" + sfx, [8, T], F32))
    aT = al(nc.sbuf_tensor("aT" + sfx, [128, 16, 8], F32))
    mT = al(nc.sbuf_tensor("mT" + sfx, [128, 16, 8], F32))
    enT = al(nc.sbuf_tensor("enT" + sfx, [128, 16, 8], F32))
    colfT = al(nc.sbuf_tensor("colfT" + sfx, [128, 16], F32))
    mseg = al(nc.sbuf_tensor("mseg" + sfx, [1, 128], F32))
    mrefc = al(nc.sbuf_tensor("mrefc" + sfx, [128, 2], F32))
    mbsb = al(nc.sbuf_tensor("mbsb" + sfx, [128, 128], F32))
    zsc = al(nc.sbuf_tensor("zsc" + sfx, [128, 128], F32))
    ezb = al(nc.sbuf_tensor("ezb" + sfx, [128, 128], BF16))
    matcT = al(nc.sbuf_tensor("matcT" + sfx, [128, NT, 128], BF16))
    hh = al(nc.sbuf_tensor("hh" + sfx, [128, 128], F32))
    cenh = al(nc.sbuf_tensor("cenh" + sfx, [128, 128], F32))
    sqh = al(nc.sbuf_tensor("sqh" + sfx, [128, 128], F32))
    ecols = al(nc.sbuf_tensor("ecols" + sfx, [128, 16], F32))
    houtn = al(nc.sbuf_tensor("houtn" + sfx, [128, 1024], BF16))
    houtT = al(nc.sbuf_tensor("houtT" + sfx, [128, 8, 128], BF16))
    woutT = al(nc.sbuf_tensor("woutT" + sfx, [128, 8, D], BF16))
    pf0 = al(nc.psum_tensor("pf0" + sfx, [128, 512], F32))
    pf1 = al(nc.psum_tensor("pf1" + sfx, [128, 512], F32))
    pd0 = al(nc.psum_tensor("pd0" + sfx, [128, 128], F32))
    pd1 = al(nc.psum_tensor("pd1" + sfx, [128, 128], F32))
    phh = al(nc.psum_tensor("phh" + sfx, [128, 258], F32))
    pmb = al(nc.psum_tensor("pmb" + sfx, [128, 128], F32))
    if True:
        pfs = [pf0, pf1]

        r.dma("y", tril01[:], C["ap"]["c_tril01"])
        r.dma("y", maskn[:], C["ap"]["c_maskadd"])
        emit_weight_T(r, Wk, 128, wkT, cc)
        emit_weight_T(r, Wv, 128, wvT, cc)
        emit_weight_T(r, Wi, 8, wiT, cc)
        emit_weight_T(r, Wf, 8, wfT, cc)
        r.wait("y", "VV", r.cnt["VV"])
        ldb = r.dma("y", wstage[0:1, 0:8], bi)
        r.wait("t", "LD", ldb)
        r.wait("t", "EV", r.cnt["MM"])
        r.add("t", lambda t: t.matmul(pf0[0:8, 0:1], wstage[0:1, 0:8],
                                      onesf[0:1, 0:1], start=True, stop=True), inc="MM")
        r.wait("v", "MM", r.cnt["MM"])
        r.addf("v", lambda v: v.tensor_scalar(out=bcols[:, 0:1], in0=pf0[0:8, 0:1],
                                              scalar1=1.0 / 15.0, scalar2=None,
                                              op0=ALU.mult))
        r.add("v", lambda v: v.tensor_copy(r.dummy2[:], r.dummy1[:]), inc="EV")
        r.wait("y", "MM", r.cnt["MM"])
        ldb = r.dma("y", wstage[0:1, 0:8], bf_)
        r.wait("t", "LD", ldb)
        r.wait("t", "EV", r.cnt["EV"])
        r.add("t", lambda t: t.matmul(pf1[0:8, 0:1], wstage[0:1, 0:8],
                                      onesf[0:1, 0:1], start=True, stop=True), inc="MM")
        r.wait("v", "MM", r.cnt["MM"])
        r.addf("v", lambda v: v.tensor_scalar(out=bcols[:, 1:2], in0=pf1[0:8, 0:1],
                                              scalar1=1.0 / 15.0, scalar2=None,
                                              op0=ALU.mult))
        r.add("v", lambda v: v.tensor_copy(r.dummy2[:], r.dummy1[:]), inc="EV")
        r.wait("y", "MM", r.cnt["MM"])
        ldl = r.dma("y", wstage[0:1, :], lnw)
        r.wait("t", "LD", ldl)
        for h in range(8):
            r.wait("t", "EV", r.cnt["MM"] + 1 - 2)
            r.add("t", lambda t, h=h: t.matmul(
                [pf0, pf1][h % 2][:, 0:128], onesf[:],
                wstage[0:1, h * 128:(h + 1) * 128], start=True, stop=True), inc="MM")
            r.wait("v", "MM", r.cnt["MM"])
            r.add("v", lambda v, h=h: v.tensor_copy(lnwb[:, h, :],
                                                    [pf0, pf1][h % 2][:, 0:128]),
                  inc="EV")
        r.barrier()

        for b in range(B_LOC):
            emit_ln_transpose(r, x[b], xlnT, cc)
            r.barrier()
            pipe = PsumPipe(r, pfs)
            emit_weight_T(r, Wq, D, woutT, cc)
            r.barrier()
            for ot in range(ND):
                for tc in range(4):
                    pairs = [(woutT[:, dt, ot * 128:(ot + 1) * 128],
                              xlnT[:, dt, tc * 512:(tc + 1) * 512]) for dt in range(ND)]
                    pipe.step(pairs, [
                        ("s", lambda sE, ps, ot=ot, tc=tc: sE.copy(
                            qT[:, ot, tc * 512:(tc + 1) * 512], ps[:])),
                    ])
            r.barrier()
            emit_weight_T(r, Wog, D, woutT, cc)
            r.barrier()
            for tt in range(NT):
                for oc in range(2):
                    pairs = [(xlnT[:, dt, tt * 128:(tt + 1) * 128],
                              woutT[:, dt, oc * 512:(oc + 1) * 512]) for dt in range(ND)]
                    pipe.step(pairs, [
                        ("s", lambda sE, ps, tt=tt, oc=oc: sE.activation(
                            ogs[:, tt, oc * 512:(oc + 1) * 512], ps[:], AF.Sigmoid)),
                    ])
            r.barrier()
            for tc in range(4):
                pairs = [(wkT[:, dt, :], xlnT[:, dt, tc * 512:(tc + 1) * 512])
                         for dt in range(ND)]
                pipe.step(pairs, [
                    ("s", lambda sE, ps, tc=tc: sE.copy(
                        kTt[:, tc * 512:(tc + 1) * 512], ps[:])),
                ])
            for tt in range(NT):
                pairs = [(xlnT[:, dt, tt * 128:(tt + 1) * 128], wvT[:, dt, :])
                         for dt in range(ND)]
                pipe.step(pairs, [
                    ("s", lambda sE, ps, tt=tt: sE.copy(vext[:, tt, 0:128],
                                                        ps[:, 0:128])),
                ], width=128)
            r.addf("v", lambda v: v.tensor_scalar(
                out=vext[:, :, 128], in0=vext[:, :, 0],
                scalar1=0.0, scalar2=1.0, op0=ALU.mult, op1=ALU.add))
            for gi, (wT, bc) in enumerate(((wiT, 0), (wfT, 1))):
                dst = gA
                base = 0 if gi == 0 else 32
                for tc in range(4):
                    pairs = [(wT[:, dt, :], xlnT[:, dt, tc * 512:(tc + 1) * 512])
                             for dt in range(ND)]
                    pipe.step(pairs, [
                        ("s", lambda sE, ps, base=base, tc=tc, bc=bc: sE.activation(
                            gA[base:base + 8, tc * 512:(tc + 1) * 512], ps[0:8, 0:512],
                            AF.Tanh, bias=bcols[0:8, bc:bc + 1], scale=1.0 / 15.0)),
                    ], width=512, rows=8)
            r.barrier()
            r.add("s", lambda sE: sE.activation(gB[:], gA[32:40, :], AF.Abs,
                                                scale=15.0))
            r.add("s", lambda sE: sE.activation(gB[:], gB[:], AF.Exp, scale=-1.0))
            r.add("s", lambda sE: sE.activation(gB[:], gB[:], AF.Ln, bias=1.0))
            r.add("s", lambda sE: sE.activation(gC[:], gA[32:40, :], AF.Relu,
                                                scale=-15.0), inc="SS")
            r.wait("v", "SS", r.cnt["SS"])
            r.add("v", lambda v: v.tensor_add(gB[:], gB[:], gC[:]))
            r.add("v", lambda v: v.tensor_scalar(out=gC[:], in0=gB[:],
                                                 scalar1=0.0, scalar2=1.0,
                                                 op0=ALU.mult, op1=ALU.add))
            r.add("v", lambda v: v.tensor_tensor_scan(out=gD[:], data0=gC[:],
                                                      data1=gB[:], initial=0.0,
                                                      op0=ALU.mult, op1=ALU.add))
            r.add("v", lambda v: v.scalar_tensor_tensor(out=gC[:], in0=gA[0:8, :],
                                                        scalar=15.0, in1=gD[:],
                                                        op0=ALU.mult, op1=ALU.add))
            r.add("v", lambda v: v.tensor_tensor_scan(out=gA[0:8, :], data0=gC[:],
                                                      data1=gC[:], initial=-1e30,
                                                      op0=ALU.max, op1=ALU.max))
            r.add("v", lambda v: v.tensor_sub(gB[:], gD[:], gA[0:8, :]),
                  inc="VV")
            r.wait("s", "VV", r.cnt["VV"])
            r.add("s", lambda sE: sE.activation(gB[:], gB[:], AF.Exp),
                  inc="SS")
            r.wait("g", "SS", r.cnt["SS"])
            r.wait("v", "SS", r.cnt["SS"])
            r.dma("g", scr_m[:, :], gA[0:8, :], inc="ST")
            for src, dst, ib in ((gC[:], aT, 0), (gA[0:8, :], mT, 0),
                                 (gB[:], enT, 0)):
                for tt in range(NT):
                    r.wait("t", "VV", r.cnt["VV"])
                    r.wait("t", "SS", r.cnt["SS"])
                    r.wait("t", "EV", r.cnt["MM"] + 1 - 2)
                    r.add("t", lambda t, src=src, tt=tt, ib=ib: t.transpose(
                        pd0[:, 0:8] if tt % 2 == 0 else pd1[:, 0:8],
                        src[:, tt * 128:(tt + 1) * 128],
                        identf[ib:ib + 8, ib:ib + 8]), inc="MM")
                    r.wait("v", "MM", r.cnt["MM"])
                    r.add("v", lambda v, dst=dst, tt=tt: v.tensor_copy(
                        dst[:, tt, :], (pd0 if tt % 2 == 0 else pd1)[:, 0:8]),
                        inc="EV")
            r.barrier()
            emit_weight_T(r, Wout, D, woutT, cc)
            r.barrier()
            emit_mlstm_mixer(r, cc, b, qT, kTt, vext, ogs, lnwb, aT, mT, enT,
                             scr_m, mseg, colfT, mrefc, mbsb, zsc, ezb, matcT, hh, cenh,
                             sqh, ecols, houtn, houtT, woutT, tril01, maskn,
                             onesf, pd0, pd1, phh, pmb, pfs, x[b], xo[b])
            r.barrier()
            cc.pop("vv_ln", None)
    stack.close()


def emit_sub_l1attn(C, xin, xout, sfx):
    nc, r, cc = C["nc"], C["r"], C["cc"]
    x, xo = xin, xout
    A = C["ap"]
    HD = 4096
    NDH = 32
    ip, op = A["L1_attn_input_proj"], A["L1_attn_output_proj"]
    alog, bp, cp = A["L1_attn_a_log"], A["L1_attn_b_proj"], A["L1_attn_c_proj"]
    dtp, skp, flq = A["L1_attn_dt_proj"], A["L1_attn_skip_proj"], A["L1_attn_floquet_phases"]
    identb, onesf = C["identb"], C["onesf"]
    dummy1, dummy2 = r.dummy1, r.dummy2
    xt0, xt1, cen, sq, cols = cc["xt"][0], cc["xt"][1], cc["cen"], cc["sq"], cc["cols"]
    xlnb0, xlnb1 = cc["xlnb"]
    pb0, pb1 = cc["ptb"]
    scr_hdT = nc.dram_tensor("scr_hdT" + sfx, [HD, T], BF16, kind="Internal").ap()
    scr_henc = nc.dram_tensor("scr_henc" + sfx, [HD, T], BF16, kind="Internal").ap()
    scr_yskip = nc.dram_tensor("scr_yskip" + sfx, [HD, T], BF16, kind="Internal").ap()
    scr_outhd = nc.dram_tensor("scr_outhd" + sfx, [HD, T], BF16, kind="Internal").ap()
    scr_bc = nc.dram_tensor("scr_bc" + sfx, [2, 16, T], BF16, kind="Internal").ap()
    import contextlib
    stack = contextlib.ExitStack()
    al = stack.enter_context
    phi2p = al(nc.sbuf_tensor("phi2p" + sfx, [128, 32], F32))
    dts = al(nc.sbuf_tensor("dts" + sfx, [128, 32], F32))
    dss = al(nc.sbuf_tensor("dss" + sfx, [128, 32, 3], F32))
    wtab = al(nc.sbuf_tensor("wtab" + sfx, [128, 32, 16], F32))
    abc = al(nc.sbuf_tensor("abc" + sfx, [128, 16], F32))
    arow = al(nc.sbuf_tensor("arow" + sfx, [1, 16], F32))
    rrow = al(nc.sbuf_tensor("rrow" + sfx, [1, 4096], F32))
    iotab = al(nc.sbuf_tensor("iotab" + sfx, [128, T], F32))
    pf0 = al(nc.psum_tensor("pf0" + sfx, [128, 512], F32))
    pf1 = al(nc.psum_tensor("pf1" + sfx, [128, 512], F32))
    pf2 = al(nc.psum_tensor("pf2" + sfx, [128, 512], F32))
    pf3 = al(nc.psum_tensor("pf3" + sfx, [128, 512], F32))
    pBc = al(nc.psum_tensor("pBc" + sfx, [16, 512], F32))
    pCc = al(nc.psum_tensor("pCc" + sfx, [16, 512], F32))
    if True:
        pfs = [pf0, pf1, pf2, pf3]
        TWO_PI = 6.283185307179586
        INV2P = 0.15915494309189535
        RINT_C = 12582912.0

        ldi = r.dma("y", rrow[0:1, 0:T], C["ap"]["c_iota"])
        r.wait("t", "LD", r.cnt["LD"])
        for tc2 in range(4):
            r.wait("t", "EV", r.cnt["MM"] + 1 - 4)
            r.add("t", lambda t, tc2=tc2: t.matmul(
                pfs[tc2][:], onesf[:], rrow[0:1, tc2 * 512:(tc2 + 1) * 512],
                start=True, stop=True), inc="MM")
            r.wait("v", "MM", r.cnt["MM"])
            r.add("v", lambda v, tc2=tc2: v.tensor_copy(
                iotab[:, tc2 * 512:(tc2 + 1) * 512], pfs[tc2][:]), inc="EV")
        for name, src, dst in (("phi", flq, phi2p), ("dt", dtp, dts)):
            r.wait("y", "VV", r.cnt["VV"])
            r.wait("y", "MM", r.cnt["MM"])
            ldr = r.dma("y", rrow[0:1, 0:HD], src)
            r.wait("t", "LD", ldr)
            for j in range(NDH):
                if j % 16 == 0:
                    r.wait("t", "EV", r.cnt["MM"] + 1 - 2)
                r.add("t", lambda t, j=j: t.matmul(
                    pfs[(j // 16) % 2][:, (j % 16):(j % 16) + 1],
                    rrow[0:1, j * 128:(j + 1) * 128], onesf[0:1, 0:1],
                    start=True, stop=True),
                    inc=("MM" if j % 16 == 15 else None))
                if j % 16 == 15:
                    r.wait("v", "MM", r.cnt["MM"])
                    r.add("v", lambda v, j=j, dst=dst: v.tensor_copy(
                        dst[:, j - 15:j + 1], pfs[(j // 16) % 2][:, 0:16]), inc="EV")
        r.addf("v", lambda v: v.tensor_scalar(out=phi2p[:], in0=phi2p[:],
                                              scalar1=INV2P, scalar2=None,
                                              op0=ALU.mult))
        r.wait("s", "VV", r.cnt["VV"])
        r.addf("s", lambda sE: sE.activation(dss[:, :, 0], dts[:], AF.Abs))
        r.addf("s", lambda sE: sE.activation(dss[:, :, 0], dss[:, :, 0], AF.Exp,
                                             scale=-1.0))
        r.addf("s", lambda sE: sE.activation(dss[:, :, 0], dss[:, :, 0], AF.Ln,
                                             bias=1.0))
        r.addf("s", lambda sE: sE.activation(dss[:, :, 1], dts[:], AF.Relu))
        r.wait("v", "SS", r.cnt["SS"])
        r.addf("v", lambda v: v.tensor_add(dts[:], dss[:, :, 0], dss[:, :, 1]))
        r.wait("y", "MM", r.cnt["MM"])
        lda = r.dma("y", arow[:], alog)
        r.wait("s", "LD", lda)
        r.addf("s", lambda sE: sE.activation(arow[:], arow[:], AF.Exp))
        r.wait("t", "SS", r.cnt["SS"])
        r.wait("t", "EV", r.cnt["MM"] + 1 - 4)
        r.add("t", lambda t: t.matmul(pf0[:, 0:16], onesf[:], arow[:],
                                      start=True, stop=True), inc="MM")
        r.wait("v", "MM", r.cnt["MM"])
        r.addf("v", lambda v: v.tensor_scalar(out=abc[:], in0=pf0[:, 0:16],
                                              scalar1=-1.0, scalar2=None,
                                              op0=ALU.mult))
        r.add("v", lambda v: v.tensor_copy(r.dummy2[:], r.dummy1[:]), inc="EV")
        for dht in range(NDH):
            r.addf("v", lambda v, dht=dht: v.tensor_scalar(
                out=wtab[:, dht, :], in0=abc[:], scalar1=dts[:, dht:dht + 1],
                scalar2=None, op0=ALU.mult))
        r.barrier()

        for b in range(B_LOC):
            with contextlib.ExitStack() as sA:
                aa = sA.enter_context
                xlnT = aa(nc.sbuf_tensor(f"xlnT{b}" + sfx, [128, 8, T], BF16))
                emit_ln_transpose(r, x[b], xlnT, cc)
                r.barrier()
                ipf = aa(nc.sbuf_tensor(f"ipf{b}" + sfx, [128, 8, 128], F32))
                ipb = aa(nc.sbuf_tensor(f"ipb{b}" + sfx, [128, 8, 128], BF16))
                hdt = aa(nc.sbuf_tensor(f"hdt{b}" + sfx, [128, T], BF16))
                het = aa(nc.sbuf_tensor(f"het{b}" + sfx, [128, T], BF16))
                mtl = aa(nc.sbuf_tensor(f"mtl{b}" + sfx, [128, T], F32))
                rtl = aa(nc.sbuf_tensor(f"rtl{b}" + sfx, [128, T], F32))
                cosv = aa(nc.sbuf_tensor(f"cosv{b}" + sfx, [128, T], BF16))
                pipe = PsumPipe(r, pfs)
                st_hd = {}
                for dht in range(NDH):
                    r.wait("y", "VV", r.cnt["VV"])
                    r.wait("y", "MM", r.cnt["MM"])
                    ldw = r.dma("y", ipf[:],
                                ip.rearrange("(dt p) c -> p dt c", p=128)
                                [:, :, dht * 128:(dht + 1) * 128])
                    r.wait("v", "LD", ldw)
                    r.wait("v", "ST", st_hd.get(dht - 2, 0))
                    r.add("v", lambda v: v.tensor_copy(ipb[:], ipf[:]), inc="VV")
                    vv_w = r.cnt["VV"]
                    for tc in range(4):
                        pairs = [(ipb[:, dt, :], xlnT[:, dt, tc * 512:(tc + 1) * 512])
                                 for dt in range(ND)]
                        pipe.step(pairs, [
                            ("s", lambda sE, ps, tc=tc: sE.copy(
                                hdt[:, tc * 512:(tc + 1) * 512], ps[:])),
                        ], extra_waits=[("t", "VV", vv_w),
                                        ("s", "ST", st_hd.get(dht - 1, 0))])
                    r.wait("v", "EV", r.cnt["EV"])
                    r.addf("v", lambda v, dht=dht: v.tensor_scalar(
                        out=mtl[:], in0=iotab[:], scalar1=phi2p[:, dht:dht + 1],
                        scalar2=-0.25, op0=ALU.mult, op1=ALU.add))
                    r.add("v", lambda v: v.tensor_scalar(
                        out=rtl[:], in0=mtl[:], scalar1=RINT_C, scalar2=-RINT_C,
                        op0=ALU.add, op1=ALU.add))
                    r.add("v", lambda v: v.tensor_sub(mtl[:], mtl[:], rtl[:]),
                          inc="VV")
                    r.wait("s", "VV", r.cnt["VV"])
                    r.add("s", lambda sE: sE.activation(cosv[:], mtl[:], AF.Sin,
                                                        scale=-TWO_PI), inc="SS")
                    r.wait("v", "SS", r.cnt["SS"])
                    r.add("v", lambda v: v.tensor_mul(het[:], hdt[:], cosv[:]),
                          inc="VV")
                    r.wait("g", "VV", r.cnt["VV"])
                    r.wait("g", "SS", r.cnt["SS"])
                    r.dma("g", scr_hdT[dht * 128:(dht + 1) * 128, :], hdt[:], inc="ST")
                    r.dma("g", scr_henc[dht * 128:(dht + 1) * 128, :], het[:], inc="ST")
                    st_hd[dht] = r.cnt["ST"]
                r.barrier()
            r.wait("y", "ST", r.cnt["ST"])
            with contextlib.ExitStack() as sC:
                ac = sC.enter_context
                bpb = ac(nc.sbuf_tensor(f"bpb{b}" + sfx, [128, NDH, 16], BF16))
                bpf = ac(nc.sbuf_tensor(f"bpf{b}" + sfx, [128, NDH, 16], F32))
                cpb = ac(nc.sbuf_tensor(f"cpb{b}" + sfx, [128, NDH, 16], BF16))
                het2a = ac(nc.sbuf_tensor(f"het2a{b}" + sfx, [128, 512], BF16))
                het2b = ac(nc.sbuf_tensor(f"het2b{b}" + sfx, [128, 512], BF16))
                Bf = ac(nc.sbuf_tensor(f"Bf{b}" + sfx, [16, T], BF16))
                Cf = ac(nc.sbuf_tensor(f"Cf{b}" + sfx, [16, T], BF16))
                ldb = r.dma("y", bpf[:], bp.rearrange("(a p) n -> p a n", p=128))
                r.wait("v", "LD", ldb)
                r.add("v", lambda v: v.tensor_copy(bpb[:], bpf[:]), inc="VV")
                r.wait("y", "VV", r.cnt["VV"])
                ldc = r.dma("y", bpf[:], cp.rearrange("(a p) n -> p a n", p=128))
                r.wait("v", "LD", ldc)
                r.add("v", lambda v: v.tensor_copy(cpb[:], bpf[:]), inc="VV")
                r.wait("t", "VV", r.cnt["VV"])
                gg_flat = []
                for tc in range(4):
                    for dht in range(NDH):
                        buf = het2a if dht % 2 == 0 else het2b
                        gi = tc * NDH + dht
                        r.wait("y", "GG", gg_flat[gi - 2] if gi >= 2 else 0)
                        ldh = r.dma("y", buf[:],
                                    scr_henc[dht * 128:(dht + 1) * 128,
                                             tc * 512:(tc + 1) * 512])
                        r.wait("t", "LD", ldh)
                        if dht == 0:
                            r.wait("t", "EV", r.cnt["EV"])
                        r.add("t", lambda t, dht=dht, buf=buf: t.matmul(
                            pBc[:], bpb[:, dht, :], buf[:],
                            start=(dht == 0), stop=(dht == NDH - 1)),
                            inc=("MM" if dht == NDH - 1 else None))
                        r.add("t", lambda t, dht=dht, buf=buf: t.matmul(
                            pCc[:], cpb[:, dht, :], buf[:],
                            start=(dht == 0), stop=(dht == NDH - 1)), inc="GG")
                        gg_flat.append(r.cnt["GG"])
                    r.wait("v", "MM", r.cnt["MM"])
                    r.add("v", lambda v, tc=tc: v.tensor_copy(
                        Bf[:, tc * 512:(tc + 1) * 512], pBc[:]))
                    r.add("v", lambda v, tc=tc: v.tensor_copy(
                        Cf[:, tc * 512:(tc + 1) * 512], pCc[:]), inc="EV")
                r.wait("g", "EV", r.cnt["EV"])
                r.dma("g", scr_bc[0], Bf[:], inc="ST")
                r.dma("g", scr_bc[1], Cf[:], inc="ST")
                r.barrier()
            r.wait("y", "ST", r.cnt["ST"])
            with contextlib.ExitStack() as sD:
                ad = sD.enter_context
                hdf = ad(nc.sbuf_tensor(f"hdf{b}" + sfx, [128, NDH, T], BF16))
                skf = ad(nc.sbuf_tensor(f"skf{b}" + sfx, [128, NDH, 128], F32))
                skb = ad(nc.sbuf_tensor(f"skb{b}" + sfx, [128, NDH, 128], BF16))
                ysk = ad(nc.sbuf_tensor(f"ysk{b}" + sfx, [128, T], BF16))
                ldh = r.dma("y", hdf[:], scr_hdT.rearrange("(a p) t -> p a t", p=128))
                pipe = PsumPipe(r, pfs)
                st_sk = {}
                for dho in range(NDH):
                    r.wait("y", "VV", r.cnt["VV"])
                    r.wait("y", "MM", r.cnt["MM"])
                    ldw = r.dma("y", skf[:],
                                skp.rearrange("(a p) c -> p a c", p=128)
                                [:, :, dho * 128:(dho + 1) * 128])
                    r.wait("v", "LD", ldw)
                    r.wait("v", "ST", st_sk.get(dho - 1, 0))
                    r.add("v", lambda v: v.tensor_copy(skb[:], skf[:]), inc="VV")
                    vv_w = r.cnt["VV"]
                    if dho == 0:
                        r.wait("t", "LD", ldh)
                    for tc in range(4):
                        pairs = [(skb[:, a, :], hdf[:, a, tc * 512:(tc + 1) * 512])
                                 for a in range(NDH)]
                        pipe.step(pairs, [
                            ("s", lambda sE, ps, tc=tc: sE.copy(
                                ysk[:, tc * 512:(tc + 1) * 512], ps[:])),
                        ], extra_waits=[("t", "VV", vv_w)])
                    r.wait("g", "EV", r.cnt["EV"])
                    r.dma("g", scr_yskip[dho * 128:(dho + 1) * 128, :], ysk[:],
                          inc="ST")
                    st_sk[dho] = r.cnt["ST"]
                r.barrier()
            r.wait("y", "ST", r.cnt["ST"])
            with contextlib.ExitStack() as sE_:
                ae = sE_.enter_context
                ygrp = ae(nc.sbuf_tensor(f"ygrp{b}" + sfx, [128, 4, T], F32))
                hencg = ae(nc.sbuf_tensor(f"hencg{b}" + sfx, [128, 4, T], BF16))
                Bbc = [ae(nc.sbuf_tensor(f"Bbc{i}{b}" + sfx, [128, T], BF16))
                       for i in range(2)]
                Cbc = [ae(nc.sbuf_tensor(f"Cbc{i}{b}" + sfx, [128, T], BF16))
                       for i in range(2)]
                wt = [ae(nc.sbuf_tensor(f"wt{i}{b}" + sfx, [128, T], F32))
                      for i in range(2)]
                ut = ae(nc.sbuf_tensor(f"ut{b}" + sfx, [128, T], F32))
                Ft = ae(nc.sbuf_tensor(f"Ft{b}" + sfx, [128, T], F32))
                tmpt = ae(nc.sbuf_tensor(f"tmpt{b}" + sfx, [128, T], F32))
                yskg = ae(nc.sbuf_tensor(f"yskg{b}" + sfx, [128, 4, T], BF16))
                emit_ssm_scan(r, b, ygrp, hencg, Bbc, Cbc, wt, ut, Ft, tmpt, yskg,
                              iotab, wtab, dts, pfs, scr_henc, scr_yskip,
                              scr_outhd, scr_bc)
                r.barrier()
            r.wait("y", "ST", r.cnt["ST"])
            with contextlib.ExitStack() as sF:
                af = sF.enter_context
                opf = af(nc.sbuf_tensor(f"opf{b}" + sfx, [128, NDH, 512], F32))
                opb = af(nc.sbuf_tensor(f"opb{b}" + sfx, [128, NDH, D], BF16))
                otl = af(nc.sbuf_tensor(f"otl{b}" + sfx, [128, NDH, 128], BF16))
                for oc in range(2):
                    ldo = r.dma("y", opf[:],
                                op.rearrange("(a p) e -> p a e", p=128)
                                [:, :, oc * 512:(oc + 1) * 512])
                    r.wait("v", "LD", ldo)
                    r.add("v", lambda v, oc=oc: v.tensor_copy(
                        opb[:, :, oc * 512:(oc + 1) * 512], opf[:]), inc="VV")
                    if oc == 0:
                        r.wait("y", "VV", r.cnt["VV"])
                r.wait("t", "VV", r.cnt["VV"])
                pipe = PsumPipe(r, pfs)
                st_f = {}
                for tt in range(NT):
                    r.wait("y", "MM", r.cnt["MM"])
                    ldt = r.dma("y", otl[:],
                                scr_outhd.rearrange("(a p) t -> p a t", p=128)
                                [:, :, tt * 128:(tt + 1) * 128])
                    xb = xt0 if tt % 2 == 0 else xt1
                    r.wait("y", "ST", st_f.get(tt - 2, 0))
                    r.wait("v", "ST", st_f.get(tt - 1, 0))
                    ldx = r.dma("y", xb[:], x[b, tt * 128:(tt + 1) * 128, :])
                    for oc in range(2):
                        pairs = [(otl[:, a, :], opb[:, a, oc * 512:(oc + 1) * 512])
                                 for a in range(NDH)]
                        pipe.step(pairs, [
                            ("v", lambda v, ps, oc=oc, xb=xb: v.tensor_tensor(
                                out=sq[:, oc * 512:(oc + 1) * 512], in0=ps[:],
                                in1=xb[:, oc * 512:(oc + 1) * 512], op=ALU.add)),
                        ], extra_waits=[("t", "LD", ldt), ("v", "LD", ldx)])
                    r.wait("g", "EV", r.cnt["EV"])
                    st_f[tt] = r.cnt["ST"] + 16
                    r.dma("g", xo[b, tt * 128:(tt + 1) * 128, :], sq[:], inc="ST")
                r.barrier()
            cc.pop("vv_ln", None)
    stack.close()


def emit_ssm_scan(r, b, ygrp, hencg, Bbc, Cbc, wt, ut, Ft, tmpt, yskg,
                  iotab, wtab, dts, pfs, scr_henc, scr_yskip,
                  scr_outhd, scr_bc):
    henc_ev = 0
    st_out = 0
    bc_gate = []
    wt_gate = []
    bchandle = scr_bc.tensor if hasattr(scr_bc, "tensor") else scr_bc
    for grp in range(8):
        r.wait("y", "VV", henc_ev)
        r.wait("y", "ST", st_out)
        ldh = r.dma("y", hencg[:],
                    scr_henc.rearrange("(g j p) t -> p (g j) t", p=128, j=4)
                    [:, grp * 4:(grp + 1) * 4, :])
        lds = r.dma("y", yskg[:],
                    scr_yskip.rearrange("(g j p) t -> p (g j) t", p=128, j=4)
                    [:, grp * 4:(grp + 1) * 4, :])
        for n in range(16):
            gbi = grp * 16 + n
            bi = gbi % 2
            r.wait("y", "VV", bc_gate[gbi - 2] if gbi >= 2 else 0)
            r.wait("g", "VV", bc_gate[gbi - 2] if gbi >= 2 else 0)
            ldb = r.dma("y", Bbc[bi][:],
                        bass.AP(bchandle, (0 * 16 + n) * T, [[0, 128], [1, T]]))
            ldc = r.dma("g", Cbc[bi][:],
                        bass.AP(bchandle, (1 * 16 + n) * T, [[0, 128], [1, T]]))
            if n == 0:
                r.wait("v", "LD", ldh)
            r.wait("v", "LD", ldb)
            r.wait("v", "LD", ldc)
            for j in range(4):
                dht = grp * 4 + j
                wfi = (grp * 16 + n) * 4 + j
                wi = wfi % 2
                r.wait("s", "VV", wt_gate[wfi - 2] if wfi >= 2 else 0)
                r.add("s", lambda sE, dht=dht, n=n, wi=wi: sE.activation(
                    wt[wi][:], iotab[:], AF.Exp,
                    bias=wtab[:, dht, n:n + 1], scale=0.0), inc="SS")
                ss_wt = r.cnt["SS"]
                r.add("v", lambda v, j=j, dht=dht, bi=bi: v.scalar_tensor_tensor(
                    out=ut[:], in0=hencg[:, j, :], scalar=dts[:, dht:dht + 1],
                    in1=Bbc[bi][:], op0=ALU.mult, op1=ALU.mult))
                r.wait("v", "SS", ss_wt)
                r.add("v", lambda v, wi=wi: v.tensor_tensor_scan(
                    out=Ft[:], data0=wt[wi][:], data1=ut[:], initial=0.0,
                    op0=ALU.mult, op1=ALU.add), inc="VV")
                wt_gate.append(r.cnt["VV"])
                if n == 0:
                    r.add("v", lambda v, j=j, bi=bi: v.tensor_mul(
                        ygrp[:, j, :], Ft[:], Cbc[bi][:]))
                else:
                    r.add("v", lambda v, bi=bi: v.tensor_mul(tmpt[:], Ft[:],
                                                             Cbc[bi][:]))
                    r.add("v", lambda v, j=j: v.tensor_add(ygrp[:, j, :],
                                                           ygrp[:, j, :], tmpt[:]))
            r.add("v", lambda v: v.tensor_copy(r.dummy2[:], r.dummy1[:]), inc="VV")
            bc_gate.append(r.cnt["VV"])
        r.wait("v", "LD", lds)
        for j in range(4):
            r.add("v", lambda v, j=j: v.tensor_add(ygrp[:, j, :], ygrp[:, j, :],
                                                   yskg[:, j, :]))
            r.add("v", lambda v, j=j: v.tensor_copy(hencg[:, j, :], ygrp[:, j, :]),
                  inc="VV")
        henc_ev = r.cnt["VV"]
        r.wait("g", "VV", r.cnt["VV"])
        r.dma("g", scr_outhd.rearrange("(g j p) t -> p (g j) t", p=128, j=4)
              [:, grp * 4:(grp + 1) * 4, :], hencg[:], inc="ST")
        st_out = r.cnt["ST"]



INPUT_SPECS = {
    "L0_attn_Wq": [D, D], "L0_attn_Wk": [128, D], "L0_attn_Wv": [128, D],
    "L0_attn_Wog": [D, D], "L0_attn_Wi": [8, D], "L0_attn_bi": [1, 8],
    "L0_attn_Wf": [8, D], "L0_attn_bf": [1, 8], "L0_attn_ln_w": [1, D],
    "L0_attn_Wout": [D, D],
    "L1_attn_input_proj": [D, 4096], "L1_attn_output_proj": [4096, D],
    "L1_attn_a_log": [1, 16], "L1_attn_b_proj": [4096, 16],
    "L1_attn_c_proj": [4096, 16], "L1_attn_dt_proj": [1, 4096],
    "L1_attn_skip_proj": [4096, 4096], "L1_attn_floquet_phases": [1, 4096],
    "L1_ffn_W_r": [D, D], "L1_ffn_W_k": [D, D], "L1_ffn_W_v": [D, D],
    "L1_ffn_mu_r": [1, D], "L1_ffn_mu_k": [1, D],
    "c_identb": ([128, 128], BF16), "c_identf": ([128, 128], F32),
    "c_ones": ([1, 128], F32), "c_tril01": ([128, 128], F32),
    "c_maskadd": ([128, 128], F32), "c_emat": ([16, 8, 128], BF16),
    "c_iota": ([1, T], F32),
}
for _p in ("q", "k", "v"):
    INPUT_SPECS[f"L0_ffn_{_p}_node_w"] = [15, D]
    INPUT_SPECS[f"L0_ffn_{_p}_node_b"] = [1, 15]
    INPUT_SPECS[f"L0_ffn_{_p}_w1"] = [16, D, 64]
    INPUT_SPECS[f"L0_ffn_{_p}_b1"] = [1, D]
    INPUT_SPECS[f"L0_ffn_{_p}_w2"] = [16, 64, D]
    INPUT_SPECS[f"L0_ffn_{_p}_b2"] = [16, D]

SUBLAYERS = [("l0ffn", "A"), ("l0attn", "B"), ("l0ffn", "C"),
             ("l1ffn", "D"), ("l1attn", "E"), ("l1ffn", "F")]

EMITTERS = {}


def build_fused(parts=None):
    import contextlib
    if parts is None:
        parts = list(range(6))
    nc = bass.Bass("TRN2", target_bir_lowering=False)
    ap = {}
    for name, spec in INPUT_SPECS.items():
        if isinstance(spec, tuple):
            shape, dt = spec
        else:
            shape, dt = spec, F32
        ap[name] = nc.dram_tensor(name, shape, dt, kind="ExternalInput").ap()
    x_in = nc.dram_tensor("x", [B_LOC, T, D], F32, kind="ExternalInput").ap()
    x_out = nc.dram_tensor("xo", [B_LOC, T, D], F32, kind="ExternalOutput").ap()
    xbufA = nc.dram_tensor("xbufA", [B_LOC, T, D], F32, kind="Internal").ap()
    xbufB = nc.dram_tensor("xbufB", [B_LOC, T, D], F32, kind="Internal").ap()

    r = R(nc)
    stack, handles = common_sems(nc)
    al = stack.enter_context
    identb = al(nc.sbuf_tensor("identb", [128, 128], BF16))
    identf = al(nc.sbuf_tensor("identf", [128, 128], F32))
    onesf = al(nc.sbuf_tensor("onesf", [1, 128], F32))
    ematsb = al(nc.sbuf_tensor("ematsb", [16, 8, 128], BF16))
    dummy1 = al(nc.sbuf_tensor("dummy1", [1, 4], F32))
    dummy2 = al(nc.sbuf_tensor("dummy2", [1, 4], F32))
    xt0 = al(nc.sbuf_tensor("xt0", [128, 1024], F32))
    xt1 = al(nc.sbuf_tensor("xt1", [128, 1024], F32))
    cen = al(nc.sbuf_tensor("cen", [128, 1024], F32))
    sq = al(nc.sbuf_tensor("sq", [128, 1024], F32))
    cols = al(nc.sbuf_tensor("cols", [128, 16], F32))
    xlnb0 = al(nc.sbuf_tensor("xlnb0", [128, 1024], BF16))
    xlnb1 = al(nc.sbuf_tensor("xlnb1", [128, 1024], BF16))
    pb0 = al(nc.psum_tensor("pb0", [128, 512], BF16))
    pb1 = al(nc.psum_tensor("pb1", [128, 512], BF16))

    with nc.Block() as block:
        r.setup_sems(handles)
        r.dummy1, r.dummy2 = dummy1, dummy2
        cc = {"xt": [xt0, xt1], "cen": cen, "sq": sq, "cols": cols,
              "ptb": [pb0, pb1], "xlnb": [xlnb0, xlnb1], "identb": identb[:]}
        C = {"nc": nc, "r": r, "cc": cc, "ap": ap, "identb": identb,
             "identf": identf, "onesf": onesf, "ematsb": ematsb}
        r.dma("y", identb[:], ap["c_identb"])
        r.dma("y", identf[:], ap["c_identf"])
        r.dma("y", onesf[:], ap["c_ones"])
        r.dma("y", ematsb[:], ap["c_emat"])
        r.wait("t", "LD", r.cnt["LD"])
        r.wait("v", "LD", r.cnt["LD"])
        r.wait("s", "LD", r.cnt["LD"])
        r.barrier()

        chain = [x_in]
        for i in parts[:-1]:
            chain.append(xbufA if len(chain) % 2 == 1 else xbufB)
        chain.append(x_out)
        for idx, i in enumerate(parts):
            name, sfx = SUBLAYERS[i]
            EMITTERS[name](C, chain[idx], chain[idx + 1], sfx)
            r.barrier()
            r.wait("y", "ST", r.cnt["ST"])
            r.barrier()
        r.wait("y", "ST", r.cnt["ST"])
        r.barrier()
        r.finalize(block)
    stack.close()
    return nc


def get_consts():
    ident = np.eye(128, dtype=np.float32)
    c = {}
    c["c_identb"] = ident.astype(ml_dtypes.bfloat16)
    c["c_identf"] = ident
    c["c_ones"] = np.ones((1, 128), np.float32)
    s_i = np.arange(128)[:, None]
    t_i = np.arange(128)[None, :]
    c["c_maskadd"] = np.where(s_i <= t_i, 0.0, -60.0).astype(np.float32)
    c["c_tril01"] = (s_i <= t_i).astype(np.float32)
    em = np.zeros((16, 8, 128), np.float32)
    for lht in range(8):
        for p_ in range(128):
            em[(lht * 128 + p_) // 64, lht, p_] = 1.0
    c["c_emat"] = em.astype(ml_dtypes.bfloat16)
    c["c_iota"] = np.arange(T, dtype=np.float32).reshape(1, T)
    return c


_PROGS = {}
_CONSTS = {}


EMITTERS["l1ffn"] = emit_sub_l1ffn
EMITTERS["l0ffn"] = emit_sub_l0ffn
EMITTERS["l0attn"] = emit_sub_l0attn
EMITTERS["l1attn"] = emit_sub_l1attn


def kernel(**inputs):
    if not _CONSTS:
        _CONSTS.update(get_consts())
    consts = _CONSTS
    x = np.ascontiguousarray(inputs["L0_attn_x"])
    shards = [x[2 * c:2 * c + 2] for c in range(8)]

    if "fused" not in _PROGS:
        _PROGS["fused"] = build_fused()

    wd = {}
    for name, spec in INPUT_SPECS.items():
        if name.startswith("c_"):
            wd[name] = consts[name]
        else:
            shape = spec if not isinstance(spec, tuple) else spec[0]
            wd[name] = np.ascontiguousarray(inputs[name].reshape(shape))
    in_maps = [dict(wd, x=shards[c]) for c in range(8)]
    res = run_bass_kernel_spmd(_PROGS["fused"], in_maps, core_ids=list(range(8)))
    shards = [np.asarray(res.results[c]["xo"]) for c in range(8)]
    return np.concatenate(shards, axis=0)
